# version 10
# baseline (speedup 1.0000x reference)
import numpy as np
import ml_dtypes
import concourse.bass as bass
import concourse.mybir as mybir
from concourse.dt import dt
from concourse.bass_utils import run_bass_kernel_spmd
from contextlib import ExitStack

F32 = dt.float32
BF16 = dt.bfloat16
I32 = dt.int32
AF = mybir.ActivationFunctionType
ALU = mybir.AluOpType
AX = mybir.AxisListType

NCORES = 8
T = 16384
TLOC = T // NCORES
D = 1024
NH = 16
HD = 64
CS = 16
HID = 3072
CH = 128
NCH = TLOC // CH
SCALE = 0.125
QS = 1.0 / 127.0
QS2 = QS * QS
EPS_Z = 1e-9
EPS_LN = 1e-5
NR = 512
ENGS = ("pe", "act", "dve", "pool", "sp")


class Op:
    __slots__ = ("engine", "emit", "deps", "inc", "val", "is_dma", "sem",
                 "needed", "strict")

    def __init__(self, engine, emit, deps, is_dma, sem, strict):
        self.engine = engine
        self.emit = emit
        self.deps = deps
        self.inc = 0
        self.val = None
        self.is_dma = is_dma
        self.sem = sem
        self.needed = False
        self.strict = strict


class Buf:
    __slots__ = ("writers", "readers", "name")

    def __init__(self, name):
        self.writers = []
        self.readers = []
        self.name = name


class Sched:
    def __init__(self):
        self.ops = []

    def add(self, engine, emit, reads=(), writes=(), deps=(), is_dma=False,
            excl=True, dkey=None, strict=False):
        dd = list(deps)
        for b in reads:
            dd += b.writers
        for b in writes:
            dd += b.readers + b.writers
        sem = None
        if is_dma:
            sem = "dma:" + (dkey or (writes[0].name if writes else "misc"))
        op = Op(engine, emit, dd, is_dma, sem, strict)
        self.ops.append(op)
        for b in reads:
            b.readers.append(op)
        for b in writes:
            if excl:
                b.writers = [op]
                b.readers = []
            else:
                b.writers.append(op)
        return op

    def finalize(self, nc, block, ctx):
        for op in self.ops:
            for d in op.deps:
                d.needed = True
        sem_names = list(ENGS)
        for op in self.ops:
            if op.is_dma and op.sem not in sem_names:
                sem_names.append(op.sem)
        sems = {n: ctx.enter_context(nc.semaphore(f"s{i}"))
                for i, n in enumerate(sem_names)}
        counts = {n: 0 for n in sem_names}
        prefix = {}
        for idx, op in enumerate(self.ops):
            if op.is_dma:
                op.inc = 16
            elif op.needed:
                op.inc = 1
                op.sem = op.engine
            if op.inc:
                counts[op.sem] += op.inc
                op.val = counts[op.sem]
                if op.is_dma:
                    prefix.setdefault(op.sem, []).append((idx, op.val))
        import bisect as _bis

        def dma_wait_val(sem, cidx):
            lst = prefix[sem]
            k = _bis.bisect_left(lst, (cidx, -1))
            return lst[k - 1][1] if k else 0

        idx_of = {id(op): i for i, op in enumerate(self.ops)}

        def body(eng_handle, ename):
            waited = {}
            for op in self.ops:
                if op.engine != ename:
                    continue
                cidx = idx_of[id(op)]
                want = {}
                for d in op.deps:
                    if d.inc == 0:
                        continue
                    if d.engine == ename and not d.is_dma and not d.strict:
                        continue
                    v = dma_wait_val(d.sem, cidx) if d.is_dma else d.val
                    if v > want.get(d.sem, 0):
                        want[d.sem] = v
                for se, v in want.items():
                    if waited.get(se, 0) >= v:
                        continue
                    eng_handle.wait_ge(sems[se], v)
                    waited[se] = v
                ins = op.emit(eng_handle)
                if op.inc:
                    ins.then_inc(sems[op.sem], op.inc)

        block.tensor(lambda e: body(e, "pe"))
        block.scalar(lambda e: body(e, "act"))
        block.vector(lambda e: body(e, "dve"))
        block.gpsimd(lambda e: body(e, "pool"))
        block.sync(lambda e: body(e, "sp"))


def build_nc(stop=None, taps=False):
    nc = bass.Bass("TRN2", target_bir_lowering=False,
                   dynamic_dma_scratch_size=16384,
                   detect_race_conditions=False)
    x_d = nc.dram_tensor("x", [TLOC, D], F32, kind="ExternalInput")
    wqkv_d = nc.dram_tensor("w_qkv", [3 * D, D], F32, kind="ExternalInput")
    wout_d = nc.dram_tensor("w_out", [D, D], F32, kind="ExternalInput")
    code_d = nc.dram_tensor("fc_code", [NH * CS, HD], F32, kind="ExternalInput")
    wg_d = nc.dram_tensor("w_gate", [HID, D], I32, kind="ExternalInput")
    wu_d = nc.dram_tensor("w_up", [HID, D], I32, kind="ExternalInput")
    wd_d = nc.dram_tensor("w_down", [D, HID], I32, kind="ExternalInput")
    ident_d = nc.dram_tensor("c_ident", [128, 128], BF16, kind="ExternalInput")
    u8_d = nc.dram_tensor("c_u8", [128, 128], BF16, kind="ExternalInput")
    onesr_d = nc.dram_tensor("c_onesr", [1, 128], BF16, kind="ExternalInput")
    m1_d = nc.dram_tensor("c_m1", [128, 128], F32, kind="ExternalInput")
    ones1_d = nc.dram_tensor("c_ones1", [128, 1], BF16, kind="ExternalInput")
    padv_d = nc.dram_tensor("c_padv", [64, 1], F32, kind="ExternalInput")
    rvec_d = nc.dram_tensor("c_rvec", [33, 8], F32, kind="ExternalInput")
    y_d = nc.dram_tensor("y", [TLOC, D], F32, kind="ExternalOutput")

    xb_s = nc.dram_tensor("s_xb", [TLOC, D], BF16, kind="Internal")
    wvb_s = nc.dram_tensor("s_wvb", [D, D], BF16, kind="Internal")
    wob_s = nc.dram_tensor("s_wob", [D, D], BF16, kind="Internal")
    wgb_s = nc.dram_tensor("s_wgb", [HID, D], BF16, kind="Internal")
    wub_s = nc.dram_tensor("s_wub", [HID, D], BF16, kind="Internal")
    wdb_s = nc.dram_tensor("s_wdb", [D, HID], BF16, kind="Internal")
    kef_s = nc.dram_tensor("s_kef", [NR, TLOC], BF16, kind="Internal")
    lnb_s = nc.dram_tensor("s_lnb", [TLOC, D], BF16, kind="Internal")
    lntm_s = nc.dram_tensor("s_lntm", [TLOC, D], F32, kind="Internal")
    dup_s = nc.dram_tensor("s_dup", [33, 1024], F32, kind="Internal")
    ccm_in = nc.dram_tensor("cc_m_in", [64, 8], F32, kind="Internal")
    ccm_out = nc.dram_tensor("cc_m_out", [64, 8], F32, kind="Internal")
    ccs_in = nc.dram_tensor("cc_s_in", [33, 1024], F32, kind="Internal")
    ccs_out = nc.dram_tensor("cc_s_out", [33 * 8, 1024], F32, kind="Internal")

    ctx = ExitStack()

    def sb(name, shape, dtp):
        return ctx.enter_context(nc.sbuf_tensor(name, shape, dtp))

    def psum(name, shape, dtp):
        return ctx.enter_context(nc.psum_tensor(name, shape, dtp))

    ident = sb("ident", [128, 128], BF16)
    u8t = sb("u8t", [128, 128], BF16)
    m1t = sb("m1t", [128, 128], F32)
    onesr = sb("onesr", [1, 128], BF16)
    ones1 = sb("ones1", [128, 1], BF16)
    padv = sb("padv", [64, 1], F32)
    rvec = sb("rvec", [33, 8], F32)
    negm = sb("negm", [64, 8], F32)
    mred = sb("mred", [64, 4], F32)
    mloc = sb("mloc", [64, 8], F32)
    carrf = [sb(f"carrf{i}", [1, NR], F32) for i in range(2)]
    carrb = [sb(f"carrb{i}", [1, NR], BF16) for i in range(2)]
    row127 = sb("row127", [1, NR], F32)
    hsb = [sb(f"hsb{i}", [128, 16], F32) for i in range(2)]
    stat = sb("stat", [128, 16], F32)

    ARK = sb("ARK", [128, 16384], BF16)
    ARB = sb("ARB", [128, 16384], BF16)
    ARA = sb("ARA", [128, 8192], BF16)
    ARG = sb("ARG", [128, 8192], BF16)
    ARG2 = sb("ARG2", [128, 8192], BF16)
    ARD = sb("ARD", [128, 2048], BF16)
    ARE = sb("ARE", [128, 16384], BF16)
    ARF = sb("ARF", [128, 8192], BF16)
    ARH = sb("ARH", [128, 8192], BF16)
    ARI = sb("ARI", [128, 8192], BF16)

    qk_bf = [ARA[:, 1024 * i:1024 * (i + 1)] for i in range(8)]
    wst = [ARG[:, 0:2048].bitcast(F32), ARG[:, 2048:4096].bitcast(F32)]
    wsti = [ARG[:, 0:2048].bitcast(I32), ARG[:, 2048:4096].bitcast(I32)]
    wbf = [ARG[:, 4096:5120], ARG[:, 5120:6144]]
    xtb = [ARG[:, 6144:7168], ARG[:, 7168:8192]]
    codeb = ARG2[0:16, 0:1024]
    codeT = ARG2[:, 1024:1280]
    codeblk = [ARG2[:, 1536 + 256 * i:1536 + 256 * (i + 1)] for i in range(8)]
    xfm = [ARB[:, 2048 * i:2048 * (i + 1)] for i in range(8)]
    wtq = [ARF[:, 512 * i:512 * (i + 1)] for i in range(8)]
    wtk = [ARF[:, 4096 + 512 * i:4096 + 512 * (i + 1)] for i in range(8)]
    wvT = [ARE[:, 1024 * i:1024 * (i + 1)] for i in range(8)]
    woT = [ARE[:, 8192 + 1024 * i:8192 + 1024 * (i + 1)] for i in range(8)]
    xsb = [ARA[:, 1024 * s:1024 * (s + 1)] for s in range(2)]
    kefm = [ARK[0:64, 2048 * p:2048 * (p + 1)] for p in range(8)]
    ketm = [ARD[:, 512 * i:512 * (i + 1)] for i in range(2)]
    mtbf = [ARG[:, 512 * i:512 * (i + 1)] for i in range(8)]
    srs = ARI[:, 0:2048].bitcast(F32)[0:33, :]
    accb = [ARI[:, 2048 + 2048 * i:2048 + 2048 * (i + 1)].bitcast(F32)[0:33, :]
            for i in range(2)]
    gth = ARI[:, 6144:8192].bitcast(F32)[0:33, :]
    dbf = ARG[:, 0:1024].bitcast(F32)
    lnd = ARG[:, 1024:2048].bitcast(F32)
    vtm = [ARG[:, 2048 + 1024 * i:2048 + 1024 * (i + 1)] for i in range(2)]
    pre1 = ARG[:, 4096 + 2048:4096 + 4096].bitcast(F32)
    sqb = ARG2[:, 0:2048].bitcast(F32)
    ln1 = ARG2[:, 2048:4096].bitcast(F32)
    ln1b = [ARG2[:, 4096 + 1024 * i:4096 + 1024 * (i + 1)] for i in range(2)]
    etm = [ARG2[:, 6144 + 512 * i:6144 + 512 * (i + 1)] for i in range(2)]
    qtm = [ARG2[:, 7168 + 512 * i:7168 + 512 * (i + 1)] for i in range(2)]
    xres = [ARH[:, 2048 * i:2048 * (i + 1)].bitcast(F32) for i in range(2)]
    qfm = [[ARH[0:64, 4096 + 2048 * s + 128 * h:4096 + 2048 * s + 128 * (h + 1)]
            for h in range(16)] for s in range(2)]
    asb = [ARI[:, 6144 + 128 * i:6144 + 128 * (i + 1)] for i in range(4)]
    xofm = [ARI[:, 6656 + 128 * i:6656 + 128 * (i + 1)] for i in range(8)]
    S32 = [ARI[0:64, 2048 * i:2048 * (i + 1)].bitcast(F32) for i in range(2)]
    Sbf = [ARI[0:64, 4096 + 1024 * i:4096 + 1024 * (i + 1)] for i in range(2)]
    hpr = ([ARK[:, 2048 * i:2048 * (i + 1)] for i in range(8)] +
           [ARA[:, 2048 * i:2048 * (i + 1)] for i in range(4)] +
           [ARG[:, 2048 * i:2048 * (i + 1)] for i in range(4)] +
           [ARG2[:, 2048 * i:2048 * (i + 1)] for i in range(4)] +
           [ARI[:, 2048 * i:2048 * (i + 1)] for i in range(4)])
    lnfm = [ARB[:, 2048 * i:2048 * (i + 1)] for i in range(8)]
    wgT = [[ARF[:, 512 * i:512 * (i + 1)] for i in range(8)],
           [ARH[:, 512 * i:512 * (i + 1)] for i in range(8)]]
    wuT = [[ARF[:, 4096 + 512 * i:4096 + 512 * (i + 1)] for i in range(8)],
           [ARH[:, 4096 + 512 * i:4096 + 512 * (i + 1)] for i in range(8)]]
    wdT = ([ARD[:, 1024 * i:1024 * (i + 1)] for i in range(2)] +
           [ARE[:, 1024 * i:1024 * (i + 1)] for i in range(16)] +
           [ARF[:, 1024 * i:1024 * (i + 1)] for i in range(6)])
    ltm = [ARB[:, 2048 * i:2048 * (i + 1)].bitcast(F32) for i in range(2)]
    outb = [ARB[:, 4096 + 2048 * i:4096 + 2048 * (i + 1)].bitcast(F32)
            for i in range(2)]
    sq2 = ARB[:, 8192:10240].bitcast(F32)
    pr2 = ARB[:, 10240:12288].bitcast(F32)
    silb = ARD[:, 1024:2048].bitcast(F32)

    PB = [psum(f"PB{i}", [128, 512], F32) for i in range(4)]
    PXO = [psum(f"PXO{i}", [128, 512], F32) for i in range(2)]
    PDS = psum("PDS", [64, 512], F32)
    PT = psum("PT", [128, 1024], BF16)

    S = Sched()
    MARK = {}
    _B = {}

    def buf(name):
        if name not in _B:
            _B[name] = Buf(name)
        return _B[name]

    tap_list = []

    def tap(name, ap, bnames, lvl=1):
        if not taps or lvl > int(taps):
            return
        td = nc.dram_tensor("tap_" + name, list(ap.shape), ap.dtype,
                            kind="ExternalOutput")
        op = S.add("pool", lambda e, td=td, ap=ap: e.dma_start(
            out=td[tuple(slice(None) for _ in ap.shape)], in_=ap),
            reads=[buf(b) for b in bnames], is_dma=True, dkey="taps")
        tap_list.append(op)

    for t, d_, nm in ((ident, ident_d, "ident"), (u8t, u8_d, "u8t"),
                      (m1t, m1_d, "m1t"),
                      (onesr, onesr_d, "onesr"), (ones1, ones1_d, "ones1"),
                      (padv, padv_d, "padv"), (rvec, rvec_d, "rvec")):
        S.add("sp", lambda e, t=t, d_=d_: e.dma_start(out=t[:, :], in_=d_[:, :]),
              writes=[buf(nm)], is_dma=True, dkey="consts")
    b_id = buf("ident")
    const_ops = list(S.ops)

    for i in range(8):
        s = i % 2
        S.add("sp", lambda e, i=i, s=s: e.dma_start(
            out=wst[s][:, :], in_=wqkv_d[128 * i:128 * (i + 1), :]),
            writes=[buf(f"wst{s}")], is_dma=True)
        S.add("act", lambda e, i=i, s=s: e.copy(out=qk_bf[i][:, :], in_=wst[s][:, :]),
              reads=[buf(f"wst{s}")], writes=[buf(f"qkbf{i}")])
    S.add("sp", lambda e: e.dma_start(
        out=wst[0][0:16, 0:NH * HD].rearrange("c (h d) -> c h d", d=HD),
        in_=code_d[:, :].rearrange("(h c) d -> c h d", c=CS)),
        writes=[buf("wst0")], is_dma=True)
    S.add("act", lambda e: e.mul(out=codeb[:, :], in_=wst[0][0:16, 0:NH * HD],
                                 mul=SCALE),
          reads=[buf("wst0")], writes=[buf("codeb")], deps=const_ops)
    b_ptc = buf("bk_pt")
    for h in range(NH):
        S.add("pe", lambda e, h=h: e.transpose(
            PT[0:64, 16 * h:16 * h + 16], codeb[:, 64 * h:64 * h + 64],
            ident[0:16, 0:16]),
            reads=[buf("codeb"), b_id], writes=[b_ptc], excl=(h == 0))
        S.add("pe", lambda e, h=h: e.transpose(
            PT[64:128, 16 * h:16 * h + 16], codeb[:, 64 * h:64 * h + 64],
            ident[0:16, 0:16]),
            reads=[buf("codeb"), b_id], writes=[b_ptc], excl=False)
    S.add("dve", lambda e: e.tensor_copy(out=codeT[:, 0:NH * CS],
                                         in_=PT[:, 0:NH * CS]),
          reads=[b_ptc], writes=[buf("codeT")])
    for i in range(8):
        S.add("dve", lambda e, i=i: e.memset(wtq[i][:, :], 0.0),
              writes=[buf(f"wtq{i}")])
        S.add("dve", lambda e, i=i: e.memset(wtk[i][:, :], 0.0),
              writes=[buf(f"wtk{i}")])
    for i in range(8):
        S.add("dve", lambda e, i=i: e.memset(codeblk[i][:, :], 0.0),
              writes=[buf(f"cbk{i}")])
        S.add("dve", lambda e, i=i: e.tensor_copy(
            out=codeblk[i][0:64, 32 * i:32 * i + 16],
            in_=codeT[0:64, 32 * i:32 * i + 16]),
            reads=[buf("codeT")], writes=[buf(f"cbk{i}")], excl=False)
        S.add("dve", lambda e, i=i: e.tensor_copy(
            out=codeblk[i][64:128, 32 * i + 16:32 * i + 32],
            in_=codeT[64:128, 32 * i + 16:32 * i + 32]),
            reads=[buf("codeT")], writes=[buf(f"cbk{i}")], excl=False)
    for which, wt in (("q", wtq), ("k", wtk)):
        if which == "k":
            for i in range(8):
                s = i % 2
                S.add("sp", lambda e, i=i, s=s: e.dma_start(
                    out=wst[s][:, :],
                    in_=wqkv_d[1024 + 128 * i:1024 + 128 * (i + 1), :]),
                    writes=[buf(f"wst{s}")], is_dma=True)
                S.add("act", lambda e, i=i, s=s: e.copy(out=qk_bf[i][:, :],
                                                        in_=wst[s][:, :]),
                      reads=[buf(f"wst{s}")], writes=[buf(f"qkbf{i}")])
        for dc in range(8):
            pb = buf("bk_pb0")
            for i in range(8):
                S.add("pe", lambda e, i=i, dc=dc: e.matmul(
                    PB[0][:, 0:256],
                    qk_bf[i][:, 128 * dc:128 * dc + 128],
                    codeblk[i][:, :], start=(i == 0), stop=(i == 7)),
                    reads=[buf(f"qkbf{i}"), buf(f"cbk{i}")], writes=[pb],
                    excl=(i == 0))
            S.add("dve", lambda e, wt=wt, dc=dc: e.tensor_copy(
                out=wt[dc].rearrange("p (h c) -> p h c", c=32)[:, :, 0:16],
                in_=PB[0][:, 0:256].rearrange("p (h c) -> p h c", c=16)),
                reads=[pb], writes=[buf(f"wt{which}{dc}")], excl=False)
    wt_done = [o for i in range(8)
               for o in (_B[f"wtq{i}"].writers + _B[f"wtk{i}"].writers)]

    pst = [ARI[:, 0:2048].bitcast(I32), ARI[:, 2048:4096].bitcast(I32)]
    pbf = [ARI[:, 4096:5120], ARI[:, 5120:6144]]
    prep_jobs = []
    for nm2, src2, scr2 in (("wg", wg_d, wgb_s), ("wu", wu_d, wub_s)):
        for i2 in range(24):
            prep_jobs.append((nm2, scr2,
                              src2[128 * i2:128 * (i2 + 1), :],
                              (128 * i2, 0)))
    for i2 in range(8):
        for c3 in range(3):
            prep_jobs.append(("wd", wdb_s,
                              wd_d[128 * i2:128 * (i2 + 1),
                                   1024 * c3:1024 * (c3 + 1)],
                              (128 * i2, 1024 * c3)))
    prep_loads, prep_casts, prep_stores = [], [], []

    def prep_load(k):
        src_ap = prep_jobs[k][2]
        s2 = k % 2
        prep_loads.append(S.add("pool", lambda e, src_ap=src_ap, s2=s2:
                                e.dma_start(out=pst[s2][:, :], in_=src_ap),
                                writes=[buf(f"pst{s2}")], is_dma=True))
    pk = [0]

    def prep_step(n):
        for _ in range(n):
            k = pk[0]
            if k >= len(prep_jobs):
                return
            pk[0] += 1
            nm2, scr2, _, (r0, c0) = prep_jobs[k]
            s2 = k % 2
            prep_casts.append(S.add("pool", lambda e, s2=s2: e.tensor_copy(
                out=pbf[s2][:, :], in_=pst[s2][:, :]),
                reads=[buf(f"pst{s2}")], writes=[buf(f"pbf{s2}")]))
            prep_stores.append(S.add(
                "pool", lambda e, scr2=scr2, r0=r0, c0=c0, s2=s2: e.dma_start(
                    out=scr2[r0:r0 + 128, c0:c0 + 1024], in_=pbf[s2][:, :]),
                reads=[buf(f"pbf{s2}")], writes=[buf(f"scr_{nm2}")],
                is_dma=True, excl=False, dkey=f"pbf{s2}"))
            if k + 2 < len(prep_jobs):
                prep_load(k + 2)

    def xres_load(j):
        s = j % 2
        S.add("pool", lambda e, j=j, s=s: e.dma_start(
            out=xres[s][:, :], in_=x_d[CH * j:CH * (j + 1), :]),
            writes=[buf(f"xres{s}")], is_dma=True)
    prep_load(0)
    prep_load(1)
    xres_load(0)
    xres_load(1)
    for j in range(NCH):
        s = j % 2
        S.add("pool", lambda e, s=s: e.tensor_copy(out=xtb[s][:, :],
                                                   in_=xres[s][:, :]),
              reads=[buf(f"xres{s}")], writes=[buf(f"xtb{s}")])
        S.add("pool", lambda e, j=j, s=s: e.dma_start(
            out=xb_s[CH * j:CH * (j + 1), :], in_=xtb[s][:, :]),
            reads=[buf(f"xtb{s}")], writes=[buf("scr_x")], is_dma=True,
            excl=False, dkey=f"xtb{s}")
        if j + 2 < NCH:
            xres_load(j + 2)
        prep_step(2)
    xfm_ops = []
    for dc in range(8):
        xfm_ops.append(S.add("sp", lambda e, dc=dc: e.dma_start(
            out=xfm[dc][:, :], in_=xb_s[:, 128 * dc:128 * dc + 128],
            transpose=True),
            reads=[buf("scr_x")], writes=[buf(f"xfm{dc}")], is_dma=True,
            deps=wt_done, dkey=f"xfmT{dc % 2}"))
    prep_step(len(prep_jobs))
    prep_guard = [prep_casts[-1], prep_loads[-1], prep_loads[-2],
                  prep_stores[-1], prep_stores[-2]]

    wvwo_casts = []
    for name, scr in (("wv", wvb_s), ("wo", wob_s)):
        for i in range(8):
            s = i % 2
            if name == "wv":
                S.add("sp", lambda e, i=i, s=s: e.dma_start(
                    out=wst[s][:, :],
                    in_=wqkv_d[2048 + 128 * i:2048 + 128 * (i + 1), :]),
                    writes=[buf(f"wst{s}")], is_dma=True)
            else:
                S.add("sp", lambda e, i=i, s=s: e.dma_start(
                    out=wst[s][:, :], in_=wout_d[128 * i:128 * (i + 1), :]),
                    writes=[buf(f"wst{s}")], is_dma=True)
            wvwo_casts.append(S.add(
                "act", lambda e, s=s: e.copy(out=wbf[s][:, :], in_=wst[s][:, :]),
                reads=[buf(f"wst{s}")], writes=[buf(f"wbf{s}")]))
            S.add("sp", lambda e, scr=scr, i=i, s=s: e.dma_start(
                out=scr[128 * i:128 * (i + 1), :], in_=wbf[s][:, :]),
                reads=[buf(f"wbf{s}")], writes=[buf(f"scr_{name}")],
                is_dma=True, excl=False, dkey=f"wbf{s}")
    wvT_ops, woT_ops = [], []
    for name, scr, dstT, lst in (("wv", wvb_s, wvT, wvT_ops),
                                 ("wo", wob_s, woT, woT_ops)):
        for dc in range(8):
            lst.append(S.add("sp", lambda e, scr=scr, dstT=dstT, dc=dc: e.dma_start(
                out=dstT[dc][:, :], in_=scr[:, 128 * dc:128 * dc + 128],
                transpose=True),
                reads=[buf(f"scr_{name}")], writes=[buf(f"{name}T{dc}")],
                is_dma=True, dkey=f"{name}T"))

    MARK["P0"] = len(S.ops)
    tap("wtq0", wtq[0][:, :], ["wtq0"], lvl=1)
    tap("wtk0", wtk[0][:, :], ["wtk0"], lvl=1)
    tap("xfm0", xfm[0][:, 0:512], ["xfm0"], lvl=1)
    tap("wvT0", wvT[0][:, :], ["wvT0"], lvl=1)
    tap("woT0", woT[0][:, :], ["woT0"], lvl=1)

    for p in range(8):
        for ts in range(4):
            gi = (p * 4 + ts) % 2
            pb = buf(f"bk_pb{gi}")
            for dc in range(8):
                S.add("pe", lambda e, p=p, ts=ts, dc=dc, gi=gi: e.matmul(
                    PB[gi][0:64, 0:512], wtk[dc][:, 64 * p:64 * p + 64],
                    xfm[dc][:, 512 * ts:512 * ts + 512],
                    start=(dc == 0), stop=(dc == 7)),
                    reads=[buf(f"wtk{dc}"), buf(f"xfm{dc}")], writes=[pb],
                    excl=(dc == 0),
                    deps=(wt_done + xfm_ops) if (p == 0 and ts == 0) else ())
            S.add("dve", lambda e, p=p, ts=ts, gi=gi: e.tensor_reduce(
                out=mred[:, ts:ts + 1], in_=PB[gi][0:64, 0:512],
                axis=AX.X, op=ALU.max),
                reads=[pb], writes=[buf("mred")], excl=(ts == 0), strict=True)
        S.add("dve", lambda e, p=p: e.tensor_reduce(
            out=mloc[:, p:p + 1], in_=mred[:, :], axis=AX.X, op=ALU.max),
            reads=[buf("mred")], writes=[buf("mloc")], excl=False, strict=True)

    tap("mlocpre", mloc[:, :], ["mloc"], lvl=1)
    S.add("act", lambda e: e.dma_start(out=ccm_in[:, :], in_=mloc[:, :]),
          reads=[buf("mloc")], writes=[buf("ccm")], is_dma=True)
    if taps and int(taps) >= 1:
        tccin = nc.dram_tensor("tap_ccmin", [64, 8], F32, kind="ExternalOutput")
        tap_list.append(S.add("act", lambda e: e.dma_start(
            out=tccin[:, :], in_=ccm_in[:, :]),
            reads=[buf("ccm")], is_dma=True, dkey="taps"))
    S.add("pool", lambda e: e.collective_compute(
        "AllReduce", ALU.max, replica_groups=[list(range(NCORES))],
        ins=[ccm_in.ap().opt()], outs=[ccm_out.ap().opt()]),
        reads=[buf("ccm")], writes=[buf("ccmo")], strict=True)
    if taps and int(taps) >= 1:
        tccout = nc.dram_tensor("tap_ccmout", [64, 8], F32, kind="ExternalOutput")
        tap_list.append(S.add("act", lambda e: e.dma_start(
            out=tccout[:, :], in_=ccm_out[:, :]),
            reads=[buf("ccmo")], is_dma=True, dkey="taps"))
    S.add("act", lambda e: e.dma_start(out=mloc[:, :], in_=ccm_out[:, :]),
          reads=[buf("ccmo")], writes=[buf("mloc")], is_dma=True, dkey="mback")
    tap("padv2", padv[:, :], ["padv"], lvl=1)
    S.add("dve", lambda e: e.tensor_scalar(
        out=negm[:, :], in0=mloc[:, :], scalar1=padv[:, 0:1], scalar2=-1.0,
        op0=ALU.max, op1=ALU.mult),
        reads=[buf("mloc"), buf("padv")], writes=[buf("negm")])

    MARK["P1"] = len(S.ops)
    tap("mloc", mloc[:, :], ["mloc"], lvl=1)
    tap("negm", negm[:, :], ["negm"], lvl=1)
    kexp_ops = []
    for p in range(8):
        for ts in range(4):
            gi = (p * 4 + ts) % 2
            pb = buf(f"bk_pb{gi}")
            for dc in range(8):
                S.add("pe", lambda e, p=p, ts=ts, dc=dc, gi=gi: e.matmul(
                    PB[gi][0:64, 0:512], wtk[dc][:, 64 * p:64 * p + 64],
                    xfm[dc][:, 512 * ts:512 * ts + 512],
                    start=(dc == 0), stop=(dc == 7)),
                    reads=[buf(f"wtk{dc}"), buf(f"xfm{dc}")], writes=[pb],
                    excl=(dc == 0))
            o = S.add("act", lambda e, p=p, ts=ts, gi=gi: e.activation(
                out=kefm[p][:, 512 * ts:512 * ts + 512], in_=PB[gi][0:64, 0:512],
                func=AF.Exp, bias=negm[:, p:p + 1], scale=1.0),
                reads=[pb, buf("negm")], writes=[buf(f"kefm{p}")], excl=False)
            kexp_ops.append(o)
        S.add("act", lambda e, p=p: e.dma_start(
            out=kef_s[64 * p:64 * (p + 1), :], in_=kefm[p][:, :]),
            reads=[buf(f"kefm{p}")], writes=[buf("scr_kef")], is_dma=True,
            excl=False, dkey="scr_kef")

    ztot_pb = buf("bk_pds")
    mt_pb = [buf("bk_pb0"), buf("bk_pb1"), buf("bk_pb2"), buf("bk_pb3"),
             buf("bk_pxo0"), buf("bk_pxo1"), buf("bk_pb0"), buf("bk_pb1")]
    for j in range(NCH):
        ks = j % 2
        S.add("sp", lambda e, j=j, ks=ks: e.dma_start(
            out=ketm[ks][:, :], in_=kef_s[:, CH * j:CH * j + 128], transpose=True),
            reads=[buf("scr_kef")], writes=[buf(f"ketm{ks}")], is_dma=True)
        S.add("pool", lambda e, j=j, ks=ks: e.dma_start(
            out=xres[ks][:, :], in_=x_d[CH * j:CH * (j + 1), :]),
            writes=[buf(f"xres{ks}")], is_dma=True)
        S.add("act", lambda e, j=j, ks=ks: e.copy(out=xsb[ks][:, :],
                                                  in_=xres[ks][:, :]),
              reads=[buf(f"xres{ks}")], writes=[buf(f"xsb{ks}")],
              deps=(kexp_ops + wt_done) if j < 2 else ())
        S.add("pe", lambda e, j=j, ks=ks: e.matmul(
            PDS[0:1, 0:NR], ones1[:, :], ketm[ks][:, :],
            start=(j == 0), stop=(j == NCH - 1)),
            reads=[buf(f"ketm{ks}"), buf("ones1")], writes=[ztot_pb],
            excl=(j == 0))
        for dc in range(6):
            pp = PB[dc] if dc < 4 else PXO[dc - 4]
            S.add("pe", lambda e, j=j, ks=ks, dc=dc, pp=pp: e.matmul(
                pp[:, 0:NR], xsb[ks][:, 128 * dc:128 * dc + 128],
                ketm[ks][:, :], start=(j == 0), stop=(j == NCH - 1)),
                reads=[buf(f"xsb{ks}"), buf(f"ketm{ks}")], writes=[mt_pb[dc]],
                excl=(j == 0))
    S.add("dve", lambda e: e.tensor_copy(out=row127[0:1, 0:NR],
                                         in_=PDS[0:1, 0:NR]),
          reads=[ztot_pb], writes=[buf("zr")], excl=False)
    for dc in range(6):
        pp = PB[dc] if dc < 4 else PXO[dc - 4]
        S.add("dve", lambda e, dc=dc, pp=pp: e.tensor_copy(
            out=mtbf[dc][:, :], in_=pp[:, 0:NR]),
            reads=[mt_pb[dc]], writes=[buf(f"mtbf{dc}")],
            deps=(wvwo_casts if dc == 0 else ()))
    for j in range(NCH):
        ks = j % 2
        S.add("sp", lambda e, j=j, ks=ks: e.dma_start(
            out=ketm[ks][:, :], in_=kef_s[:, CH * j:CH * j + 128], transpose=True),
            reads=[buf("scr_kef")], writes=[buf(f"ketm{ks}")], is_dma=True)
        S.add("pool", lambda e, j=j, ks=ks: e.dma_start(
            out=xres[ks][:, :], in_=x_d[CH * j:CH * (j + 1), :]),
            writes=[buf(f"xres{ks}")], is_dma=True)
        S.add("act", lambda e, j=j, ks=ks: e.copy(out=xsb[ks][:, :],
                                                  in_=xres[ks][:, :]),
              reads=[buf(f"xres{ks}")], writes=[buf(f"xsb{ks}")])
        for dc in range(6, 8):
            S.add("pe", lambda e, j=j, ks=ks, dc=dc: e.matmul(
                PB[dc - 6][:, 0:NR], xsb[ks][:, 128 * dc:128 * dc + 128],
                ketm[ks][:, :], start=(j == 0), stop=(j == NCH - 1)),
                reads=[buf(f"xsb{ks}"), buf(f"ketm{ks}")], writes=[mt_pb[dc]],
                excl=(j == 0))
    for dc in range(6, 8):
        S.add("dve", lambda e, dc=dc: e.tensor_copy(
            out=mtbf[dc][:, :], in_=PB[dc - 6][:, 0:NR]),
            reads=[mt_pb[dc]], writes=[buf(f"mtbf{dc}")])
    for hb in range(2):
        pbsr = buf("bk_pds")
        for h in range(8 * hb, 8 * hb + 8):
            for dc in range(8):
                S.add("pe", lambda e, h=h, dc=dc, hb=hb: e.matmul(
                    PDS[0:32, 64 * h - 512 * hb:64 * h - 512 * hb + 64],
                    mtbf[dc][:, 32 * h:32 * h + 32],
                    wvT[dc][:, 64 * h:64 * h + 64], start=(dc == 0),
                    stop=(dc == 7)),
                    reads=[buf(f"mtbf{dc}"), buf(f"wvT{dc}")], writes=[pbsr],
                    excl=(h == 8 * hb and dc == 0),
                    deps=(wvT_ops if (h == 0 and dc == 0) else ()))
        S.add("dve", lambda e, hb=hb: e.tensor_copy(
            out=srs[0:32, 512 * hb:512 * hb + 512], in_=PDS[0:32, :]),
            reads=[pbsr], writes=[buf("srs")], excl=False, deps=prep_guard)
    S.add("pool", lambda e: e.dma_start(out=ccs_in[0:32, :], in_=srs[0:32, :]),
          reads=[buf("srs")], writes=[buf("ccs")], is_dma=True, excl=False)
    S.add("pool", lambda e: e.dma_start(out=ccs_in[32:33, 0:NR],
                                        in_=row127[:, :]),
          reads=[buf("zr")], writes=[buf("ccs")], is_dma=True, excl=False,
          dkey="ccs")
    S.add("dve", lambda e: e.memset(carrf[1][:, :], 0.0),
          writes=[buf("carrf1")])
    S.add("pool", lambda e: e.dma_start(out=ccs_in[32:33, NR:1024],
                                        in_=carrf[1][:, :]),
          reads=[buf("carrf1")], writes=[buf("ccs")], is_dma=True, excl=False,
          dkey="ccs")
    S.add("pool", lambda e: e.collective_compute(
        "AllGather", ALU.bypass, replica_groups=[list(range(NCORES))],
        ins=[ccs_in.ap().opt()], outs=[ccs_out.ap().opt()]),
        reads=[buf("ccs")], writes=[buf("ccso")], strict=True)

    MARK["P2"] = len(S.ops)
    tap("kefm0", kefm[0][:, 0:512], ["kefm0"], lvl=4)
    tap("srs", srs[:, :], ["srs"], lvl=4)
    tap("mtbf0", mtbf[0][:, :], ["mtbf0"], lvl=4)
    S.add("dve", lambda e: e.memset(accb[0][:, :], 0.0), writes=[buf("accb0")],
          deps=prep_guard)
    for r in range(NCORES):
        S.add("pool", lambda e, r=r: e.dma_start(
            out=gth[:, :], in_=ccs_out[33 * r:33 * (r + 1), :]),
            reads=[buf("ccso")], writes=[buf("gth")], is_dma=True)
        a, b2 = r % 2, (r + 1) % 2
        S.add("dve", lambda e, r=r, a=a, b2=b2: e.scalar_tensor_tensor(
            out=accb[b2][:, :], in0=gth[:, :], scalar=rvec[:, r:r + 1],
            in1=accb[a][:, :], op0=ALU.mult, op1=ALU.add),
            reads=[buf("gth"), buf(f"accb{a}"), buf("rvec")],
            writes=[buf(f"accb{b2}")],
            deps=(prep_guard if r == 0 else ()))
    accf = buf(f"accb{NCORES % 2}")
    accT = accb[NCORES % 2]
    S.add("pool", lambda e: e.dma_start(out=dup_s[:, :], in_=accT[:, :]),
          reads=[accf], writes=[buf("dup")], is_dma=True)
    S.add("pool", lambda e: e.dma_start(out=S32[0][32:64, :], in_=dup_s[0:32, :]),
          reads=[buf("dup")], writes=[buf("S32_0")], is_dma=True, excl=False,
          dkey="sdup", deps=prep_guard)
    S.add("pool", lambda e: e.dma_start(out=row127[:, :],
                                        in_=dup_s[32:33, 0:NR]),
          reads=[buf("dup"), buf("zr")], writes=[buf("row127"), buf("zr")],
          is_dma=True)
    S.add("dve", lambda e: e.tensor_scalar(
        out=carrf[0][:, :], in0=row127[:, :], scalar1=EPS_Z, scalar2=8.0,
        op0=ALU.add, op1=ALU.mult),
        reads=[buf("row127")], writes=[buf("carrf0")])
    S.add("act", lambda e: e.copy(out=carrb[0][:, :], in_=carrf[0][:, :]),
          reads=[buf("carrf0")], writes=[buf("carrb0")])
    S.add("dve", lambda e: e.tensor_copy(out=S32[0][0:32, :], in_=accT[0:32, :]),
          reads=[accf], writes=[buf("S32_0")], excl=False)
    S.add("act", lambda e: e.copy(out=Sbf[0][:, :], in_=S32[0][:, :]),
          reads=[buf("S32_0")], writes=[buf("Sbf_0")])

    MARK["P3"] = len(S.ops)
    tap("acc", accT[:, :], [f"accb{NCORES % 2}"], lvl=4)
    tap("carrf0", carrf[0][:, :], ["carrf0"], lvl=4)
    tap("S32i", S32[0][:, :], ["S32_0"], lvl=4)
    mtbf_rd = [o for dc2 in range(8) for o in _B[f"mtbf{dc2}"].readers]
    for s in range(2):
        for h in range(NH):
            if h % 2 == 0:
                S.add("dve", lambda e, s=s, h=h: e.memset(
                    qfm[s][h][32:64, :], 0.0), writes=[buf(f"qfm{s}_{h}")])
            else:
                S.add("dve", lambda e, s=s, h=h: e.memset(
                    qfm[s][h][0:32, :], 0.0), writes=[buf(f"qfm{s}_{h}")])
    prev_a_evacs = []
    for j in range(NCH):
        MARK[f"C{j}"] = len(S.ops)
        sj = j % 2
        so = 8 * sj
        ks = j % 2
        S.add("sp", lambda e, j=j, ks=ks: e.dma_start(
            out=ketm[ks][:, :], in_=kef_s[:, CH * j:CH * j + 128], transpose=True),
            reads=[buf("scr_kef")], writes=[buf(f"ketm{ks}")], is_dma=True)
        pbq = buf("bk_pb0")
        for dc in range(8):
            S.add("pe", lambda e, j=j, dc=dc: e.matmul(
                PB[0][:, 0:NR], xfm[dc][:, CH * j:CH * j + 128], wtq[dc][:, :],
                start=(dc == 0), stop=(dc == 7)),
                reads=[buf(f"xfm{dc}"), buf(f"wtq{dc}")], writes=[pbq],
                excl=(dc == 0))
        S.add("act", lambda e, sj=sj: e.activation(
            out=etm[sj][:, :], in_=PB[0][:, 0:NR], func=AF.Exp),
            reads=[pbq], writes=[buf(f"etm{sj}")])
        for half in range(2):
            pbv = buf(f"bk_pb{2 + half}")
            for dc in range(8):
                S.add("pe", lambda e, j=j, dc=dc, half=half: e.matmul(
                    PB[2 + half][:, 0:512], xfm[dc][:, CH * j:CH * j + 128],
                    wvT[dc][:, 512 * half:512 * half + 512],
                    start=(dc == 0), stop=(dc == 7)),
                    reads=[buf(f"xfm{dc}"), buf(f"wvT{dc}")], writes=[pbv],
                    excl=(dc == 0),
                    deps=(woT_ops if (j == 0 and dc == 0 and half == 0) else ()))
            S.add("dve", lambda e, half=half, sj=sj: e.tensor_copy(
                out=vtm[sj][:, 512 * half:512 * half + 512],
                in_=PB[2 + half][:, 0:512]),
                reads=[pbv], writes=[buf(f"vtm{sj}")], excl=(half == 0),
                deps=(mtbf_rd if j < 2 else ()))
        pbz = buf("bk_pb1")
        zop1 = S.add("pe", lambda e, j=j, ks=ks: e.matmul(
            PB[1][:, 0:NR], u8t[:, :], ketm[ks][:, :], start=True, stop=False),
            reads=[buf("u8t"), buf(f"ketm{ks}")], writes=[pbz],
            deps=prev_a_evacs)
        zop2 = S.add("pe", lambda e, j=j: e.matmul(
            PB[1][:, 0:NR], onesr[:, :], carrb[j % 2][:, :],
            start=False, stop=True),
            reads=[buf("onesr"), buf(f"carrb{j % 2}")], writes=[pbz], excl=False)
        r127op = None
        S.add("dve", lambda e, sj=sj: e.tensor_reduce(
            out=hsb[sj][:, :],
            in_=etm[sj].rearrange("p (h c) -> p h c", c=32)[:, :, 0:16],
            axis=AX.X, op=ALU.add),
            reads=[buf(f"etm{sj}")], writes=[buf(f"hsb{sj}")], strict=True)
        dop = S.add("dve", lambda e, sj=sj: e.tensor_tensor(
            out=dbf.rearrange("p (h z) -> p h z", z=32),
            in0=PB[1][:, 0:NR].rearrange("p (h z) -> p h z", z=32),
            in1=hsb[sj].rearrange("p (h o) -> p h o", o=1).broadcast_to(
                [128, 16, 32]),
            op=ALU.mult),
            reads=[pbz, buf(f"hsb{sj}")], writes=[buf("dbf")],
            deps=(mtbf_rd if j == 0 else ()))
        S.add("act", lambda e: e.activation(out=lnd[:, :], in_=dbf[:, :],
                                            func=AF.Ln),
              reads=[buf("dbf")], writes=[buf("lnd")])
        S.add("dve", lambda e: e.tensor_tensor(
            out=dbf[:, :], in0=PB[0][:, 0:NR], in1=lnd[:, :], op=ALU.subtract),
            reads=[pbq, buf("lnd")], writes=[buf("dbf")])
        S.add("act", lambda e, sj=sj: e.activation(
            out=qtm[sj][:, :], in_=dbf[:, :], func=AF.Exp),
            reads=[buf("dbf")], writes=[buf(f"qtm{sj}")])
        S.add("pe", lambda e, j=j, ks=ks: e.matmul(
            PB[0][0:1, 0:NR], ones1[:, :], ketm[ks][:, :],
            start=True, stop=True),
            reads=[buf("ones1"), buf(f"ketm{ks}")], writes=[pbq])
        S.add("dve", lambda e, j=j: e.scalar_tensor_tensor(
            out=carrf[(j + 1) % 2][:, :], in0=PB[0][0:1, 0:NR], scalar=8.0,
            in1=carrf[j % 2][:, :], op0=ALU.mult, op1=ALU.add),
            reads=[pbq, buf(f"carrf{j % 2}")],
            writes=[buf(f"carrf{(j + 1) % 2}")])
        S.add("act", lambda e, j=j: e.copy(out=carrb[(j + 1) % 2][:, :],
                                           in_=carrf[(j + 1) % 2][:, :]),
              reads=[buf(f"carrf{(j + 1) % 2}")],
              writes=[buf(f"carrb{(j + 1) % 2}")])
        ptq = buf("bk_pt")
        for h in range(NH):
            p, lo = h >> 1, 32 * (h & 1)
            S.add("pe", lambda e, sj=sj, h=h, p=p, lo=lo: e.transpose(
                PT[lo:lo + 32, 128 * p:128 * p + 128],
                qtm[sj][:, 32 * h:32 * h + 32], ident[:, :]),
                reads=[buf(f"qtm{sj}"), b_id], writes=[ptq], excl=(h == 0))
        for h in range(NH):
            p, lo = h >> 1, 32 * (h & 1)
            S.add("dve", lambda e, sj=sj, h=h, p=p, lo=lo: e.tensor_copy(
                out=qfm[sj][h][lo:lo + 32, :],
                in_=PT[lo:lo + 32, 128 * p:128 * p + 128]),
                reads=[ptq], writes=[buf(f"qfm{sj}_{h}")], excl=False)
        a_evacs = []
        for hb in range(2):
            pds_b = buf("bk_pds")
            for h in range(8 * hb, 8 * hb + 8):
                p, off, vc = h >> 1, 32 * (h & 1), 64 * h
                sa = h % 2
                pa = buf(f"pa{h % 4}")
                S.add("pe", lambda e, j=j, p=p, h=h: e.matmul(
                    PB[1][:, 128 * (h % 4):128 * (h % 4) + 128],
                    kefm[p][0:64, CH * j:CH * j + 128],
                    qfm[j % 2][h][0:64, :], start=True, stop=True),
                    reads=[buf(f"kefm{p}"), buf(f"qfm{j % 2}_{h}")], writes=[pa],
                    deps=([dop, zop1, zop2] if h < 4 else []))
                sa4 = h % 4
                aev = S.add("dve", lambda e, h=h, sa4=sa4: e.scalar_tensor_tensor(
                    out=asb[sa4][:, :],
                    in0=PB[1][:, 128 * (h % 4):128 * (h % 4) + 128],
                    scalar=1.0, in1=m1t[:, :], op0=ALU.mult, op1=ALU.mult),
                    reads=[pa, buf("m1t")], writes=[buf(f"asb{sa4}")])
                a_evacs.append(aev)
                ti = h >> 1
                pxo = buf(f"bk_pxo{ti // 4}")
                S.add("pe", lambda e, h=h, vc=vc, ti=ti, j=j: e.matmul(
                    PXO[ti // 4][64 * (h & 1):64 * (h & 1) + 64,
                                 128 * (ti % 4):128 * (ti % 4) + 128],
                    vtm[j % 2][:, vc:vc + 64], asb[h % 4][:, :],
                    start=True, stop=False),
                    reads=[buf(f"vtm{j % 2}"), buf(f"asb{h % 4}")], writes=[pxo],
                    excl=((h & 1) == 0))
                S.add("pe", lambda e, h=h, p=p, vc=vc, ti=ti, j=j:
                      e.matmul(
                    PXO[ti // 4][64 * (h & 1):64 * (h & 1) + 64,
                                 128 * (ti % 4):128 * (ti % 4) + 128],
                    Sbf[j % 2][0:64, vc:vc + 64],
                    qfm[j % 2][h][0:64, :],
                    start=False, stop=True),
                    reads=[buf(f"Sbf_{j % 2}"), buf(f"qfm{j % 2}_{h}")],
                    writes=[pxo], excl=False)
                S.add("pe", lambda e, h=h, vc=vc, j=j, ks=ks, hb=hb: e.matmul(
                    PDS[0:32, 64 * h - 512 * hb:64 * h - 512 * hb + 64],
                    ketm[ks][:, 32 * h:32 * h + 32], vtm[j % 2][:, vc:vc + 64],
                    start=True, stop=True),
                    reads=[buf(f"ketm{ks}"), buf(f"vtm{j % 2}")], writes=[pds_b],
                    excl=(h == 8 * hb))
                S.add("pe", lambda e, h=h, vc=vc, j=j, ks=ks, hb=hb: e.matmul(
                    PDS[32:64, 64 * h - 512 * hb:64 * h - 512 * hb + 64],
                    ketm[ks][:, 32 * h:32 * h + 32], vtm[j % 2][:, vc:vc + 64],
                    start=True, stop=True),
                    reads=[buf(f"ketm{ks}"), buf(f"vtm{j % 2}")], writes=[pds_b],
                    excl=False)
            S.add("dve", lambda e, j=j, hb=hb: e.tensor_tensor(
                out=S32[(j + 1) % 2][0:64, 512 * hb:512 * hb + 512],
                in0=S32[j % 2][0:64, 512 * hb:512 * hb + 512], in1=PDS[:, :],
                op=ALU.add),
                reads=[buf(f"S32_{j % 2}"), pds_b],
                writes=[buf(f"S32_{(j + 1) % 2}")], excl=(hb == 0))
        prev_a_evacs = a_evacs[-4:] + [dop]
        S.add("act", lambda e, j=j: e.copy(out=Sbf[(j + 1) % 2][:, :],
                                           in_=S32[(j + 1) % 2][:, :]),
              reads=[buf(f"S32_{(j + 1) % 2}")],
              writes=[buf(f"Sbf_{(j + 1) % 2}")])
        for ti in range(8):
            S.add("dve", lambda e, ti=ti: e.tensor_copy(
                out=xofm[ti][:, :],
                in_=PXO[ti // 4][:, 128 * (ti % 4):128 * (ti % 4) + 128]),
                reads=[buf(f"bk_pxo{ti // 4}")], writes=[buf(f"xofm{ti}")])
        for half in range(2):
            pbo = buf(f"bk_pb{2 + half}")
            for ti in range(8):
                S.add("pe", lambda e, ti=ti, half=half: e.matmul(
                    PB[2 + half][:, 0:512], xofm[ti][:, :],
                    woT[ti][:, 512 * half:512 * half + 512],
                    start=(ti == 0), stop=(ti == 7)),
                    reads=[buf(f"xofm{ti}"), buf(f"woT{ti}")], writes=[pbo],
                    excl=(ti == 0))
        S.add("pool", lambda e, j=j, sj=sj: e.dma_start(
            out=xres[sj][:, :], in_=x_d[CH * j:CH * (j + 1), :]),
            writes=[buf(f"xres{sj}")], is_dma=True,
            deps=([o for dc2 in range(8) for o in _B[f"mtbf{dc2}"].readers]
                  if j < 2 else ()))
        for half in range(2):
            S.add("dve", lambda e, half=half, sj=sj: e.tensor_tensor(
                out=pre1[:, 512 * half:512 * half + 512],
                in0=PB[2 + half][:, 0:512],
                in1=xres[sj][:, 512 * half:512 * half + 512], op=ALU.add),
                reads=[buf(f"bk_pb{2 + half}"), buf(f"xres{sj}")],
                writes=[buf("pre1")], excl=(half == 0))
        if j == 0:
            tap("etm0", etm[0][:, :], ["etm0"], lvl=2)
            tap("hsb0", hsb[0][:, :], ["hsb0"], lvl=2)
            tap("qtm0", qtm[0][:, :], ["qtm0"], lvl=2)
            tap("ketm0", ketm[0][:, :], ["ketm0"], lvl=2)
            tap("vtm0", vtm[0][:, :], ["vtm0"], lvl=2)
            tap("qfm0", qfm[0][0][0:64, :], ["qfm0_0"], lvl=2)
            tap("qfm1", qfm[0][1][0:64, :], ["qfm0_1"], lvl=2)
            tap("asb0", asb[0][:, :], ["asb0"], lvl=2)
            tap("xofm0", xofm[0][:, :], ["xofm0"], lvl=2)
            tap("S32n", S32[1][:, :], ["S32_1"], lvl=2)
            tap("pre1", pre1[:, :], ["pre1"], lvl=2)
        _emit_ln(S, buf, pre1, sqb, stat, so, ln1, "ln1", "pre1", "sqb")
        if j == 0:
            tap("ln1", ln1[:, :], ["ln1"], lvl=2)
        S.add("act", lambda e, sj=sj: e.copy(out=ln1b[sj][:, :], in_=ln1[:, :]),
              reads=[buf("ln1")], writes=[buf(f"ln1b{sj}")])
        S.add("pool", lambda e, j=j: e.dma_start(
            out=lntm_s[CH * j:CH * (j + 1), :], in_=ln1[:, :]),
            reads=[buf("ln1")], writes=[buf("scr_lntm")], is_dma=True,
            excl=False, dkey="lntm")
        S.add("pool", lambda e, j=j, sj=sj: e.dma_start(
            out=lnb_s[CH * j:CH * (j + 1), :], in_=ln1b[sj][:, :]),
            reads=[buf(f"ln1b{sj}")], writes=[buf("scr_lnb")], is_dma=True,
            excl=False, dkey=f"lnb{sj}")

    MARK["P4"] = len(S.ops)
    xfm_readers = [o for dc in range(8) for o in _B[f"xfm{dc}"].readers]
    lnfm_ops = []
    for dc in range(8):
        lnfm_ops.append(S.add("sp", lambda e, dc=dc: e.dma_start(
            out=lnfm[dc][:, :], in_=lnb_s[:, 128 * dc:128 * dc + 128],
            transpose=True),
            reads=[buf("scr_lnb")], writes=[buf(f"lnfm{dc}")], is_dma=True,
            deps=xfm_readers, dkey="lnfmT"))
    wtqk_readers = [o for i in range(8)
                    for o in (_B[f"wtq{i}"].readers + _B[f"wtk{i}"].readers)]
    scanB_readers = []
    for nm in (["xres0", "xres1", "asb0", "asb1", "asb2", "asb3"] +
               [f"xofm{i}" for i in range(8)] +
               [f"qfm{s}_{h}" for s in range(2) for h in range(16)]):
        scanB_readers += _B[nm].readers + _B[nm].writers
    for hg in range(6):
        sl = hg % 2
        slot_ops = []
        for dc in range(8):
            dep0 = []
            if hg == 0:
                dep0 = wtqk_readers
            elif hg == 1:
                dep0 = scanB_readers
            slot_ops.append(S.add("sp", lambda e, hg=hg, sl=sl, dc=dc: e.dma_start(
                out=wgT[sl][dc][:, :],
                in_=wgb_s[512 * hg:512 * (hg + 1), 128 * dc:128 * dc + 128],
                transpose=True),
                reads=[buf("scr_wg")], writes=[buf(f"wgT{sl}_{dc}")],
                is_dma=True, deps=dep0, dkey=f"wgT{sl}"))
            slot_ops.append(S.add("sp", lambda e, hg=hg, sl=sl, dc=dc: e.dma_start(
                out=wuT[sl][dc][:, :],
                in_=wub_s[512 * hg:512 * (hg + 1), 128 * dc:128 * dc + 128],
                transpose=True),
                reads=[buf("scr_wu")], writes=[buf(f"wuT{sl}_{dc}")],
                is_dma=True, deps=dep0, dkey=f"wuT{sl}"))
        for hc in range(4):
            hh = 4 * hg + hc
            for ts2 in range(4):
                pg, pu = buf("bk_pb0"), buf("bk_pb1")
                for dc in range(8):
                    S.add("pe", lambda e, sl=sl, hc=hc, ts2=ts2, dc=dc: e.matmul(
                        PB[0][:, 0:512], wgT[sl][dc][:, 128 * hc:128 * hc + 128],
                        lnfm[dc][:, 512 * ts2:512 * ts2 + 512],
                        start=(dc == 0), stop=(dc == 7)),
                        reads=[buf(f"wgT{sl}_{dc}"), buf(f"lnfm{dc}")],
                        writes=[pg], excl=(dc == 0),
                        deps=(slot_ops + lnfm_ops) if (hc == 0 and ts2 == 0
                                                       and dc == 0) else ())
                for dc in range(8):
                    S.add("pe", lambda e, sl=sl, hc=hc, ts2=ts2, dc=dc: e.matmul(
                        PB[1][:, 0:512], wuT[sl][dc][:, 128 * hc:128 * hc + 128],
                        lnfm[dc][:, 512 * ts2:512 * ts2 + 512],
                        start=(dc == 0), stop=(dc == 7)),
                        reads=[buf(f"wuT{sl}_{dc}"), buf(f"lnfm{dc}")],
                        writes=[pu], excl=(dc == 0))
                S.add("act", lambda e: e.activation(
                    out=silb[:, 0:512], in_=PB[0][:, 0:512], func=AF.Silu,
                    scale=QS),
                    reads=[pg], writes=[buf("silb")],
                    deps=(scanB_readers if (hg == 0 and hc == 0 and ts2 == 0)
                          else ()))
                S.add("dve", lambda e, hh=hh, ts2=ts2: e.tensor_tensor(
                    out=hpr[hh][:, 512 * ts2:512 * ts2 + 512],
                    in0=silb[:, 0:512], in1=PB[1][:, 0:512], op=ALU.mult),
                    reads=[buf("silb"), pu], writes=[buf(f"hpr{hh}")],
                    excl=False)
    tap("lnfm0", lnfm[0][:, 0:512], ["lnfm0"], lvl=2)
    tap("hpr0", hpr[0][:, 0:512], ["hpr0"], lvl=2)
    tap("hpr16", hpr[16][:, 0:512], ["hpr16"], lvl=2)
    wvwo_readers = [o for dc in range(8)
                    for o in (_B[f"wvT{dc}"].readers + _B[f"woT{dc}"].readers)]
    sket_readers = (_B["ketm0"].readers + _B["ketm1"].readers +
                    _B["S32_0"].readers + _B["S32_1"].readers +
                    _B["Sbf_0"].readers + _B["Sbf_1"].readers)
    wdT_ops = []
    for hh in range(24):
        wdT_ops.append(S.add("sp", lambda e, hh=hh: e.dma_start(
            out=wdT[hh][:, :], in_=wdb_s[:, 128 * hh:128 * hh + 128],
            transpose=True),
            reads=[buf("scr_wd")], writes=[buf(f"wdT{hh}")], is_dma=True,
            deps=((sket_readers if hh == 0 else _B["silb"].readers)
                  if hh < 2 else
                  (wvwo_readers if hh == 2 else
                   ([o for sl2 in range(1) for dc2 in range(8)
                     for o in _B[f"wgT0_{dc2}"].readers +
                     _B[f"wuT0_{dc2}"].readers] if hh == 18 else []))),
            dkey=f"wdT{hh % 3}"))
    yops = []
    for tj in range(NCH):
        sj = tj % 2
        so = 8 * sj
        S.add("pool", lambda e, tj=tj, sj=sj: e.dma_start(
            out=ltm[sj][:, :], in_=lntm_s[CH * tj:CH * (tj + 1), :]),
            reads=[buf("scr_lntm")], writes=[buf(f"ltm{sj}")], is_dma=True,
            deps=([o for dc2 in range(8) for o in _B[f"lnfm{dc2}"].readers]
                  if tj < 2 else []))
        for half in range(2):
            pbo = buf(f"bk_pb{2 + half}")
            for hh in range(24):
                S.add("pe", lambda e, tj=tj, half=half, hh=hh: e.matmul(
                    PB[2 + half][:, 0:512], hpr[hh][:, CH * tj:CH * tj + 128],
                    wdT[hh][:, 512 * half:512 * half + 512],
                    start=(hh == 0), stop=(hh == 23)),
                    reads=[buf(f"hpr{hh}"), buf(f"wdT{hh}")], writes=[pbo],
                    excl=(hh == 0),
                    deps=(wdT_ops if (tj == 0 and half == 0 and hh == 0)
                          else ()))
            S.add("dve", lambda e, half=half, sj=sj: e.scalar_tensor_tensor(
                out=pr2[:, 512 * half:512 * half + 512],
                in0=PB[2 + half][:, 0:512], scalar=QS2,
                in1=ltm[sj][:, 512 * half:512 * half + 512],
                op0=ALU.mult, op1=ALU.add),
                reads=[buf(f"bk_pb{2 + half}"), buf(f"ltm{sj}")],
                writes=[buf("pr2")], excl=(half == 0))
        if tj == 0:
            tap("ltm0", ltm[0][:, :], ["ltm0"], lvl=2)
            tap("pr2", pr2[:, :], ["pr2"], lvl=2)
        _emit_ln(S, buf, pr2, sq2, stat, so, outb[sj], f"outb{sj}", "pr2", "sq2")
        yops.append(S.add("pool", lambda e, tj=tj, sj=sj: e.dma_start(
            out=y_d[CH * tj:CH * (tj + 1), :], in_=outb[sj][:, :]),
            reads=[buf(f"outb{sj}")], writes=[buf("yout")], is_dma=True,
            excl=False, dkey=f"y{sj}"))
    S.add("sp", lambda e: e.nop(), deps=yops + tap_list)
    MARK["ALL"] = len(S.ops)

    import builtins
    builtins._KMARK = dict(MARK)
    if stop is not None:
        k = MARK[stop] if isinstance(stop, str) else stop
        S.ops = S.ops[:k]
        last = {}
        for op in S.ops:
            last[op.engine] = op
        S.add("sp", lambda e: e.nop(),
              deps=[o for o in last.values()])

    with nc.Block() as block:
        S.finalize(nc, block, ctx)
    ctx.close()
    return nc


def _emit_ln(S, buf, src, sq, stat, so, dst, dst_name, src_name, sq_name):
    st = buf(f"stat{so}")
    S.add("dve", lambda e: e.tensor_reduce(
        out=stat[:, so:so + 1], in_=src[:, :], axis=AX.X, op=ALU.add),
        reads=[buf(src_name)], writes=[st], strict=True)
    S.add("act", lambda e: e.activation(out=sq[:, :], in_=src[:, :],
                                        func=AF.Square),
          reads=[buf(src_name)], writes=[buf(sq_name)], strict=True)
    S.add("dve", lambda e: e.tensor_reduce(
        out=stat[:, so + 1:so + 2], in_=sq[:, :], axis=AX.X, op=ALU.add),
        reads=[buf(sq_name)], writes=[st], excl=False, strict=True)
    S.add("dve", lambda e: e.tensor_scalar(
        out=stat[:, so + 2:so + 3], in0=stat[:, so:so + 1], scalar1=1.0 / D,
        scalar2=None, op0=ALU.mult),
        reads=[st], writes=[st], excl=False, strict=True)
    S.add("dve", lambda e: e.tensor_scalar(
        out=stat[:, so + 3:so + 4], in0=stat[:, so + 2:so + 3], scalar1=-1.0,
        scalar2=None, op0=ALU.mult),
        reads=[st], writes=[st], excl=False, strict=True)
    S.add("dve", lambda e: e.tensor_scalar(
        out=stat[:, so + 4:so + 5], in0=stat[:, so + 1:so + 2], scalar1=1.0 / D,
        scalar2=EPS_LN, op0=ALU.mult, op1=ALU.add),
        reads=[st], writes=[st], excl=False, strict=True)
    S.add("dve", lambda e: e.scalar_tensor_tensor(
        out=stat[:, so + 5:so + 6], in0=stat[:, so + 2:so + 3],
        scalar=stat[:, so + 2:so + 3], in1=stat[:, so + 4:so + 5],
        op0=ALU.mult, op1=ALU.subtract),
        reads=[st], writes=[st], excl=False, strict=True)
    S.add("act", lambda e: e.activation(
        out=stat[:, so + 6:so + 7], in_=stat[:, so + 5:so + 6], func=AF.Sqrt,
        scale=-1.0),
        reads=[st], writes=[st], excl=False, strict=True)
    S.add("dve", lambda e: e.reciprocal(
        out=stat[:, so + 7:so + 8], in_=stat[:, so + 6:so + 7]),
        reads=[st], writes=[st], excl=False, strict=True)
    S.add("dve", lambda e: e.tensor_scalar(
        out=dst[:, :], in0=src[:, :], scalar1=stat[:, so + 3:so + 4],
        scalar2=stat[:, so + 7:so + 8], op0=ALU.add, op1=ALU.mult),
        reads=[buf(src_name), st], writes=[buf(dst_name)], strict=True)


_NC_CACHE = {}


def kernel(**inputs):
    x = np.ascontiguousarray(np.asarray(inputs["L0_attn_x"]).reshape(T, D))
    wqkv = np.ascontiguousarray(np.asarray(inputs["L0_attn_w_qkv"]))
    wout = np.ascontiguousarray(np.asarray(inputs["L0_attn_w_out"]))
    code = np.ascontiguousarray(np.asarray(inputs["L0_attn_fc_code"])
                                .reshape(NH * CS, HD))
    wg = np.ascontiguousarray(np.asarray(inputs["L0_ffn_w_gate"], np.int32))
    wu = np.ascontiguousarray(np.asarray(inputs["L0_ffn_w_up"], np.int32))
    wd = np.ascontiguousarray(np.asarray(inputs["L0_ffn_w_down"], np.int32))

    ident = np.eye(128, dtype=np.float32).astype(ml_dtypes.bfloat16)
    st = np.arange(128)
    u8 = (8.0 * (st[:, None] <= st[None, :])).astype(ml_dtypes.bfloat16)
    m1 = (st[:, None] <= st[None, :]).astype(np.float32)
    onesr = np.ones((1, 128), dtype=ml_dtypes.bfloat16)
    ones1 = np.ones((128, 1), dtype=ml_dtypes.bfloat16)
    rr = np.arange(64)
    padv = np.where((rr % 32) >= 16, 1e9, -3.0e38).astype(np.float32)[:, None]

    if "nc" not in _NC_CACHE:
        _NC_CACHE["nc"] = build_nc()
    nc = _NC_CACHE["nc"]

    in_maps = []
    for c in range(NCORES):
        rvec = np.zeros((33, 8), dtype=np.float32)
        rvec[:, :c] = 1.0
        in_maps.append({
            "x": np.asarray(x[TLOC * c:TLOC * (c + 1)], np.float32),
            "w_qkv": np.asarray(wqkv, np.float32),
            "w_out": np.asarray(wout, np.float32),
            "fc_code": np.asarray(code, np.float32),
            "w_gate": wg, "w_up": wu, "w_down": wd,
            "c_ident": ident, "c_u8": u8, "c_m1": m1, "c_onesr": onesr, "c_ones1": ones1,
            "c_padv": padv, "c_rvec": rvec,
        })
    res = run_bass_kernel_spmd(nc, in_maps, core_ids=list(range(NCORES)))
    out = np.concatenate([np.asarray(r["y"]) for r in res.results], axis=0)
    return out.reshape(1, T, D)


# version 11
# speedup vs baseline: 1.1164x; 1.1164x over previous
import numpy as np
import ml_dtypes
import concourse.bass as bass
import concourse.mybir as mybir
from concourse.dt import dt
from concourse.bass_utils import run_bass_kernel_spmd
from contextlib import ExitStack

F32 = dt.float32
BF16 = dt.bfloat16
I32 = dt.int32
AF = mybir.ActivationFunctionType
ALU = mybir.AluOpType
AX = mybir.AxisListType

NCORES = 8
T = 16384
TLOC = T // NCORES
D = 1024
NH = 16
HD = 64
CS = 16
HID = 3072
CH = 128
NCH = TLOC // CH
SCALE = 0.125
QS = 1.0 / 127.0
QS2 = QS * QS
EPS_Z = 1e-9
EPS_LN = 1e-5
NR = 512
ENGS = ("pe", "act", "dve", "pool", "sp")


class Op:
    __slots__ = ("engine", "emit", "deps", "inc", "val", "is_dma", "sem",
                 "needed", "strict")

    def __init__(self, engine, emit, deps, is_dma, sem, strict):
        self.engine = engine
        self.emit = emit
        self.deps = deps
        self.inc = 0
        self.val = None
        self.is_dma = is_dma
        self.sem = sem
        self.needed = False
        self.strict = strict


class Buf:
    __slots__ = ("writers", "readers", "name")

    def __init__(self, name):
        self.writers = []
        self.readers = []
        self.name = name


class Sched:
    def __init__(self):
        self.ops = []

    def add(self, engine, emit, reads=(), writes=(), deps=(), is_dma=False,
            excl=True, dkey=None, strict=False):
        dd = list(deps)
        for b in reads:
            dd += b.writers
        for b in writes:
            dd += b.readers + b.writers
        sem = None
        if is_dma:
            sem = "dma:" + (dkey or (writes[0].name if writes else "misc"))
        op = Op(engine, emit, dd, is_dma, sem, strict)
        self.ops.append(op)
        for b in reads:
            b.readers.append(op)
        for b in writes:
            if excl:
                b.writers = [op]
                b.readers = []
            else:
                b.writers.append(op)
        return op

    def finalize(self, nc, block, ctx):
        for op in self.ops:
            for d in op.deps:
                d.needed = True
        sem_names = list(ENGS)
        for op in self.ops:
            if op.is_dma and op.sem not in sem_names:
                sem_names.append(op.sem)
        sems = {n: ctx.enter_context(nc.semaphore(f"s{i}"))
                for i, n in enumerate(sem_names)}
        counts = {n: 0 for n in sem_names}
        prefix = {}
        for idx, op in enumerate(self.ops):
            if op.is_dma:
                op.inc = 16
            elif op.needed:
                op.inc = 1
                op.sem = op.engine
            if op.inc:
                counts[op.sem] += op.inc
                op.val = counts[op.sem]
                if op.is_dma:
                    prefix.setdefault(op.sem, []).append((idx, op.val))
        import bisect as _bis

        def dma_wait_val(sem, cidx):
            lst = prefix[sem]
            k = _bis.bisect_left(lst, (cidx, -1))
            return lst[k - 1][1] if k else 0

        idx_of = {id(op): i for i, op in enumerate(self.ops)}

        def body(eng_handle, ename):
            waited = {}
            for op in self.ops:
                if op.engine != ename:
                    continue
                cidx = idx_of[id(op)]
                want = {}
                for d in op.deps:
                    if d.inc == 0:
                        continue
                    if d.engine == ename and not d.is_dma and not d.strict:
                        continue
                    v = dma_wait_val(d.sem, cidx) if d.is_dma else d.val
                    if v > want.get(d.sem, 0):
                        want[d.sem] = v
                for se, v in want.items():
                    if waited.get(se, 0) >= v:
                        continue
                    eng_handle.wait_ge(sems[se], v)
                    waited[se] = v
                ins = op.emit(eng_handle)
                if op.inc:
                    ins.then_inc(sems[op.sem], op.inc)

        block.tensor(lambda e: body(e, "pe"))
        block.scalar(lambda e: body(e, "act"))
        block.vector(lambda e: body(e, "dve"))
        block.gpsimd(lambda e: body(e, "pool"))
        block.sync(lambda e: body(e, "sp"))


def build_nc(stop=None, taps=False):
    nc = bass.Bass("TRN2", target_bir_lowering=False,
                   dynamic_dma_scratch_size=16384,
                   detect_race_conditions=False)
    x_d = nc.dram_tensor("x", [TLOC, D], F32, kind="ExternalInput")
    wqkv_d = nc.dram_tensor("w_qkv", [3 * D, D], F32, kind="ExternalInput")
    wout_d = nc.dram_tensor("w_out", [D, D], F32, kind="ExternalInput")
    code_d = nc.dram_tensor("fc_code", [NH * CS, HD], F32, kind="ExternalInput")
    wg_d = nc.dram_tensor("w_gate", [HID, D], I32, kind="ExternalInput")
    wu_d = nc.dram_tensor("w_up", [HID, D], I32, kind="ExternalInput")
    wd_d = nc.dram_tensor("w_down", [D, HID], I32, kind="ExternalInput")
    ident_d = nc.dram_tensor("c_ident", [128, 128], BF16, kind="ExternalInput")
    u8_d = nc.dram_tensor("c_u8", [128, 128], BF16, kind="ExternalInput")
    onesr_d = nc.dram_tensor("c_onesr", [1, 128], BF16, kind="ExternalInput")
    m1_d = nc.dram_tensor("c_m1", [128, 128], F32, kind="ExternalInput")
    ones1_d = nc.dram_tensor("c_ones1", [128, 1], BF16, kind="ExternalInput")
    padv_d = nc.dram_tensor("c_padv", [64, 1], F32, kind="ExternalInput")
    rvec_d = nc.dram_tensor("c_rvec", [33, 8], F32, kind="ExternalInput")
    y_d = nc.dram_tensor("y", [TLOC, D], F32, kind="ExternalOutput")

    xb_s = nc.dram_tensor("s_xb", [TLOC, D], BF16, kind="Internal")
    wvb_s = nc.dram_tensor("s_wvb", [D, D], BF16, kind="Internal")
    wob_s = nc.dram_tensor("s_wob", [D, D], BF16, kind="Internal")
    wgb_s = nc.dram_tensor("s_wgb", [HID, D], BF16, kind="Internal")
    wub_s = nc.dram_tensor("s_wub", [HID, D], BF16, kind="Internal")
    wdb_s = nc.dram_tensor("s_wdb", [D, HID], BF16, kind="Internal")
    kef_s = nc.dram_tensor("s_kef", [NR, TLOC], BF16, kind="Internal")
    lnb_s = nc.dram_tensor("s_lnb", [TLOC, D], BF16, kind="Internal")
    lntm_s = nc.dram_tensor("s_lntm", [TLOC, D], F32, kind="Internal")
    dup_s = nc.dram_tensor("s_dup", [33, 1024], F32, kind="Internal")
    ccm_in = nc.dram_tensor("cc_m_in", [64, 8], F32, kind="Internal")
    ccm_out = nc.dram_tensor("cc_m_out", [64, 8], F32, kind="Internal")
    ccs_in = nc.dram_tensor("cc_s_in", [33, 1024], F32, kind="Internal")
    ccs_out = nc.dram_tensor("cc_s_out", [33 * 8, 1024], F32, kind="Internal")

    ctx = ExitStack()

    def sb(name, shape, dtp):
        return ctx.enter_context(nc.sbuf_tensor(name, shape, dtp))

    def psum(name, shape, dtp):
        return ctx.enter_context(nc.psum_tensor(name, shape, dtp))

    ident = sb("ident", [128, 128], BF16)
    u8t = sb("u8t", [128, 128], BF16)
    m1t = sb("m1t", [128, 128], F32)
    onesr = sb("onesr", [1, 128], BF16)
    ones1 = sb("ones1", [128, 1], BF16)
    padv = sb("padv", [64, 1], F32)
    rvec = sb("rvec", [33, 8], F32)
    negm = sb("negm", [64, 8], F32)
    mred = sb("mred", [64, 4], F32)
    mloc = sb("mloc", [64, 8], F32)
    carrf = [sb(f"carrf{i}", [1, NR], F32) for i in range(2)]
    carrb = [sb(f"carrb{i}", [1, NR], BF16) for i in range(2)]
    row127 = sb("row127", [1, NR], F32)
    hsb = [sb(f"hsb{i}", [128, 16], F32) for i in range(2)]
    stat = sb("stat", [128, 16], F32)

    ARK = sb("ARK", [128, 16384], BF16)
    ARB = sb("ARB", [128, 16384], BF16)
    ARA = sb("ARA", [128, 8192], BF16)
    ARG = sb("ARG", [128, 8192], BF16)
    ARG2 = sb("ARG2", [128, 8192], BF16)
    ARD = sb("ARD", [128, 2048], BF16)
    ARE = sb("ARE", [128, 16384], BF16)
    ARF = sb("ARF", [128, 8192], BF16)
    ARH = sb("ARH", [128, 8192], BF16)
    ARI = sb("ARI", [128, 8192], BF16)

    qk_bf = [ARA[:, 1024 * i:1024 * (i + 1)] for i in range(8)]
    wst = [ARG[:, 0:2048].bitcast(F32), ARG[:, 2048:4096].bitcast(F32)]
    wsti = [ARG[:, 0:2048].bitcast(I32), ARG[:, 2048:4096].bitcast(I32)]
    wbf = [ARG[:, 4096:5120], ARG[:, 5120:6144]]
    xtb = [ARG[:, 6144:7168], ARG[:, 7168:8192]]
    codeb = ARG2[0:16, 0:1024]
    codeT = ARG2[:, 1024:1280]
    codeblk = [ARG2[:, 1536 + 256 * i:1536 + 256 * (i + 1)] for i in range(8)]
    xfm = [ARB[:, 2048 * i:2048 * (i + 1)] for i in range(8)]
    wtq = [ARF[:, 512 * i:512 * (i + 1)] for i in range(8)]
    wtk = [ARF[:, 4096 + 512 * i:4096 + 512 * (i + 1)] for i in range(8)]
    wvT = [ARE[:, 1024 * i:1024 * (i + 1)] for i in range(8)]
    woT = [ARE[:, 8192 + 1024 * i:8192 + 1024 * (i + 1)] for i in range(8)]
    xsb = [ARA[:, 1024 * s:1024 * (s + 1)] for s in range(2)]
    kefm = [ARK[0:64, 2048 * p:2048 * (p + 1)] for p in range(8)]
    ketm = [ARD[:, 512 * i:512 * (i + 1)] for i in range(2)]
    mtbf = [ARG[:, 512 * i:512 * (i + 1)] for i in range(8)]
    srs = ARI[:, 0:2048].bitcast(F32)[0:33, :]
    accb = [ARI[:, 2048 + 2048 * i:2048 + 2048 * (i + 1)].bitcast(F32)[0:33, :]
            for i in range(2)]
    gth = ARI[:, 6144:8192].bitcast(F32)[0:33, :]
    dbf = ARG[:, 0:1024].bitcast(F32)
    lnd = ARG[:, 1024:2048].bitcast(F32)
    vtm = [ARG[:, 2048 + 1024 * i:2048 + 1024 * (i + 1)] for i in range(2)]
    pre1 = ARG[:, 4096 + 2048:4096 + 4096].bitcast(F32)
    sqb = ARG2[:, 0:2048].bitcast(F32)
    ln1 = ARG2[:, 2048:4096].bitcast(F32)
    ln1b = [ARG2[:, 4096 + 1024 * i:4096 + 1024 * (i + 1)] for i in range(2)]
    etm = [ARG2[:, 6144 + 512 * i:6144 + 512 * (i + 1)] for i in range(2)]
    qtm = [ARG2[:, 7168 + 512 * i:7168 + 512 * (i + 1)] for i in range(2)]
    xres = [ARH[:, 2048 * i:2048 * (i + 1)].bitcast(F32) for i in range(2)]
    qfm = [[ARH[0:64, 4096 + 2048 * s + 128 * h:4096 + 2048 * s + 128 * (h + 1)]
            for h in range(16)] for s in range(2)]
    asb = [ARI[:, 6144 + 128 * i:6144 + 128 * (i + 1)] for i in range(4)]
    xofm = [ARI[:, 6656 + 128 * i:6656 + 128 * (i + 1)] for i in range(8)]
    S32 = [ARI[0:64, 2048 * i:2048 * (i + 1)].bitcast(F32) for i in range(2)]
    Sbf = [ARI[0:64, 4096 + 1024 * i:4096 + 1024 * (i + 1)] for i in range(2)]
    hpr = ([ARK[:, 2048 * i:2048 * (i + 1)] for i in range(8)] +
           [ARA[:, 2048 * i:2048 * (i + 1)] for i in range(4)] +
           [ARG[:, 2048 * i:2048 * (i + 1)] for i in range(4)] +
           [ARG2[:, 2048 * i:2048 * (i + 1)] for i in range(4)] +
           [ARI[:, 2048 * i:2048 * (i + 1)] for i in range(4)])
    lnfm = [ARB[:, 2048 * i:2048 * (i + 1)] for i in range(8)]
    wgT = [[ARF[:, 512 * i:512 * (i + 1)] for i in range(8)],
           [ARH[:, 512 * i:512 * (i + 1)] for i in range(8)]]
    wuT = [[ARF[:, 4096 + 512 * i:4096 + 512 * (i + 1)] for i in range(8)],
           [ARH[:, 4096 + 512 * i:4096 + 512 * (i + 1)] for i in range(8)]]
    wdT = ([ARD[:, 1024 * i:1024 * (i + 1)] for i in range(2)] +
           [ARE[:, 1024 * i:1024 * (i + 1)] for i in range(16)] +
           [ARF[:, 1024 * i:1024 * (i + 1)] for i in range(6)])
    ltm = [ARB[:, 2048 * i:2048 * (i + 1)].bitcast(F32) for i in range(2)]
    outb = [ARB[:, 4096 + 2048 * i:4096 + 2048 * (i + 1)].bitcast(F32)
            for i in range(2)]
    sq2 = ARB[:, 8192:10240].bitcast(F32)
    pr2 = ARB[:, 10240:12288].bitcast(F32)
    silb = ARD[:, 1024:2048].bitcast(F32)

    PB = [psum(f"PB{i}", [128, 512], F32) for i in range(4)]
    PXO = [psum(f"PXO{i}", [128, 512], F32) for i in range(2)]
    PDS = psum("PDS", [64, 512], F32)
    PT = psum("PT", [128, 1024], BF16)

    S = Sched()
    MARK = {}
    _B = {}

    def buf(name):
        if name not in _B:
            _B[name] = Buf(name)
        return _B[name]

    tap_list = []

    def tap(name, ap, bnames, lvl=1):
        if not taps or lvl > int(taps):
            return
        td = nc.dram_tensor("tap_" + name, list(ap.shape), ap.dtype,
                            kind="ExternalOutput")
        op = S.add("pool", lambda e, td=td, ap=ap: e.dma_start(
            out=td[tuple(slice(None) for _ in ap.shape)], in_=ap),
            reads=[buf(b) for b in bnames], is_dma=True, dkey="taps")
        tap_list.append(op)

    for t, d_, nm in ((ident, ident_d, "ident"), (u8t, u8_d, "u8t"),
                      (m1t, m1_d, "m1t"),
                      (onesr, onesr_d, "onesr"), (ones1, ones1_d, "ones1"),
                      (padv, padv_d, "padv"), (rvec, rvec_d, "rvec")):
        S.add("sp", lambda e, t=t, d_=d_: e.dma_start(out=t[:, :], in_=d_[:, :]),
              writes=[buf(nm)], is_dma=True, dkey="consts")
    b_id = buf("ident")
    const_ops = list(S.ops)

    for i in range(8):
        s = i % 2
        S.add("sp", lambda e, i=i, s=s: e.dma_start(
            out=wst[s][:, :], in_=wqkv_d[128 * i:128 * (i + 1), :]),
            writes=[buf(f"wst{s}")], is_dma=True)
        S.add("act", lambda e, i=i, s=s: e.copy(out=qk_bf[i][:, :], in_=wst[s][:, :]),
              reads=[buf(f"wst{s}")], writes=[buf(f"qkbf{i}")])
    S.add("sp", lambda e: e.dma_start(
        out=wst[0][0:16, 0:NH * HD].rearrange("c (h d) -> c h d", d=HD),
        in_=code_d[:, :].rearrange("(h c) d -> c h d", c=CS)),
        writes=[buf("wst0")], is_dma=True)
    S.add("act", lambda e: e.mul(out=codeb[:, :], in_=wst[0][0:16, 0:NH * HD],
                                 mul=SCALE),
          reads=[buf("wst0")], writes=[buf("codeb")], deps=const_ops)
    b_ptc = buf("bk_pt")
    for h in range(NH):
        S.add("pe", lambda e, h=h: e.transpose(
            PT[0:64, 16 * h:16 * h + 16], codeb[:, 64 * h:64 * h + 64],
            ident[0:16, 0:16]),
            reads=[buf("codeb"), b_id], writes=[b_ptc], excl=(h == 0))
        S.add("pe", lambda e, h=h: e.transpose(
            PT[64:128, 16 * h:16 * h + 16], codeb[:, 64 * h:64 * h + 64],
            ident[0:16, 0:16]),
            reads=[buf("codeb"), b_id], writes=[b_ptc], excl=False)
    S.add("dve", lambda e: e.tensor_copy(out=codeT[:, 0:NH * CS],
                                         in_=PT[:, 0:NH * CS]),
          reads=[b_ptc], writes=[buf("codeT")])
    for i in range(8):
        S.add("dve", lambda e, i=i: e.memset(wtq[i][:, :], 0.0),
              writes=[buf(f"wtq{i}")])
        S.add("dve", lambda e, i=i: e.memset(wtk[i][:, :], 0.0),
              writes=[buf(f"wtk{i}")])
    for i in range(8):
        S.add("dve", lambda e, i=i: e.memset(codeblk[i][:, :], 0.0),
              writes=[buf(f"cbk{i}")])
        S.add("dve", lambda e, i=i: e.tensor_copy(
            out=codeblk[i][0:64, 32 * i:32 * i + 16],
            in_=codeT[0:64, 32 * i:32 * i + 16]),
            reads=[buf("codeT")], writes=[buf(f"cbk{i}")], excl=False)
        S.add("dve", lambda e, i=i: e.tensor_copy(
            out=codeblk[i][64:128, 32 * i + 16:32 * i + 32],
            in_=codeT[64:128, 32 * i + 16:32 * i + 32]),
            reads=[buf("codeT")], writes=[buf(f"cbk{i}")], excl=False)
    for which, wt in (("q", wtq), ("k", wtk)):
        if which == "k":
            for i in range(8):
                s = i % 2
                S.add("sp", lambda e, i=i, s=s: e.dma_start(
                    out=wst[s][:, :],
                    in_=wqkv_d[1024 + 128 * i:1024 + 128 * (i + 1), :]),
                    writes=[buf(f"wst{s}")], is_dma=True)
                S.add("act", lambda e, i=i, s=s: e.copy(out=qk_bf[i][:, :],
                                                        in_=wst[s][:, :]),
                      reads=[buf(f"wst{s}")], writes=[buf(f"qkbf{i}")])
        for dc in range(8):
            pb = buf("bk_pb0")
            for i in range(8):
                S.add("pe", lambda e, i=i, dc=dc: e.matmul(
                    PB[0][:, 0:256],
                    qk_bf[i][:, 128 * dc:128 * dc + 128],
                    codeblk[i][:, :], start=(i == 0), stop=(i == 7)),
                    reads=[buf(f"qkbf{i}"), buf(f"cbk{i}")], writes=[pb],
                    excl=(i == 0))
            S.add("dve", lambda e, wt=wt, dc=dc: e.tensor_copy(
                out=wt[dc].rearrange("p (h c) -> p h c", c=32)[:, :, 0:16],
                in_=PB[0][:, 0:256].rearrange("p (h c) -> p h c", c=16)),
                reads=[pb], writes=[buf(f"wt{which}{dc}")], excl=False)
    wt_done = [o for i in range(8)
               for o in (_B[f"wtq{i}"].writers + _B[f"wtk{i}"].writers)]

    pst = [ARI[:, 0:2048].bitcast(I32), ARI[:, 2048:4096].bitcast(I32)]
    pbf = [ARI[:, 4096:5120], ARI[:, 5120:6144]]
    prep_jobs = []
    for nm2, src2, scr2 in (("wg", wg_d, wgb_s), ("wu", wu_d, wub_s)):
        for i2 in range(24):
            prep_jobs.append((nm2, scr2,
                              src2[128 * i2:128 * (i2 + 1), :],
                              (128 * i2, 0)))
    for i2 in range(8):
        for c3 in range(3):
            prep_jobs.append(("wd", wdb_s,
                              wd_d[128 * i2:128 * (i2 + 1),
                                   1024 * c3:1024 * (c3 + 1)],
                              (128 * i2, 1024 * c3)))
    prep_loads, prep_casts, prep_stores = [], [], []

    def prep_load(k):
        src_ap = prep_jobs[k][2]
        s2 = k % 2
        prep_loads.append(S.add("pool", lambda e, src_ap=src_ap, s2=s2:
                                e.dma_start(out=pst[s2][:, :], in_=src_ap),
                                writes=[buf(f"pst{s2}")], is_dma=True))
    pk = [0]

    def prep_step(n):
        for _ in range(n):
            k = pk[0]
            if k >= len(prep_jobs):
                return
            pk[0] += 1
            nm2, scr2, _, (r0, c0) = prep_jobs[k]
            s2 = k % 2
            prep_casts.append(S.add("pool", lambda e, s2=s2: e.tensor_copy(
                out=pbf[s2][:, :], in_=pst[s2][:, :]),
                reads=[buf(f"pst{s2}")], writes=[buf(f"pbf{s2}")]))
            prep_stores.append(S.add(
                "pool", lambda e, scr2=scr2, r0=r0, c0=c0, s2=s2: e.dma_start(
                    out=scr2[r0:r0 + 128, c0:c0 + 1024], in_=pbf[s2][:, :]),
                reads=[buf(f"pbf{s2}")], writes=[buf(f"scr_{nm2}")],
                is_dma=True, excl=False, dkey=f"pbf{s2}"))
            if k + 2 < len(prep_jobs):
                prep_load(k + 2)

    def xres_load(j):
        s = j % 2
        S.add("pool", lambda e, j=j, s=s: e.dma_start(
            out=xres[s][:, :], in_=x_d[CH * j:CH * (j + 1), :]),
            writes=[buf(f"xres{s}")], is_dma=True)
    prep_load(0)
    prep_load(1)
    xres_load(0)
    xres_load(1)
    for j in range(NCH):
        s = j % 2
        S.add("pool", lambda e, s=s: e.tensor_copy(out=xtb[s][:, :],
                                                   in_=xres[s][:, :]),
              reads=[buf(f"xres{s}")], writes=[buf(f"xtb{s}")])
        S.add("pool", lambda e, j=j, s=s: e.dma_start(
            out=xb_s[CH * j:CH * (j + 1), :], in_=xtb[s][:, :]),
            reads=[buf(f"xtb{s}")], writes=[buf("scr_x")], is_dma=True,
            excl=False, dkey=f"xtb{s}")
        if j + 2 < NCH:
            xres_load(j + 2)
        prep_step(2)
    xfm_ops = []
    for dc in range(8):
        xfm_ops.append(S.add("sp", lambda e, dc=dc: e.dma_start(
            out=xfm[dc][:, :], in_=xb_s[:, 128 * dc:128 * dc + 128],
            transpose=True),
            reads=[buf("scr_x")], writes=[buf(f"xfm{dc}")], is_dma=True,
            deps=wt_done, dkey=f"xfmT{dc % 2}"))
    prep_step(len(prep_jobs))
    prep_guard = [prep_casts[-1], prep_loads[-1], prep_loads[-2],
                  prep_stores[-1], prep_stores[-2]]

    wvwo_casts = []
    for name, scr in (("wv", wvb_s), ("wo", wob_s)):
        for i in range(8):
            s = i % 2
            if name == "wv":
                S.add("sp", lambda e, i=i, s=s: e.dma_start(
                    out=wst[s][:, :],
                    in_=wqkv_d[2048 + 128 * i:2048 + 128 * (i + 1), :]),
                    writes=[buf(f"wst{s}")], is_dma=True)
            else:
                S.add("sp", lambda e, i=i, s=s: e.dma_start(
                    out=wst[s][:, :], in_=wout_d[128 * i:128 * (i + 1), :]),
                    writes=[buf(f"wst{s}")], is_dma=True)
            wvwo_casts.append(S.add(
                "act", lambda e, s=s: e.copy(out=wbf[s][:, :], in_=wst[s][:, :]),
                reads=[buf(f"wst{s}")], writes=[buf(f"wbf{s}")]))
            S.add("sp", lambda e, scr=scr, i=i, s=s: e.dma_start(
                out=scr[128 * i:128 * (i + 1), :], in_=wbf[s][:, :]),
                reads=[buf(f"wbf{s}")], writes=[buf(f"scr_{name}")],
                is_dma=True, excl=False, dkey=f"wbf{s}")
    wvT_ops, woT_ops = [], []
    for name, scr, dstT, lst in (("wv", wvb_s, wvT, wvT_ops),
                                 ("wo", wob_s, woT, woT_ops)):
        for dc in range(8):
            lst.append(S.add("sp", lambda e, scr=scr, dstT=dstT, dc=dc: e.dma_start(
                out=dstT[dc][:, :], in_=scr[:, 128 * dc:128 * dc + 128],
                transpose=True),
                reads=[buf(f"scr_{name}")], writes=[buf(f"{name}T{dc}")],
                is_dma=True, dkey=f"{name}T"))

    MARK["P0"] = len(S.ops)
    tap("wtq0", wtq[0][:, :], ["wtq0"], lvl=1)
    tap("wtk0", wtk[0][:, :], ["wtk0"], lvl=1)
    tap("xfm0", xfm[0][:, 0:512], ["xfm0"], lvl=1)
    tap("wvT0", wvT[0][:, :], ["wvT0"], lvl=1)
    tap("woT0", woT[0][:, :], ["woT0"], lvl=1)

    for p in range(8):
        for ts in range(4):
            gi = (p * 4 + ts) % 2
            pb = buf(f"bk_pb{gi}")
            for dc in range(8):
                S.add("pe", lambda e, p=p, ts=ts, dc=dc, gi=gi: e.matmul(
                    PB[gi][0:64, 0:512], wtk[dc][:, 64 * p:64 * p + 64],
                    xfm[dc][:, 512 * ts:512 * ts + 512],
                    start=(dc == 0), stop=(dc == 7)),
                    reads=[buf(f"wtk{dc}"), buf(f"xfm{dc}")], writes=[pb],
                    excl=(dc == 0),
                    deps=(wt_done + xfm_ops) if (p == 0 and ts == 0) else ())
            S.add("dve", lambda e, p=p, ts=ts, gi=gi: e.tensor_reduce(
                out=mred[:, ts:ts + 1], in_=PB[gi][0:64, 0:512],
                axis=AX.X, op=ALU.max),
                reads=[pb], writes=[buf("mred")], excl=(ts == 0), strict=True)
        S.add("dve", lambda e, p=p: e.tensor_reduce(
            out=mloc[:, p:p + 1], in_=mred[:, :], axis=AX.X, op=ALU.max),
            reads=[buf("mred")], writes=[buf("mloc")], excl=False, strict=True)

    tap("mlocpre", mloc[:, :], ["mloc"], lvl=1)
    S.add("pool", lambda e: e.dma_start(out=ccm_in[:, :], in_=mloc[:, :]),
          reads=[buf("mloc")], writes=[buf("ccm")], is_dma=True)
    if taps and int(taps) >= 1:
        tccin = nc.dram_tensor("tap_ccmin", [64, 8], F32, kind="ExternalOutput")
        tap_list.append(S.add("act", lambda e: e.dma_start(
            out=tccin[:, :], in_=ccm_in[:, :]),
            reads=[buf("ccm")], is_dma=True, dkey="taps"))
    S.add("pool", lambda e: e.collective_compute(
        "AllReduce", ALU.max, replica_groups=[list(range(NCORES))],
        ins=[ccm_in.ap().opt()], outs=[ccm_out.ap().opt()]),
        reads=[buf("ccm")], writes=[buf("ccmo")], strict=True)
    if taps and int(taps) >= 1:
        tccout = nc.dram_tensor("tap_ccmout", [64, 8], F32, kind="ExternalOutput")
        tap_list.append(S.add("act", lambda e: e.dma_start(
            out=tccout[:, :], in_=ccm_out[:, :]),
            reads=[buf("ccmo")], is_dma=True, dkey="taps"))
    S.add("pool", lambda e: e.dma_start(out=mloc[:, :], in_=ccm_out[:, :]),
          reads=[buf("ccmo")], writes=[buf("mloc")], is_dma=True, dkey="mback")
    tap("padv2", padv[:, :], ["padv"], lvl=1)
    S.add("dve", lambda e: e.tensor_scalar(
        out=negm[:, :], in0=mloc[:, :], scalar1=padv[:, 0:1], scalar2=-1.0,
        op0=ALU.max, op1=ALU.mult),
        reads=[buf("mloc"), buf("padv")], writes=[buf("negm")])

    MARK["P1"] = len(S.ops)
    tap("mloc", mloc[:, :], ["mloc"], lvl=1)
    tap("negm", negm[:, :], ["negm"], lvl=1)
    kexp_ops = []
    for p in range(8):
        for ts in range(4):
            gi = (p * 4 + ts) % 2
            pb = buf(f"bk_pb{gi}")
            for dc in range(8):
                S.add("pe", lambda e, p=p, ts=ts, dc=dc, gi=gi: e.matmul(
                    PB[gi][0:64, 0:512], wtk[dc][:, 64 * p:64 * p + 64],
                    xfm[dc][:, 512 * ts:512 * ts + 512],
                    start=(dc == 0), stop=(dc == 7)),
                    reads=[buf(f"wtk{dc}"), buf(f"xfm{dc}")], writes=[pb],
                    excl=(dc == 0))
            o = S.add("act", lambda e, p=p, ts=ts, gi=gi: e.activation(
                out=kefm[p][:, 512 * ts:512 * ts + 512], in_=PB[gi][0:64, 0:512],
                func=AF.Exp, bias=negm[:, p:p + 1], scale=1.0),
                reads=[pb, buf("negm")], writes=[buf(f"kefm{p}")], excl=False)
            kexp_ops.append(o)
        S.add("act", lambda e, p=p: e.dma_start(
            out=kef_s[64 * p:64 * (p + 1), :], in_=kefm[p][:, :]),
            reads=[buf(f"kefm{p}")], writes=[buf("scr_kef")], is_dma=True,
            excl=False, dkey="scr_kef")

    ztot_pb = buf("bk_pds")
    mt_pb = [buf("bk_pb0"), buf("bk_pb1"), buf("bk_pb2"), buf("bk_pb3"),
             buf("bk_pxo0"), buf("bk_pxo1"), buf("bk_pb0"), buf("bk_pb1")]
    for j in range(NCH):
        ks = j % 2
        S.add("sp", lambda e, j=j, ks=ks: e.dma_start(
            out=ketm[ks][:, :], in_=kef_s[:, CH * j:CH * j + 128], transpose=True),
            reads=[buf("scr_kef")], writes=[buf(f"ketm{ks}")], is_dma=True)
        S.add("pool", lambda e, j=j, ks=ks: e.dma_start(
            out=xres[ks][:, :], in_=x_d[CH * j:CH * (j + 1), :]),
            writes=[buf(f"xres{ks}")], is_dma=True)
        S.add("act", lambda e, j=j, ks=ks: e.copy(out=xsb[ks][:, :],
                                                  in_=xres[ks][:, :]),
              reads=[buf(f"xres{ks}")], writes=[buf(f"xsb{ks}")],
              deps=(kexp_ops + wt_done) if j < 2 else ())
        S.add("pe", lambda e, j=j, ks=ks: e.matmul(
            PDS[0:1, 0:NR], ones1[:, :], ketm[ks][:, :],
            start=(j == 0), stop=(j == NCH - 1)),
            reads=[buf(f"ketm{ks}"), buf("ones1")], writes=[ztot_pb],
            excl=(j == 0))
        for dc in range(6):
            pp = PB[dc] if dc < 4 else PXO[dc - 4]
            S.add("pe", lambda e, j=j, ks=ks, dc=dc, pp=pp: e.matmul(
                pp[:, 0:NR], xsb[ks][:, 128 * dc:128 * dc + 128],
                ketm[ks][:, :], start=(j == 0), stop=(j == NCH - 1)),
                reads=[buf(f"xsb{ks}"), buf(f"ketm{ks}")], writes=[mt_pb[dc]],
                excl=(j == 0))
    S.add("dve", lambda e: e.tensor_copy(out=row127[0:1, 0:NR],
                                         in_=PDS[0:1, 0:NR]),
          reads=[ztot_pb], writes=[buf("zr")], excl=False)
    for dc in range(6):
        pp = PB[dc] if dc < 4 else PXO[dc - 4]
        S.add("dve", lambda e, dc=dc, pp=pp: e.tensor_copy(
            out=mtbf[dc][:, :], in_=pp[:, 0:NR]),
            reads=[mt_pb[dc]], writes=[buf(f"mtbf{dc}")],
            deps=(wvwo_casts if dc == 0 else ()))
    for j in range(NCH):
        ks = j % 2
        S.add("sp", lambda e, j=j, ks=ks: e.dma_start(
            out=ketm[ks][:, :], in_=kef_s[:, CH * j:CH * j + 128], transpose=True),
            reads=[buf("scr_kef")], writes=[buf(f"ketm{ks}")], is_dma=True)
        S.add("pool", lambda e, j=j, ks=ks: e.dma_start(
            out=xres[ks][:, :], in_=x_d[CH * j:CH * (j + 1), :]),
            writes=[buf(f"xres{ks}")], is_dma=True)
        S.add("act", lambda e, j=j, ks=ks: e.copy(out=xsb[ks][:, :],
                                                  in_=xres[ks][:, :]),
              reads=[buf(f"xres{ks}")], writes=[buf(f"xsb{ks}")])
        for dc in range(6, 8):
            S.add("pe", lambda e, j=j, ks=ks, dc=dc: e.matmul(
                PB[dc - 6][:, 0:NR], xsb[ks][:, 128 * dc:128 * dc + 128],
                ketm[ks][:, :], start=(j == 0), stop=(j == NCH - 1)),
                reads=[buf(f"xsb{ks}"), buf(f"ketm{ks}")], writes=[mt_pb[dc]],
                excl=(j == 0))
    for dc in range(6, 8):
        S.add("dve", lambda e, dc=dc: e.tensor_copy(
            out=mtbf[dc][:, :], in_=PB[dc - 6][:, 0:NR]),
            reads=[mt_pb[dc]], writes=[buf(f"mtbf{dc}")])
    for hb in range(2):
        pbsr = buf("bk_pds")
        for h in range(8 * hb, 8 * hb + 8):
            for dc in range(8):
                S.add("pe", lambda e, h=h, dc=dc, hb=hb: e.matmul(
                    PDS[0:32, 64 * h - 512 * hb:64 * h - 512 * hb + 64],
                    mtbf[dc][:, 32 * h:32 * h + 32],
                    wvT[dc][:, 64 * h:64 * h + 64], start=(dc == 0),
                    stop=(dc == 7)),
                    reads=[buf(f"mtbf{dc}"), buf(f"wvT{dc}")], writes=[pbsr],
                    excl=(h == 8 * hb and dc == 0),
                    deps=(wvT_ops if (h == 0 and dc == 0) else ()))
        S.add("dve", lambda e, hb=hb: e.tensor_copy(
            out=srs[0:32, 512 * hb:512 * hb + 512], in_=PDS[0:32, :]),
            reads=[pbsr], writes=[buf("srs")], excl=False, deps=prep_guard)
    S.add("pool", lambda e: e.dma_start(out=ccs_in[0:32, :], in_=srs[0:32, :]),
          reads=[buf("srs")], writes=[buf("ccs")], is_dma=True, excl=False)
    S.add("pool", lambda e: e.dma_start(out=ccs_in[32:33, 0:NR],
                                        in_=row127[:, :]),
          reads=[buf("zr")], writes=[buf("ccs")], is_dma=True, excl=False,
          dkey="ccs")
    S.add("dve", lambda e: e.memset(carrf[1][:, :], 0.0),
          writes=[buf("carrf1")])
    S.add("pool", lambda e: e.dma_start(out=ccs_in[32:33, NR:1024],
                                        in_=carrf[1][:, :]),
          reads=[buf("carrf1")], writes=[buf("ccs")], is_dma=True, excl=False,
          dkey="ccs")
    S.add("pool", lambda e: e.collective_compute(
        "AllGather", ALU.bypass, replica_groups=[list(range(NCORES))],
        ins=[ccs_in.ap().opt()], outs=[ccs_out.ap().opt()]),
        reads=[buf("ccs")], writes=[buf("ccso")], strict=True)

    MARK["P2"] = len(S.ops)
    tap("kefm0", kefm[0][:, 0:512], ["kefm0"], lvl=4)
    tap("srs", srs[:, :], ["srs"], lvl=4)
    tap("mtbf0", mtbf[0][:, :], ["mtbf0"], lvl=4)
    S.add("dve", lambda e: e.memset(accb[0][:, :], 0.0), writes=[buf("accb0")],
          deps=prep_guard)
    for r in range(NCORES):
        S.add("pool", lambda e, r=r: e.dma_start(
            out=gth[:, :], in_=ccs_out[33 * r:33 * (r + 1), :]),
            reads=[buf("ccso")], writes=[buf("gth")], is_dma=True)
        a, b2 = r % 2, (r + 1) % 2
        S.add("dve", lambda e, r=r, a=a, b2=b2: e.scalar_tensor_tensor(
            out=accb[b2][:, :], in0=gth[:, :], scalar=rvec[:, r:r + 1],
            in1=accb[a][:, :], op0=ALU.mult, op1=ALU.add),
            reads=[buf("gth"), buf(f"accb{a}"), buf("rvec")],
            writes=[buf(f"accb{b2}")],
            deps=(prep_guard if r == 0 else ()))
    accf = buf(f"accb{NCORES % 2}")
    accT = accb[NCORES % 2]
    S.add("pool", lambda e: e.dma_start(out=dup_s[:, :], in_=accT[:, :]),
          reads=[accf], writes=[buf("dup")], is_dma=True)
    S.add("pool", lambda e: e.dma_start(out=S32[0][32:64, :], in_=dup_s[0:32, :]),
          reads=[buf("dup")], writes=[buf("S32_0")], is_dma=True, excl=False,
          dkey="sdup", deps=prep_guard)
    S.add("pool", lambda e: e.dma_start(out=row127[:, :],
                                        in_=dup_s[32:33, 0:NR]),
          reads=[buf("dup"), buf("zr")], writes=[buf("row127"), buf("zr")],
          is_dma=True)
    S.add("dve", lambda e: e.tensor_scalar(
        out=carrf[0][:, :], in0=row127[:, :], scalar1=EPS_Z, scalar2=8.0,
        op0=ALU.add, op1=ALU.mult),
        reads=[buf("row127")], writes=[buf("carrf0")])
    S.add("act", lambda e: e.copy(out=carrb[0][:, :], in_=carrf[0][:, :]),
          reads=[buf("carrf0")], writes=[buf("carrb0")])
    S.add("dve", lambda e: e.tensor_copy(out=S32[0][0:32, :], in_=accT[0:32, :]),
          reads=[accf], writes=[buf("S32_0")], excl=False)
    S.add("act", lambda e: e.copy(out=Sbf[0][:, :], in_=S32[0][:, :]),
          reads=[buf("S32_0")], writes=[buf("Sbf_0")])

    MARK["P3"] = len(S.ops)
    tap("acc", accT[:, :], [f"accb{NCORES % 2}"], lvl=4)
    tap("carrf0", carrf[0][:, :], ["carrf0"], lvl=4)
    tap("S32i", S32[0][:, :], ["S32_0"], lvl=4)
    mtbf_rd = [o for dc2 in range(8) for o in _B[f"mtbf{dc2}"].readers]
    for s in range(2):
        for h in range(NH):
            if h % 2 == 0:
                S.add("dve", lambda e, s=s, h=h: e.memset(
                    qfm[s][h][32:64, :], 0.0), writes=[buf(f"qfm{s}_{h}")])
            else:
                S.add("dve", lambda e, s=s, h=h: e.memset(
                    qfm[s][h][0:32, :], 0.0), writes=[buf(f"qfm{s}_{h}")])
    prev_a_evacs = []
    for j in range(NCH):
        MARK[f"C{j}"] = len(S.ops)
        sj = j % 2
        so = 8 * sj
        ks = j % 2
        S.add("sp", lambda e, j=j, ks=ks: e.dma_start(
            out=ketm[ks][:, :], in_=kef_s[:, CH * j:CH * j + 128], transpose=True),
            reads=[buf("scr_kef")], writes=[buf(f"ketm{ks}")], is_dma=True)
        pbq = buf("bk_pb0")
        for dc in range(8):
            S.add("pe", lambda e, j=j, dc=dc: e.matmul(
                PB[0][:, 0:NR], xfm[dc][:, CH * j:CH * j + 128], wtq[dc][:, :],
                start=(dc == 0), stop=(dc == 7)),
                reads=[buf(f"xfm{dc}"), buf(f"wtq{dc}")], writes=[pbq],
                excl=(dc == 0))
        S.add("act", lambda e, sj=sj: e.activation(
            out=etm[sj][:, :], in_=PB[0][:, 0:NR], func=AF.Exp),
            reads=[pbq], writes=[buf(f"etm{sj}")])
        for half in range(2):
            pbv = buf(f"bk_pb{2 + half}")
            for dc in range(8):
                S.add("pe", lambda e, j=j, dc=dc, half=half: e.matmul(
                    PB[2 + half][:, 0:512], xfm[dc][:, CH * j:CH * j + 128],
                    wvT[dc][:, 512 * half:512 * half + 512],
                    start=(dc == 0), stop=(dc == 7)),
                    reads=[buf(f"xfm{dc}"), buf(f"wvT{dc}")], writes=[pbv],
                    excl=(dc == 0),
                    deps=(woT_ops if (j == 0 and dc == 0 and half == 0) else ()))
            S.add("dve", lambda e, half=half, sj=sj: e.tensor_copy(
                out=vtm[sj][:, 512 * half:512 * half + 512],
                in_=PB[2 + half][:, 0:512]),
                reads=[pbv], writes=[buf(f"vtm{sj}")], excl=(half == 0),
                deps=(mtbf_rd if j < 2 else ()))
        pbz = buf("bk_pb1")
        zop1 = S.add("pe", lambda e, j=j, ks=ks: e.matmul(
            PB[1][:, 0:NR], u8t[:, :], ketm[ks][:, :], start=True, stop=False),
            reads=[buf("u8t"), buf(f"ketm{ks}")], writes=[pbz],
            deps=prev_a_evacs)
        zop2 = S.add("pe", lambda e, j=j: e.matmul(
            PB[1][:, 0:NR], onesr[:, :], carrb[j % 2][:, :],
            start=False, stop=True),
            reads=[buf("onesr"), buf(f"carrb{j % 2}")], writes=[pbz], excl=False)
        r127op = None
        S.add("dve", lambda e, sj=sj: e.tensor_reduce(
            out=hsb[sj][:, :],
            in_=etm[sj].rearrange("p (h c) -> p h c", c=32)[:, :, 0:16],
            axis=AX.X, op=ALU.add),
            reads=[buf(f"etm{sj}")], writes=[buf(f"hsb{sj}")], strict=True)
        dop = S.add("dve", lambda e, sj=sj: e.tensor_tensor(
            out=dbf.rearrange("p (h z) -> p h z", z=32),
            in0=PB[1][:, 0:NR].rearrange("p (h z) -> p h z", z=32),
            in1=hsb[sj].rearrange("p (h o) -> p h o", o=1).broadcast_to(
                [128, 16, 32]),
            op=ALU.mult),
            reads=[pbz, buf(f"hsb{sj}")], writes=[buf("dbf")],
            deps=(mtbf_rd if j == 0 else ()))
        S.add("act", lambda e: e.activation(out=lnd[:, :], in_=dbf[:, :],
                                            func=AF.Ln),
              reads=[buf("dbf")], writes=[buf("lnd")])
        S.add("dve", lambda e: e.tensor_tensor(
            out=dbf[:, :], in0=PB[0][:, 0:NR], in1=lnd[:, :], op=ALU.subtract),
            reads=[pbq, buf("lnd")], writes=[buf("dbf")])
        S.add("act", lambda e, sj=sj: e.activation(
            out=qtm[sj][:, :], in_=dbf[:, :], func=AF.Exp),
            reads=[buf("dbf")], writes=[buf(f"qtm{sj}")])
        S.add("pe", lambda e, j=j, ks=ks: e.matmul(
            PB[0][0:1, 0:NR], ones1[:, :], ketm[ks][:, :],
            start=True, stop=True),
            reads=[buf("ones1"), buf(f"ketm{ks}")], writes=[pbq])
        S.add("dve", lambda e, j=j: e.scalar_tensor_tensor(
            out=carrf[(j + 1) % 2][:, :], in0=PB[0][0:1, 0:NR], scalar=8.0,
            in1=carrf[j % 2][:, :], op0=ALU.mult, op1=ALU.add),
            reads=[pbq, buf(f"carrf{j % 2}")],
            writes=[buf(f"carrf{(j + 1) % 2}")])
        S.add("act", lambda e, j=j: e.copy(out=carrb[(j + 1) % 2][:, :],
                                           in_=carrf[(j + 1) % 2][:, :]),
              reads=[buf(f"carrf{(j + 1) % 2}")],
              writes=[buf(f"carrb{(j + 1) % 2}")])
        ptq = buf("bk_pt")
        for h in range(NH):
            p, lo = h >> 1, 32 * (h & 1)
            S.add("pe", lambda e, sj=sj, h=h, p=p, lo=lo: e.transpose(
                PT[lo:lo + 32, 128 * p:128 * p + 128],
                qtm[sj][:, 32 * h:32 * h + 32], ident[:, :]),
                reads=[buf(f"qtm{sj}"), b_id], writes=[ptq], excl=(h == 0))
        for h in range(NH):
            p, lo = h >> 1, 32 * (h & 1)
            S.add("dve", lambda e, sj=sj, h=h, p=p, lo=lo: e.tensor_copy(
                out=qfm[sj][h][lo:lo + 32, :],
                in_=PT[lo:lo + 32, 128 * p:128 * p + 128]),
                reads=[ptq], writes=[buf(f"qfm{sj}_{h}")], excl=False)
        a_evacs = []
        for hb in range(2):
            pds_b = buf("bk_pds")
            for h in range(8 * hb, 8 * hb + 8):
                p, off, vc = h >> 1, 32 * (h & 1), 64 * h
                sa = h % 2
                pa = buf(f"pa{h % 4}")
                S.add("pe", lambda e, j=j, p=p, h=h: e.matmul(
                    PB[1][:, 128 * (h % 4):128 * (h % 4) + 128],
                    kefm[p][0:64, CH * j:CH * j + 128],
                    qfm[j % 2][h][0:64, :], start=True, stop=True),
                    reads=[buf(f"kefm{p}"), buf(f"qfm{j % 2}_{h}")], writes=[pa],
                    deps=([dop, zop1, zop2] if h < 4 else []))
                sa4 = h % 4
                aev = S.add("dve", lambda e, h=h, sa4=sa4: e.scalar_tensor_tensor(
                    out=asb[sa4][:, :],
                    in0=PB[1][:, 128 * (h % 4):128 * (h % 4) + 128],
                    scalar=1.0, in1=m1t[:, :], op0=ALU.mult, op1=ALU.mult),
                    reads=[pa, buf("m1t")], writes=[buf(f"asb{sa4}")])
                a_evacs.append(aev)
                ti = h >> 1
                pxo = buf(f"bk_pxo{ti // 4}")
                S.add("pe", lambda e, h=h, vc=vc, ti=ti, j=j: e.matmul(
                    PXO[ti // 4][64 * (h & 1):64 * (h & 1) + 64,
                                 128 * (ti % 4):128 * (ti % 4) + 128],
                    vtm[j % 2][:, vc:vc + 64], asb[h % 4][:, :],
                    start=True, stop=False),
                    reads=[buf(f"vtm{j % 2}"), buf(f"asb{h % 4}")], writes=[pxo],
                    excl=((h & 1) == 0))
                S.add("pe", lambda e, h=h, p=p, vc=vc, ti=ti, j=j:
                      e.matmul(
                    PXO[ti // 4][64 * (h & 1):64 * (h & 1) + 64,
                                 128 * (ti % 4):128 * (ti % 4) + 128],
                    Sbf[j % 2][0:64, vc:vc + 64],
                    qfm[j % 2][h][0:64, :],
                    start=False, stop=True),
                    reads=[buf(f"Sbf_{j % 2}"), buf(f"qfm{j % 2}_{h}")],
                    writes=[pxo], excl=False)
                S.add("pe", lambda e, h=h, vc=vc, j=j, ks=ks, hb=hb: e.matmul(
                    PDS[0:32, 64 * h - 512 * hb:64 * h - 512 * hb + 64],
                    ketm[ks][:, 32 * h:32 * h + 32], vtm[j % 2][:, vc:vc + 64],
                    start=True, stop=True),
                    reads=[buf(f"ketm{ks}"), buf(f"vtm{j % 2}")], writes=[pds_b],
                    excl=(h == 8 * hb))
                S.add("pe", lambda e, h=h, vc=vc, j=j, ks=ks, hb=hb: e.matmul(
                    PDS[32:64, 64 * h - 512 * hb:64 * h - 512 * hb + 64],
                    ketm[ks][:, 32 * h:32 * h + 32], vtm[j % 2][:, vc:vc + 64],
                    start=True, stop=True),
                    reads=[buf(f"ketm{ks}"), buf(f"vtm{j % 2}")], writes=[pds_b],
                    excl=False)
            S.add("dve", lambda e, j=j, hb=hb: e.tensor_tensor(
                out=S32[(j + 1) % 2][0:64, 512 * hb:512 * hb + 512],
                in0=S32[j % 2][0:64, 512 * hb:512 * hb + 512], in1=PDS[:, :],
                op=ALU.add),
                reads=[buf(f"S32_{j % 2}"), pds_b],
                writes=[buf(f"S32_{(j + 1) % 2}")], excl=(hb == 0))
        prev_a_evacs = a_evacs[-4:] + [dop]
        S.add("act", lambda e, j=j: e.copy(out=Sbf[(j + 1) % 2][:, :],
                                           in_=S32[(j + 1) % 2][:, :]),
              reads=[buf(f"S32_{(j + 1) % 2}")],
              writes=[buf(f"Sbf_{(j + 1) % 2}")])
        for ti in range(8):
            S.add("dve", lambda e, ti=ti: e.tensor_copy(
                out=xofm[ti][:, :],
                in_=PXO[ti // 4][:, 128 * (ti % 4):128 * (ti % 4) + 128]),
                reads=[buf(f"bk_pxo{ti // 4}")], writes=[buf(f"xofm{ti}")])
        for half in range(2):
            pbo = buf(f"bk_pb{2 + half}")
            for ti in range(8):
                S.add("pe", lambda e, ti=ti, half=half: e.matmul(
                    PB[2 + half][:, 0:512], xofm[ti][:, :],
                    woT[ti][:, 512 * half:512 * half + 512],
                    start=(ti == 0), stop=(ti == 7)),
                    reads=[buf(f"xofm{ti}"), buf(f"woT{ti}")], writes=[pbo],
                    excl=(ti == 0))
        S.add("pool", lambda e, j=j, sj=sj: e.dma_start(
            out=xres[sj][:, :], in_=x_d[CH * j:CH * (j + 1), :]),
            writes=[buf(f"xres{sj}")], is_dma=True,
            deps=([o for dc2 in range(8) for o in _B[f"mtbf{dc2}"].readers]
                  if j < 2 else ()))
        for half in range(2):
            S.add("dve", lambda e, half=half, sj=sj: e.tensor_tensor(
                out=pre1[:, 512 * half:512 * half + 512],
                in0=PB[2 + half][:, 0:512],
                in1=xres[sj][:, 512 * half:512 * half + 512], op=ALU.add),
                reads=[buf(f"bk_pb{2 + half}"), buf(f"xres{sj}")],
                writes=[buf("pre1")], excl=(half == 0))
        if j == 0:
            tap("etm0", etm[0][:, :], ["etm0"], lvl=2)
            tap("hsb0", hsb[0][:, :], ["hsb0"], lvl=2)
            tap("qtm0", qtm[0][:, :], ["qtm0"], lvl=2)
            tap("ketm0", ketm[0][:, :], ["ketm0"], lvl=2)
            tap("vtm0", vtm[0][:, :], ["vtm0"], lvl=2)
            tap("qfm0", qfm[0][0][0:64, :], ["qfm0_0"], lvl=2)
            tap("qfm1", qfm[0][1][0:64, :], ["qfm0_1"], lvl=2)
            tap("asb0", asb[0][:, :], ["asb0"], lvl=2)
            tap("xofm0", xofm[0][:, :], ["xofm0"], lvl=2)
            tap("S32n", S32[1][:, :], ["S32_1"], lvl=2)
            tap("pre1", pre1[:, :], ["pre1"], lvl=2)
        _emit_ln(S, buf, pre1, sqb, stat, so, ln1, "ln1", "pre1", "sqb")
        if j == 0:
            tap("ln1", ln1[:, :], ["ln1"], lvl=2)
        S.add("act", lambda e, sj=sj: e.copy(out=ln1b[sj][:, :], in_=ln1[:, :]),
              reads=[buf("ln1")], writes=[buf(f"ln1b{sj}")])
        S.add("pool", lambda e, j=j: e.dma_start(
            out=lntm_s[CH * j:CH * (j + 1), :], in_=ln1[:, :]),
            reads=[buf("ln1")], writes=[buf("scr_lntm")], is_dma=True,
            excl=False, dkey="lntm")
        S.add("pool", lambda e, j=j, sj=sj: e.dma_start(
            out=lnb_s[CH * j:CH * (j + 1), :], in_=ln1b[sj][:, :]),
            reads=[buf(f"ln1b{sj}")], writes=[buf("scr_lnb")], is_dma=True,
            excl=False, dkey=f"lnb{sj}")

    MARK["P4"] = len(S.ops)
    xfm_readers = [o for dc in range(8) for o in _B[f"xfm{dc}"].readers]
    lnfm_ops = []
    for dc in range(8):
        lnfm_ops.append(S.add("sp", lambda e, dc=dc: e.dma_start(
            out=lnfm[dc][:, :], in_=lnb_s[:, 128 * dc:128 * dc + 128],
            transpose=True),
            reads=[buf("scr_lnb")], writes=[buf(f"lnfm{dc}")], is_dma=True,
            deps=xfm_readers, dkey="lnfmT"))
    wtqk_readers = [o for i in range(8)
                    for o in (_B[f"wtq{i}"].readers + _B[f"wtk{i}"].readers)]
    scanB_readers = []
    for nm in (["xres0", "xres1", "asb0", "asb1", "asb2", "asb3"] +
               [f"xofm{i}" for i in range(8)] +
               [f"qfm{s}_{h}" for s in range(2) for h in range(16)]):
        scanB_readers += _B[nm].readers + _B[nm].writers
    for hg in range(6):
        sl = hg % 2
        slot_ops = []
        for dc in range(8):
            dep0 = []
            if hg == 0:
                dep0 = wtqk_readers
            elif hg == 1:
                dep0 = scanB_readers
            slot_ops.append(S.add("sp", lambda e, hg=hg, sl=sl, dc=dc: e.dma_start(
                out=wgT[sl][dc][:, :],
                in_=wgb_s[512 * hg:512 * (hg + 1), 128 * dc:128 * dc + 128],
                transpose=True),
                reads=[buf("scr_wg")], writes=[buf(f"wgT{sl}_{dc}")],
                is_dma=True, deps=dep0, dkey=f"wgT{sl}"))
            slot_ops.append(S.add("sp", lambda e, hg=hg, sl=sl, dc=dc: e.dma_start(
                out=wuT[sl][dc][:, :],
                in_=wub_s[512 * hg:512 * (hg + 1), 128 * dc:128 * dc + 128],
                transpose=True),
                reads=[buf("scr_wu")], writes=[buf(f"wuT{sl}_{dc}")],
                is_dma=True, deps=dep0, dkey=f"wuT{sl}"))
        for hc in range(4):
            hh = 4 * hg + hc
            for ts2 in range(4):
                pg, pu = buf("bk_pb0"), buf("bk_pb1")
                for dc in range(8):
                    S.add("pe", lambda e, sl=sl, hc=hc, ts2=ts2, dc=dc: e.matmul(
                        PB[0][:, 0:512], wgT[sl][dc][:, 128 * hc:128 * hc + 128],
                        lnfm[dc][:, 512 * ts2:512 * ts2 + 512],
                        start=(dc == 0), stop=(dc == 7)),
                        reads=[buf(f"wgT{sl}_{dc}"), buf(f"lnfm{dc}")],
                        writes=[pg], excl=(dc == 0),
                        deps=(slot_ops + lnfm_ops) if (hc == 0 and ts2 == 0
                                                       and dc == 0) else ())
                for dc in range(8):
                    S.add("pe", lambda e, sl=sl, hc=hc, ts2=ts2, dc=dc: e.matmul(
                        PB[1][:, 0:512], wuT[sl][dc][:, 128 * hc:128 * hc + 128],
                        lnfm[dc][:, 512 * ts2:512 * ts2 + 512],
                        start=(dc == 0), stop=(dc == 7)),
                        reads=[buf(f"wuT{sl}_{dc}"), buf(f"lnfm{dc}")],
                        writes=[pu], excl=(dc == 0))
                S.add("act", lambda e: e.activation(
                    out=silb[:, 0:512], in_=PB[0][:, 0:512], func=AF.Silu,
                    scale=QS),
                    reads=[pg], writes=[buf("silb")],
                    deps=(scanB_readers if (hg == 0 and hc == 0 and ts2 == 0)
                          else ()))
                S.add("dve", lambda e, hh=hh, ts2=ts2: e.tensor_tensor(
                    out=hpr[hh][:, 512 * ts2:512 * ts2 + 512],
                    in0=silb[:, 0:512], in1=PB[1][:, 0:512], op=ALU.mult),
                    reads=[buf("silb"), pu], writes=[buf(f"hpr{hh}")],
                    excl=False)
    tap("lnfm0", lnfm[0][:, 0:512], ["lnfm0"], lvl=2)
    tap("hpr0", hpr[0][:, 0:512], ["hpr0"], lvl=2)
    tap("hpr16", hpr[16][:, 0:512], ["hpr16"], lvl=2)
    wvwo_readers = [o for dc in range(8)
                    for o in (_B[f"wvT{dc}"].readers + _B[f"woT{dc}"].readers)]
    sket_readers = (_B["ketm0"].readers + _B["ketm1"].readers +
                    _B["S32_0"].readers + _B["S32_1"].readers +
                    _B["Sbf_0"].readers + _B["Sbf_1"].readers)
    wdT_ops = []
    for hh in range(24):
        wdT_ops.append(S.add("sp", lambda e, hh=hh: e.dma_start(
            out=wdT[hh][:, :], in_=wdb_s[:, 128 * hh:128 * hh + 128],
            transpose=True),
            reads=[buf("scr_wd")], writes=[buf(f"wdT{hh}")], is_dma=True,
            deps=((sket_readers if hh == 0 else _B["silb"].readers)
                  if hh < 2 else
                  (wvwo_readers if hh == 2 else
                   ([o for sl2 in range(1) for dc2 in range(8)
                     for o in _B[f"wgT0_{dc2}"].readers +
                     _B[f"wuT0_{dc2}"].readers] if hh == 18 else []))),
            dkey=f"wdT{hh % 3}"))
    yops = []
    for tj in range(NCH):
        sj = tj % 2
        so = 8 * sj
        S.add("pool", lambda e, tj=tj, sj=sj: e.dma_start(
            out=ltm[sj][:, :], in_=lntm_s[CH * tj:CH * (tj + 1), :]),
            reads=[buf("scr_lntm")], writes=[buf(f"ltm{sj}")], is_dma=True,
            deps=([o for dc2 in range(8) for o in _B[f"lnfm{dc2}"].readers]
                  if tj < 2 else []))
        for half in range(2):
            pbo = buf(f"bk_pb{2 + half}")
            for hh in range(24):
                S.add("pe", lambda e, tj=tj, half=half, hh=hh: e.matmul(
                    PB[2 + half][:, 0:512], hpr[hh][:, CH * tj:CH * tj + 128],
                    wdT[hh][:, 512 * half:512 * half + 512],
                    start=(hh == 0), stop=(hh == 23)),
                    reads=[buf(f"hpr{hh}"), buf(f"wdT{hh}")], writes=[pbo],
                    excl=(hh == 0),
                    deps=(wdT_ops if (tj == 0 and half == 0 and hh == 0)
                          else ()))
            S.add("dve", lambda e, half=half, sj=sj: e.scalar_tensor_tensor(
                out=pr2[:, 512 * half:512 * half + 512],
                in0=PB[2 + half][:, 0:512], scalar=QS2,
                in1=ltm[sj][:, 512 * half:512 * half + 512],
                op0=ALU.mult, op1=ALU.add),
                reads=[buf(f"bk_pb{2 + half}"), buf(f"ltm{sj}")],
                writes=[buf("pr2")], excl=(half == 0))
        if tj == 0:
            tap("ltm0", ltm[0][:, :], ["ltm0"], lvl=2)
            tap("pr2", pr2[:, :], ["pr2"], lvl=2)
        _emit_ln(S, buf, pr2, sq2, stat, so, outb[sj], f"outb{sj}", "pr2", "sq2")
        yops.append(S.add("pool", lambda e, tj=tj, sj=sj: e.dma_start(
            out=y_d[CH * tj:CH * (tj + 1), :], in_=outb[sj][:, :]),
            reads=[buf(f"outb{sj}")], writes=[buf("yout")], is_dma=True,
            excl=False, dkey=f"y{sj}"))
    S.add("sp", lambda e: e.nop(), deps=yops + tap_list)
    MARK["ALL"] = len(S.ops)

    import builtins
    builtins._KMARK = dict(MARK)
    if stop is not None:
        k = MARK[stop] if isinstance(stop, str) else stop
        S.ops = S.ops[:k]
        last = {}
        for op in S.ops:
            last[op.engine] = op
        S.add("sp", lambda e: e.nop(),
              deps=[o for o in last.values()])

    with nc.Block() as block:
        S.finalize(nc, block, ctx)
    ctx.close()
    return nc


def _emit_ln(S, buf, src, sq, stat, so, dst, dst_name, src_name, sq_name):
    st = buf(f"stat{so}")
    S.add("dve", lambda e: e.tensor_reduce(
        out=stat[:, so:so + 1], in_=src[:, :], axis=AX.X, op=ALU.add),
        reads=[buf(src_name)], writes=[st], strict=True)
    S.add("act", lambda e: e.activation(out=sq[:, :], in_=src[:, :],
                                        func=AF.Square),
          reads=[buf(src_name)], writes=[buf(sq_name)], strict=True)
    S.add("dve", lambda e: e.tensor_reduce(
        out=stat[:, so + 1:so + 2], in_=sq[:, :], axis=AX.X, op=ALU.add),
        reads=[buf(sq_name)], writes=[st], excl=False, strict=True)
    S.add("dve", lambda e: e.tensor_scalar(
        out=stat[:, so + 2:so + 3], in0=stat[:, so:so + 1], scalar1=1.0 / D,
        scalar2=None, op0=ALU.mult),
        reads=[st], writes=[st], excl=False, strict=True)
    S.add("dve", lambda e: e.tensor_scalar(
        out=stat[:, so + 3:so + 4], in0=stat[:, so + 2:so + 3], scalar1=-1.0,
        scalar2=None, op0=ALU.mult),
        reads=[st], writes=[st], excl=False, strict=True)
    S.add("dve", lambda e: e.tensor_scalar(
        out=stat[:, so + 4:so + 5], in0=stat[:, so + 1:so + 2], scalar1=1.0 / D,
        scalar2=EPS_LN, op0=ALU.mult, op1=ALU.add),
        reads=[st], writes=[st], excl=False, strict=True)
    S.add("dve", lambda e: e.scalar_tensor_tensor(
        out=stat[:, so + 5:so + 6], in0=stat[:, so + 2:so + 3],
        scalar=stat[:, so + 2:so + 3], in1=stat[:, so + 4:so + 5],
        op0=ALU.mult, op1=ALU.subtract),
        reads=[st], writes=[st], excl=False, strict=True)
    S.add("act", lambda e: e.activation(
        out=stat[:, so + 6:so + 7], in_=stat[:, so + 5:so + 6], func=AF.Sqrt,
        scale=-1.0),
        reads=[st], writes=[st], excl=False, strict=True)
    S.add("dve", lambda e: e.reciprocal(
        out=stat[:, so + 7:so + 8], in_=stat[:, so + 6:so + 7]),
        reads=[st], writes=[st], excl=False, strict=True)
    S.add("dve", lambda e: e.tensor_scalar(
        out=dst[:, :], in0=src[:, :], scalar1=stat[:, so + 3:so + 4],
        scalar2=stat[:, so + 7:so + 8], op0=ALU.add, op1=ALU.mult),
        reads=[buf(src_name), st], writes=[buf(dst_name)], strict=True)


_NC_CACHE = {}


def kernel(**inputs):
    x = np.ascontiguousarray(np.asarray(inputs["L0_attn_x"]).reshape(T, D))
    wqkv = np.ascontiguousarray(np.asarray(inputs["L0_attn_w_qkv"]))
    wout = np.ascontiguousarray(np.asarray(inputs["L0_attn_w_out"]))
    code = np.ascontiguousarray(np.asarray(inputs["L0_attn_fc_code"])
                                .reshape(NH * CS, HD))
    wg = np.ascontiguousarray(np.asarray(inputs["L0_ffn_w_gate"], np.int32))
    wu = np.ascontiguousarray(np.asarray(inputs["L0_ffn_w_up"], np.int32))
    wd = np.ascontiguousarray(np.asarray(inputs["L0_ffn_w_down"], np.int32))

    ident = np.eye(128, dtype=np.float32).astype(ml_dtypes.bfloat16)
    st = np.arange(128)
    u8 = (8.0 * (st[:, None] <= st[None, :])).astype(ml_dtypes.bfloat16)
    m1 = (st[:, None] <= st[None, :]).astype(np.float32)
    onesr = np.ones((1, 128), dtype=ml_dtypes.bfloat16)
    ones1 = np.ones((128, 1), dtype=ml_dtypes.bfloat16)
    rr = np.arange(64)
    padv = np.where((rr % 32) >= 16, 1e9, -3.0e38).astype(np.float32)[:, None]

    if "nc" not in _NC_CACHE:
        _NC_CACHE["nc"] = build_nc()
    nc = _NC_CACHE["nc"]

    in_maps = []
    for c in range(NCORES):
        rvec = np.zeros((33, 8), dtype=np.float32)
        rvec[:, :c] = 1.0
        in_maps.append({
            "x": np.asarray(x[TLOC * c:TLOC * (c + 1)], np.float32),
            "w_qkv": np.asarray(wqkv, np.float32),
            "w_out": np.asarray(wout, np.float32),
            "fc_code": np.asarray(code, np.float32),
            "w_gate": wg, "w_up": wu, "w_down": wd,
            "c_ident": ident, "c_u8": u8, "c_m1": m1, "c_onesr": onesr, "c_ones1": ones1,
            "c_padv": padv, "c_rvec": rvec,
        })
    res = run_bass_kernel_spmd(nc, in_maps, core_ids=list(range(NCORES)))
    out = np.concatenate([np.asarray(r["y"]) for r in res.results], axis=0)
    return out.reshape(1, T, D)


# version 13
# speedup vs baseline: 1.1907x; 1.0665x over previous
import numpy as np
import ml_dtypes
import concourse.bass as bass
import concourse.mybir as mybir
from concourse.dt import dt
from concourse.bass_utils import run_bass_kernel_spmd
from contextlib import ExitStack

F32 = dt.float32
BF16 = dt.bfloat16
I32 = dt.int32
AF = mybir.ActivationFunctionType
ALU = mybir.AluOpType
AX = mybir.AxisListType

NCORES = 8
T = 16384
TLOC = T // NCORES
D = 1024
NH = 16
HD = 64
CS = 16
HID = 3072
CH = 128
NCH = TLOC // CH
SCALE = 0.125
QS = 1.0 / 127.0
QS2 = QS * QS
EPS_Z = 1e-9
EPS_LN = 1e-5
NR = 512
ENGS = ("pe", "act", "dve", "pool", "sp")


class Op:
    __slots__ = ("engine", "emit", "deps", "inc", "val", "is_dma", "sem",
                 "needed", "strict")

    def __init__(self, engine, emit, deps, is_dma, sem, strict):
        self.engine = engine
        self.emit = emit
        self.deps = deps
        self.inc = 0
        self.val = None
        self.is_dma = is_dma
        self.sem = sem
        self.needed = False
        self.strict = strict


class Buf:
    __slots__ = ("writers", "readers", "name")

    def __init__(self, name):
        self.writers = []
        self.readers = []
        self.name = name


class Sched:
    def __init__(self):
        self.ops = []

    def add(self, engine, emit, reads=(), writes=(), deps=(), is_dma=False,
            excl=True, dkey=None, strict=False):
        dd = list(deps)
        for b in reads:
            dd += b.writers
        for b in writes:
            dd += b.readers + b.writers
        sem = None
        if is_dma:
            sem = "dma:" + (dkey or (writes[0].name if writes else "misc"))
        op = Op(engine, emit, dd, is_dma, sem, strict)
        self.ops.append(op)
        for b in reads:
            b.readers.append(op)
        for b in writes:
            if excl:
                b.writers = [op]
                b.readers = []
            else:
                b.writers.append(op)
        return op

    def finalize(self, nc, block, ctx):
        for op in self.ops:
            for d in op.deps:
                d.needed = True
        sem_names = list(ENGS)
        for op in self.ops:
            if op.is_dma and op.sem not in sem_names:
                sem_names.append(op.sem)
        sems = {n: ctx.enter_context(nc.semaphore(f"s{i}"))
                for i, n in enumerate(sem_names)}
        counts = {n: 0 for n in sem_names}
        prefix = {}
        for idx, op in enumerate(self.ops):
            if op.is_dma:
                op.inc = 16
            elif op.needed:
                op.inc = 1
                op.sem = op.engine
            if op.inc:
                counts[op.sem] += op.inc
                op.val = counts[op.sem]
                if op.is_dma:
                    prefix.setdefault(op.sem, []).append((idx, op.val))
        import bisect as _bis

        def dma_wait_val(sem, cidx):
            lst = prefix[sem]
            k = _bis.bisect_left(lst, (cidx, -1))
            return lst[k - 1][1] if k else 0

        idx_of = {id(op): i for i, op in enumerate(self.ops)}

        def body(eng_handle, ename):
            waited = {}
            for op in self.ops:
                if op.engine != ename:
                    continue
                cidx = idx_of[id(op)]
                want = {}
                for d in op.deps:
                    if d.inc == 0:
                        continue
                    if d.engine == ename and not d.is_dma and not d.strict:
                        continue
                    v = dma_wait_val(d.sem, cidx) if d.is_dma else d.val
                    if v > want.get(d.sem, 0):
                        want[d.sem] = v
                for se, v in want.items():
                    if waited.get(se, 0) >= v:
                        continue
                    eng_handle.wait_ge(sems[se], v)
                    waited[se] = v
                ins = op.emit(eng_handle)
                if op.inc:
                    ins.then_inc(sems[op.sem], op.inc)

        block.tensor(lambda e: body(e, "pe"))
        block.scalar(lambda e: body(e, "act"))
        block.vector(lambda e: body(e, "dve"))
        block.gpsimd(lambda e: body(e, "pool"))
        block.sync(lambda e: body(e, "sp"))


def build_nc(stop=None, taps=False):
    nc = bass.Bass("TRN2", target_bir_lowering=False,
                   dynamic_dma_scratch_size=16384,
                   detect_race_conditions=False)
    x_d = nc.dram_tensor("x", [TLOC, D], F32, kind="ExternalInput")
    wqkv_d = nc.dram_tensor("w_qkv", [3 * D, D], F32, kind="ExternalInput")
    wout_d = nc.dram_tensor("w_out", [D, D], F32, kind="ExternalInput")
    code_d = nc.dram_tensor("fc_code", [NH * CS, HD], F32, kind="ExternalInput")
    wg_d = nc.dram_tensor("w_gate", [HID, D], I32, kind="ExternalInput")
    wu_d = nc.dram_tensor("w_up", [HID, D], I32, kind="ExternalInput")
    wd_d = nc.dram_tensor("w_down", [D, HID], I32, kind="ExternalInput")
    ident_d = nc.dram_tensor("c_ident", [128, 128], BF16, kind="ExternalInput")
    u8_d = nc.dram_tensor("c_u8", [128, 128], BF16, kind="ExternalInput")
    onesr_d = nc.dram_tensor("c_onesr", [1, 128], BF16, kind="ExternalInput")
    m1_d = nc.dram_tensor("c_m1", [128, 128], F32, kind="ExternalInput")
    ones1_d = nc.dram_tensor("c_ones1", [128, 1], BF16, kind="ExternalInput")
    padv_d = nc.dram_tensor("c_padv", [64, 1], F32, kind="ExternalInput")
    rvec_d = nc.dram_tensor("c_rvec", [33, 8], F32, kind="ExternalInput")
    y_d = nc.dram_tensor("y", [TLOC, D], F32, kind="ExternalOutput")

    xb_s = nc.dram_tensor("s_xb", [TLOC, D], BF16, kind="Internal")
    wvb_s = nc.dram_tensor("s_wvb", [D, D], BF16, kind="Internal")
    wob_s = nc.dram_tensor("s_wob", [D, D], BF16, kind="Internal")
    wgb_s = nc.dram_tensor("s_wgb", [HID, D], BF16, kind="Internal")
    wub_s = nc.dram_tensor("s_wub", [HID, D], BF16, kind="Internal")
    wdb_s = nc.dram_tensor("s_wdb", [D, HID], BF16, kind="Internal")
    kef_s = nc.dram_tensor("s_kef", [NR, TLOC], BF16, kind="Internal")
    lnb_s = nc.dram_tensor("s_lnb", [TLOC, D], BF16, kind="Internal")
    lntm_s = nc.dram_tensor("s_lntm", [TLOC, D], F32, kind="Internal")
    dup_s = nc.dram_tensor("s_dup", [33, 1024], F32, kind="Internal")
    ccm_in = nc.dram_tensor("cc_m_in", [64, 8], F32, kind="Internal")
    ccm_out = nc.dram_tensor("cc_m_out", [64, 8], F32, kind="Internal")
    ccs_in = nc.dram_tensor("cc_s_in", [33, 1024], F32, kind="Internal")
    ccs_out = nc.dram_tensor("cc_s_out", [33 * 8, 1024], F32, kind="Internal")

    ctx = ExitStack()

    def sb(name, shape, dtp):
        return ctx.enter_context(nc.sbuf_tensor(name, shape, dtp))

    def psum(name, shape, dtp):
        return ctx.enter_context(nc.psum_tensor(name, shape, dtp))

    ident = sb("ident", [128, 128], BF16)
    u8t = sb("u8t", [128, 128], BF16)
    m1t = sb("m1t", [128, 128], F32)
    onesr = sb("onesr", [1, 128], BF16)
    ones1 = sb("ones1", [128, 1], BF16)
    padv = sb("padv", [64, 1], F32)
    rvec = sb("rvec", [33, 8], F32)
    negm = sb("negm", [64, 8], F32)
    mred = sb("mred", [64, 4], F32)
    mloc = sb("mloc", [64, 8], F32)
    carrf = [sb(f"carrf{i}", [1, NR], F32) for i in range(2)]
    carrb = [sb(f"carrb{i}", [1, NR], BF16) for i in range(2)]
    row127 = sb("row127", [1, NR], F32)
    hsb = [sb(f"hsb{i}", [128, 16], F32) for i in range(2)]
    stat = sb("stat", [128, 16], F32)

    ARK = sb("ARK", [128, 16384], BF16)
    ARB = sb("ARB", [128, 16384], BF16)
    ARA = sb("ARA", [128, 8192], BF16)
    ARG = sb("ARG", [128, 8192], BF16)
    ARG2 = sb("ARG2", [128, 8192], BF16)
    ARD = sb("ARD", [128, 2048], BF16)
    ARE = sb("ARE", [128, 16384], BF16)
    ARF = sb("ARF", [128, 8192], BF16)
    ARH = sb("ARH", [128, 8192], BF16)
    ARI = sb("ARI", [128, 8192], BF16)

    qk_bf = [ARA[:, 1024 * i:1024 * (i + 1)] for i in range(8)]
    wst = [ARG[:, 0:2048].bitcast(F32), ARG[:, 2048:4096].bitcast(F32)]
    wsti = [ARG[:, 0:2048].bitcast(I32), ARG[:, 2048:4096].bitcast(I32)]
    wbf = [ARG[:, 4096:5120], ARG[:, 5120:6144]]
    xtb = [ARG[:, 6144:7168], ARG[:, 7168:8192]]
    codeb = ARG2[0:16, 0:1024]
    codeT = ARG2[:, 1024:1280]
    codeblk = [ARG2[:, 1536 + 256 * i:1536 + 256 * (i + 1)] for i in range(8)]
    xfm = [ARB[:, 2048 * i:2048 * (i + 1)] for i in range(8)]
    wtq = [ARF[:, 512 * i:512 * (i + 1)] for i in range(8)]
    wtk = [ARF[:, 4096 + 512 * i:4096 + 512 * (i + 1)] for i in range(8)]
    wvT = [ARE[:, 1024 * i:1024 * (i + 1)] for i in range(8)]
    woT = [ARE[:, 8192 + 1024 * i:8192 + 1024 * (i + 1)] for i in range(8)]
    xsb = [ARA[:, 1024 * s:1024 * (s + 1)] for s in range(2)]
    kefm = [ARK[0:64, 2048 * p:2048 * (p + 1)] for p in range(8)]
    ketm = [ARD[:, 512 * i:512 * (i + 1)] for i in range(2)]
    mtbf = [ARG[:, 512 * i:512 * (i + 1)] for i in range(8)]
    srs = ARI[:, 0:2048].bitcast(F32)[0:33, :]
    accb = [ARI[:, 2048 + 2048 * i:2048 + 2048 * (i + 1)].bitcast(F32)[0:33, :]
            for i in range(2)]
    gth = ARI[:, 6144:8192].bitcast(F32)[0:33, :]
    dbf = ARG[:, 0:1024].bitcast(F32)
    lnd = ARG[:, 1024:2048].bitcast(F32)
    vtm = [ARG[:, 2048 + 1024 * i:2048 + 1024 * (i + 1)] for i in range(2)]
    pre1 = ARG[:, 4096 + 2048:4096 + 4096].bitcast(F32)
    sqb = ARG2[:, 0:2048].bitcast(F32)
    ln1 = ARG2[:, 2048:4096].bitcast(F32)
    ln1b = [ARG2[:, 4096 + 1024 * i:4096 + 1024 * (i + 1)] for i in range(2)]
    etm = [ARG2[:, 6144 + 512 * i:6144 + 512 * (i + 1)] for i in range(2)]
    qtm = [ARG2[:, 7168 + 512 * i:7168 + 512 * (i + 1)] for i in range(2)]
    xres = [ARH[:, 2048 * i:2048 * (i + 1)].bitcast(F32) for i in range(2)]
    qfm = [[ARH[0:64, 4096 + 2048 * s + 128 * h:4096 + 2048 * s + 128 * (h + 1)]
            for h in range(16)] for s in range(2)]
    asb = [ARI[:, 6144 + 128 * i:6144 + 128 * (i + 1)] for i in range(4)]
    xofm = [ARI[:, 6656 + 128 * i:6656 + 128 * (i + 1)] for i in range(8)]
    S32 = [ARI[0:64, 2048 * i:2048 * (i + 1)].bitcast(F32) for i in range(2)]
    Sbf = [ARI[0:64, 4096 + 1024 * i:4096 + 1024 * (i + 1)] for i in range(2)]
    hpr = ([ARK[:, 2048 * i:2048 * (i + 1)] for i in range(8)] +
           [ARA[:, 2048 * i:2048 * (i + 1)] for i in range(4)] +
           [ARG[:, 2048 * i:2048 * (i + 1)] for i in range(4)] +
           [ARG2[:, 2048 * i:2048 * (i + 1)] for i in range(4)] +
           [ARI[:, 2048 * i:2048 * (i + 1)] for i in range(4)])
    lnfm = [ARB[:, 2048 * i:2048 * (i + 1)] for i in range(8)]
    wgT = [[ARF[:, 512 * i:512 * (i + 1)] for i in range(8)],
           [ARH[:, 512 * i:512 * (i + 1)] for i in range(8)]]
    wuT = [[ARF[:, 4096 + 512 * i:4096 + 512 * (i + 1)] for i in range(8)],
           [ARH[:, 4096 + 512 * i:4096 + 512 * (i + 1)] for i in range(8)]]
    wdT = ([ARD[:, 1024 * i:1024 * (i + 1)] for i in range(2)] +
           [ARE[:, 1024 * i:1024 * (i + 1)] for i in range(16)] +
           [ARF[:, 1024 * i:1024 * (i + 1)] for i in range(6)])
    ltm = [ARB[:, 2048 * i:2048 * (i + 1)].bitcast(F32) for i in range(2)]
    outb = [ARB[:, 4096 + 2048 * i:4096 + 2048 * (i + 1)].bitcast(F32)
            for i in range(2)]
    sq2 = ARB[:, 8192:10240].bitcast(F32)
    pr2 = ARB[:, 10240:12288].bitcast(F32)
    silb = ARD[:, 1024:2048].bitcast(F32)

    PB = [psum(f"PB{i}", [128, 512], F32) for i in range(4)]
    PXO = [psum(f"PXO{i}", [128, 512], F32) for i in range(2)]
    PDS = psum("PDS", [64, 512], F32)
    PT = psum("PT", [128, 1024], BF16)

    S = Sched()
    MARK = {}
    _B = {}

    def buf(name):
        if name not in _B:
            _B[name] = Buf(name)
        return _B[name]

    tap_list = []

    def tap(name, ap, bnames, lvl=1):
        if not taps or lvl > int(taps):
            return
        td = nc.dram_tensor("tap_" + name, list(ap.shape), ap.dtype,
                            kind="ExternalOutput")
        op = S.add("pool", lambda e, td=td, ap=ap: e.dma_start(
            out=td[tuple(slice(None) for _ in ap.shape)], in_=ap),
            reads=[buf(b) for b in bnames], is_dma=True, dkey="taps")
        tap_list.append(op)

    for t, d_, nm in ((ident, ident_d, "ident"), (u8t, u8_d, "u8t"),
                      (m1t, m1_d, "m1t"),
                      (onesr, onesr_d, "onesr"), (ones1, ones1_d, "ones1"),
                      (padv, padv_d, "padv"), (rvec, rvec_d, "rvec")):
        S.add("sp", lambda e, t=t, d_=d_: e.dma_start(out=t[:, :], in_=d_[:, :]),
              writes=[buf(nm)], is_dma=True, dkey="consts")
    b_id = buf("ident")
    const_ops = list(S.ops)

    for i in range(8):
        s = i % 2
        S.add("sp", lambda e, i=i, s=s: e.dma_start(
            out=wst[s][:, :], in_=wqkv_d[128 * i:128 * (i + 1), :]),
            writes=[buf(f"wst{s}")], is_dma=True)
        S.add("act", lambda e, i=i, s=s: e.copy(out=qk_bf[i][:, :], in_=wst[s][:, :]),
              reads=[buf(f"wst{s}")], writes=[buf(f"qkbf{i}")])
    S.add("sp", lambda e: e.dma_start(
        out=wst[0][0:16, 0:NH * HD].rearrange("c (h d) -> c h d", d=HD),
        in_=code_d[:, :].rearrange("(h c) d -> c h d", c=CS)),
        writes=[buf("wst0")], is_dma=True)
    S.add("act", lambda e: e.mul(out=codeb[:, :], in_=wst[0][0:16, 0:NH * HD],
                                 mul=SCALE),
          reads=[buf("wst0")], writes=[buf("codeb")], deps=const_ops)
    b_ptc = buf("bk_pt")
    for h in range(NH):
        S.add("pe", lambda e, h=h: e.transpose(
            PT[0:64, 16 * h:16 * h + 16], codeb[:, 64 * h:64 * h + 64],
            ident[0:16, 0:16]),
            reads=[buf("codeb"), b_id], writes=[b_ptc], excl=(h == 0))
        S.add("pe", lambda e, h=h: e.transpose(
            PT[64:128, 16 * h:16 * h + 16], codeb[:, 64 * h:64 * h + 64],
            ident[0:16, 0:16]),
            reads=[buf("codeb"), b_id], writes=[b_ptc], excl=False)
    S.add("dve", lambda e: e.tensor_copy(out=codeT[:, 0:NH * CS],
                                         in_=PT[:, 0:NH * CS]),
          reads=[b_ptc], writes=[buf("codeT")])
    for i in range(8):
        S.add("dve", lambda e, i=i: e.memset(wtq[i][:, :], 0.0),
              writes=[buf(f"wtq{i}")])
        S.add("dve", lambda e, i=i: e.memset(wtk[i][:, :], 0.0),
              writes=[buf(f"wtk{i}")])
    for i in range(8):
        S.add("dve", lambda e, i=i: e.memset(codeblk[i][:, :], 0.0),
              writes=[buf(f"cbk{i}")])
        S.add("dve", lambda e, i=i: e.tensor_copy(
            out=codeblk[i][0:64, 32 * i:32 * i + 16],
            in_=codeT[0:64, 32 * i:32 * i + 16]),
            reads=[buf("codeT")], writes=[buf(f"cbk{i}")], excl=False)
        S.add("dve", lambda e, i=i: e.tensor_copy(
            out=codeblk[i][64:128, 32 * i + 16:32 * i + 32],
            in_=codeT[64:128, 32 * i + 16:32 * i + 32]),
            reads=[buf("codeT")], writes=[buf(f"cbk{i}")], excl=False)
    for which, wt in (("q", wtq), ("k", wtk)):
        if which == "k":
            for i in range(8):
                s = i % 2
                S.add("sp", lambda e, i=i, s=s: e.dma_start(
                    out=wst[s][:, :],
                    in_=wqkv_d[1024 + 128 * i:1024 + 128 * (i + 1), :]),
                    writes=[buf(f"wst{s}")], is_dma=True)
                S.add("act", lambda e, i=i, s=s: e.copy(out=qk_bf[i][:, :],
                                                        in_=wst[s][:, :]),
                      reads=[buf(f"wst{s}")], writes=[buf(f"qkbf{i}")])
        for dc in range(8):
            pb = buf("bk_pb0")
            for i in range(8):
                S.add("pe", lambda e, i=i, dc=dc: e.matmul(
                    PB[0][:, 0:256],
                    qk_bf[i][:, 128 * dc:128 * dc + 128],
                    codeblk[i][:, :], start=(i == 0), stop=(i == 7)),
                    reads=[buf(f"qkbf{i}"), buf(f"cbk{i}")], writes=[pb],
                    excl=(i == 0))
            S.add("dve", lambda e, wt=wt, dc=dc: e.tensor_copy(
                out=wt[dc].rearrange("p (h c) -> p h c", c=32)[:, :, 0:16],
                in_=PB[0][:, 0:256].rearrange("p (h c) -> p h c", c=16)),
                reads=[pb], writes=[buf(f"wt{which}{dc}")], excl=False)
    wt_done = [o for i in range(8)
               for o in (_B[f"wtq{i}"].writers + _B[f"wtk{i}"].writers)]

    pst = [ARI[:, 0:2048].bitcast(I32), ARI[:, 2048:4096].bitcast(I32)]
    pbf = [ARI[:, 4096:5120], ARI[:, 5120:6144]]
    prep_jobs = []
    for nm2, src2, scr2 in (("wg", wg_d, wgb_s), ("wu", wu_d, wub_s)):
        for i2 in range(24):
            prep_jobs.append((nm2, scr2,
                              src2[128 * i2:128 * (i2 + 1), :],
                              (128 * i2, 0)))
    for i2 in range(8):
        for c3 in range(3):
            prep_jobs.append(("wd", wdb_s,
                              wd_d[128 * i2:128 * (i2 + 1),
                                   1024 * c3:1024 * (c3 + 1)],
                              (128 * i2, 1024 * c3)))
    prep_loads, prep_casts, prep_stores = [], [], []

    def prep_load(k):
        src_ap = prep_jobs[k][2]
        s2 = k % 2
        prep_loads.append(S.add("pool", lambda e, src_ap=src_ap, s2=s2:
                                e.dma_start(out=pst[s2][:, :], in_=src_ap),
                                writes=[buf(f"pst{s2}")], is_dma=True))
    pk = [0]

    def prep_step(n):
        for _ in range(n):
            k = pk[0]
            if k >= len(prep_jobs):
                return
            pk[0] += 1
            nm2, scr2, _, (r0, c0) = prep_jobs[k]
            s2 = k % 2
            prep_casts.append(S.add("pool", lambda e, s2=s2: e.tensor_copy(
                out=pbf[s2][:, :], in_=pst[s2][:, :]),
                reads=[buf(f"pst{s2}")], writes=[buf(f"pbf{s2}")]))
            prep_stores.append(S.add(
                "pool", lambda e, scr2=scr2, r0=r0, c0=c0, s2=s2: e.dma_start(
                    out=scr2[r0:r0 + 128, c0:c0 + 1024], in_=pbf[s2][:, :]),
                reads=[buf(f"pbf{s2}")], writes=[buf(f"scr_{nm2}")],
                is_dma=True, excl=False, dkey=f"pbf{s2}"))
            if k + 2 < len(prep_jobs):
                prep_load(k + 2)

    def xres_load(j):
        s = j % 2
        S.add("pool", lambda e, j=j, s=s: e.dma_start(
            out=xres[s][:, :], in_=x_d[CH * j:CH * (j + 1), :]),
            writes=[buf(f"xres{s}")], is_dma=True)
    prep_load(0)
    prep_load(1)
    xres_load(0)
    xres_load(1)
    for j in range(NCH):
        s = j % 2
        S.add("pool", lambda e, s=s: e.tensor_copy(out=xtb[s][:, :],
                                                   in_=xres[s][:, :]),
              reads=[buf(f"xres{s}")], writes=[buf(f"xtb{s}")])
        S.add("pool", lambda e, j=j, s=s: e.dma_start(
            out=xb_s[CH * j:CH * (j + 1), :], in_=xtb[s][:, :]),
            reads=[buf(f"xtb{s}")], writes=[buf("scr_x")], is_dma=True,
            excl=False, dkey=f"xtb{s}")
        if j + 2 < NCH:
            xres_load(j + 2)
        prep_step(2)
    xfm_ops = []
    for dc in range(8):
        xfm_ops.append(S.add("sp", lambda e, dc=dc: e.dma_start(
            out=xfm[dc][:, :], in_=xb_s[:, 128 * dc:128 * dc + 128],
            transpose=True),
            reads=[buf("scr_x")], writes=[buf(f"xfm{dc}")], is_dma=True,
            deps=wt_done, dkey=f"xfmT{dc % 2}"))
    prep_step(len(prep_jobs))
    prep_guard = [prep_casts[-1], prep_loads[-1], prep_loads[-2],
                  prep_stores[-1], prep_stores[-2]]

    wvwo_casts = []
    for name, scr in (("wv", wvb_s), ("wo", wob_s)):
        for i in range(8):
            s = i % 2
            if name == "wv":
                S.add("sp", lambda e, i=i, s=s: e.dma_start(
                    out=wst[s][:, :],
                    in_=wqkv_d[2048 + 128 * i:2048 + 128 * (i + 1), :]),
                    writes=[buf(f"wst{s}")], is_dma=True)
            else:
                S.add("sp", lambda e, i=i, s=s: e.dma_start(
                    out=wst[s][:, :], in_=wout_d[128 * i:128 * (i + 1), :]),
                    writes=[buf(f"wst{s}")], is_dma=True)
            wvwo_casts.append(S.add(
                "act", lambda e, s=s: e.copy(out=wbf[s][:, :], in_=wst[s][:, :]),
                reads=[buf(f"wst{s}")], writes=[buf(f"wbf{s}")]))
            S.add("sp", lambda e, scr=scr, i=i, s=s: e.dma_start(
                out=scr[128 * i:128 * (i + 1), :], in_=wbf[s][:, :]),
                reads=[buf(f"wbf{s}")], writes=[buf(f"scr_{name}")],
                is_dma=True, excl=False, dkey=f"wbf{s}")
    wvT_ops, woT_ops = [], []
    for name, scr, dstT, lst in (("wv", wvb_s, wvT, wvT_ops),
                                 ("wo", wob_s, woT, woT_ops)):
        for dc in range(8):
            lst.append(S.add("sp", lambda e, scr=scr, dstT=dstT, dc=dc: e.dma_start(
                out=dstT[dc][:, :], in_=scr[:, 128 * dc:128 * dc + 128],
                transpose=True),
                reads=[buf(f"scr_{name}")], writes=[buf(f"{name}T{dc}")],
                is_dma=True, dkey=f"{name}T"))

    MARK["P0"] = len(S.ops)
    tap("wtq0", wtq[0][:, :], ["wtq0"], lvl=1)
    tap("wtk0", wtk[0][:, :], ["wtk0"], lvl=1)
    tap("xfm0", xfm[0][:, 0:512], ["xfm0"], lvl=1)
    tap("wvT0", wvT[0][:, :], ["wvT0"], lvl=1)
    tap("woT0", woT[0][:, :], ["woT0"], lvl=1)

    for p in range(8):
        for ts in range(4):
            gi = (p * 4 + ts) % 2
            pb = buf(f"bk_pb{gi}")
            for dc in range(8):
                S.add("pe", lambda e, p=p, ts=ts, dc=dc, gi=gi: e.matmul(
                    PB[gi][0:64, 0:512], wtk[dc][:, 64 * p:64 * p + 64],
                    xfm[dc][:, 512 * ts:512 * ts + 512],
                    start=(dc == 0), stop=(dc == 7)),
                    reads=[buf(f"wtk{dc}"), buf(f"xfm{dc}")], writes=[pb],
                    excl=(dc == 0),
                    deps=(wt_done + xfm_ops) if (p == 0 and ts == 0) else ())
            S.add("dve", lambda e, p=p, ts=ts, gi=gi: e.tensor_reduce(
                out=mred[:, ts:ts + 1], in_=PB[gi][0:64, 0:512],
                axis=AX.X, op=ALU.max),
                reads=[pb], writes=[buf("mred")], excl=(ts == 0), strict=True)
        S.add("dve", lambda e, p=p: e.tensor_reduce(
            out=mloc[:, p:p + 1], in_=mred[:, :], axis=AX.X, op=ALU.max),
            reads=[buf("mred")], writes=[buf("mloc")], excl=False, strict=True)

    tap("mlocpre", mloc[:, :], ["mloc"], lvl=1)
    S.add("pool", lambda e: e.dma_start(out=ccm_in[:, :], in_=mloc[:, :]),
          reads=[buf("mloc")], writes=[buf("ccm")], is_dma=True)
    if taps and int(taps) >= 1:
        tccin = nc.dram_tensor("tap_ccmin", [64, 8], F32, kind="ExternalOutput")
        tap_list.append(S.add("act", lambda e: e.dma_start(
            out=tccin[:, :], in_=ccm_in[:, :]),
            reads=[buf("ccm")], is_dma=True, dkey="taps"))
    S.add("pool", lambda e: e.collective_compute(
        "AllReduce", ALU.max, replica_groups=[list(range(NCORES))],
        ins=[ccm_in.ap().opt()], outs=[ccm_out.ap().opt()]),
        reads=[buf("ccm")], writes=[buf("ccmo")], strict=True)
    if taps and int(taps) >= 1:
        tccout = nc.dram_tensor("tap_ccmout", [64, 8], F32, kind="ExternalOutput")
        tap_list.append(S.add("act", lambda e: e.dma_start(
            out=tccout[:, :], in_=ccm_out[:, :]),
            reads=[buf("ccmo")], is_dma=True, dkey="taps"))
    S.add("pool", lambda e: e.dma_start(out=mloc[:, :], in_=ccm_out[:, :]),
          reads=[buf("ccmo")], writes=[buf("mloc")], is_dma=True, dkey="mback")
    tap("padv2", padv[:, :], ["padv"], lvl=1)
    S.add("dve", lambda e: e.tensor_scalar(
        out=negm[:, :], in0=mloc[:, :], scalar1=padv[:, 0:1], scalar2=-1.0,
        op0=ALU.max, op1=ALU.mult),
        reads=[buf("mloc"), buf("padv")], writes=[buf("negm")])

    MARK["P1"] = len(S.ops)
    tap("mloc", mloc[:, :], ["mloc"], lvl=1)
    tap("negm", negm[:, :], ["negm"], lvl=1)
    kexp_ops = []
    for p in range(8):
        for ts in range(4):
            gi = (p * 4 + ts) % 2
            pb = buf(f"bk_pb{gi}")
            for dc in range(8):
                S.add("pe", lambda e, p=p, ts=ts, dc=dc, gi=gi: e.matmul(
                    PB[gi][0:64, 0:512], wtk[dc][:, 64 * p:64 * p + 64],
                    xfm[dc][:, 512 * ts:512 * ts + 512],
                    start=(dc == 0), stop=(dc == 7)),
                    reads=[buf(f"wtk{dc}"), buf(f"xfm{dc}")], writes=[pb],
                    excl=(dc == 0))
            o = S.add("act", lambda e, p=p, ts=ts, gi=gi: e.activation(
                out=kefm[p][:, 512 * ts:512 * ts + 512], in_=PB[gi][0:64, 0:512],
                func=AF.Exp, bias=negm[:, p:p + 1], scale=1.0),
                reads=[pb, buf("negm")], writes=[buf(f"kefm{p}")], excl=False)
            kexp_ops.append(o)
        S.add("act", lambda e, p=p: e.dma_start(
            out=kef_s[64 * p:64 * (p + 1), :], in_=kefm[p][:, :]),
            reads=[buf(f"kefm{p}")], writes=[buf("scr_kef")], is_dma=True,
            excl=False, dkey="scr_kef")

    ztot_pb = buf("bk_pds")
    mt_pb = [buf("bk_pb0"), buf("bk_pb1"), buf("bk_pb2"), buf("bk_pb3"),
             buf("bk_pxo0"), buf("bk_pxo1"), buf("bk_pb0"), buf("bk_pb1")]
    for j in range(NCH):
        ks = j % 2
        S.add("sp", lambda e, j=j, ks=ks: e.dma_start(
            out=ketm[ks][:, :], in_=kef_s[:, CH * j:CH * j + 128], transpose=True),
            reads=[buf("scr_kef")], writes=[buf(f"ketm{ks}")], is_dma=True)
        S.add("pool", lambda e, j=j, ks=ks: e.dma_start(
            out=xres[ks][:, :], in_=x_d[CH * j:CH * (j + 1), :]),
            writes=[buf(f"xres{ks}")], is_dma=True)
        S.add("act", lambda e, j=j, ks=ks: e.copy(out=xsb[ks][:, :],
                                                  in_=xres[ks][:, :]),
              reads=[buf(f"xres{ks}")], writes=[buf(f"xsb{ks}")],
              deps=(kexp_ops + wt_done) if j < 2 else ())
        S.add("pe", lambda e, j=j, ks=ks: e.matmul(
            PDS[0:1, 0:NR], ones1[:, :], ketm[ks][:, :],
            start=(j == 0), stop=(j == NCH - 1)),
            reads=[buf(f"ketm{ks}"), buf("ones1")], writes=[ztot_pb],
            excl=(j == 0))
        for dc in range(6):
            pp = PB[dc] if dc < 4 else PXO[dc - 4]
            S.add("pe", lambda e, j=j, ks=ks, dc=dc, pp=pp: e.matmul(
                pp[:, 0:NR], xsb[ks][:, 128 * dc:128 * dc + 128],
                ketm[ks][:, :], start=(j == 0), stop=(j == NCH - 1)),
                reads=[buf(f"xsb{ks}"), buf(f"ketm{ks}")], writes=[mt_pb[dc]],
                excl=(j == 0))
    S.add("dve", lambda e: e.tensor_copy(out=row127[0:1, 0:NR],
                                         in_=PDS[0:1, 0:NR]),
          reads=[ztot_pb], writes=[buf("zr")], excl=False)
    for dc in range(6):
        pp = PB[dc] if dc < 4 else PXO[dc - 4]
        S.add("dve", lambda e, dc=dc, pp=pp: e.tensor_copy(
            out=mtbf[dc][:, :], in_=pp[:, 0:NR]),
            reads=[mt_pb[dc]], writes=[buf(f"mtbf{dc}")],
            deps=(wvwo_casts if dc == 0 else ()))
    for j in range(NCH):
        ks = j % 2
        S.add("sp", lambda e, j=j, ks=ks: e.dma_start(
            out=ketm[ks][:, :], in_=kef_s[:, CH * j:CH * j + 128], transpose=True),
            reads=[buf("scr_kef")], writes=[buf(f"ketm{ks}")], is_dma=True)
        S.add("pool", lambda e, j=j, ks=ks: e.dma_start(
            out=xres[ks][:, :], in_=x_d[CH * j:CH * (j + 1), :]),
            writes=[buf(f"xres{ks}")], is_dma=True)
        S.add("act", lambda e, j=j, ks=ks: e.copy(out=xsb[ks][:, :],
                                                  in_=xres[ks][:, :]),
              reads=[buf(f"xres{ks}")], writes=[buf(f"xsb{ks}")])
        for dc in range(6, 8):
            S.add("pe", lambda e, j=j, ks=ks, dc=dc: e.matmul(
                PB[dc - 6][:, 0:NR], xsb[ks][:, 128 * dc:128 * dc + 128],
                ketm[ks][:, :], start=(j == 0), stop=(j == NCH - 1)),
                reads=[buf(f"xsb{ks}"), buf(f"ketm{ks}")], writes=[mt_pb[dc]],
                excl=(j == 0))
    for dc in range(6, 8):
        S.add("dve", lambda e, dc=dc: e.tensor_copy(
            out=mtbf[dc][:, :], in_=PB[dc - 6][:, 0:NR]),
            reads=[mt_pb[dc]], writes=[buf(f"mtbf{dc}")])
    for hb in range(2):
        pbsr = buf("bk_pds")
        for h in range(8 * hb, 8 * hb + 8):
            for dc in range(8):
                S.add("pe", lambda e, h=h, dc=dc, hb=hb: e.matmul(
                    PDS[0:32, 64 * h - 512 * hb:64 * h - 512 * hb + 64],
                    mtbf[dc][:, 32 * h:32 * h + 32],
                    wvT[dc][:, 64 * h:64 * h + 64], start=(dc == 0),
                    stop=(dc == 7)),
                    reads=[buf(f"mtbf{dc}"), buf(f"wvT{dc}")], writes=[pbsr],
                    excl=(h == 8 * hb and dc == 0),
                    deps=(wvT_ops if (h == 0 and dc == 0) else ()))
        S.add("dve", lambda e, hb=hb: e.tensor_copy(
            out=srs[0:32, 512 * hb:512 * hb + 512], in_=PDS[0:32, :]),
            reads=[pbsr], writes=[buf("srs")], excl=False, deps=prep_guard)
    S.add("pool", lambda e: e.dma_start(out=ccs_in[0:32, :], in_=srs[0:32, :]),
          reads=[buf("srs")], writes=[buf("ccs")], is_dma=True, excl=False)
    S.add("pool", lambda e: e.dma_start(out=ccs_in[32:33, 0:NR],
                                        in_=row127[:, :]),
          reads=[buf("zr")], writes=[buf("ccs")], is_dma=True, excl=False,
          dkey="ccs")
    S.add("dve", lambda e: e.memset(carrf[1][:, :], 0.0),
          writes=[buf("carrf1")])
    S.add("pool", lambda e: e.dma_start(out=ccs_in[32:33, NR:1024],
                                        in_=carrf[1][:, :]),
          reads=[buf("carrf1")], writes=[buf("ccs")], is_dma=True, excl=False,
          dkey="ccs")
    S.add("pool", lambda e: e.collective_compute(
        "AllGather", ALU.bypass, replica_groups=[list(range(NCORES))],
        ins=[ccs_in.ap().opt()], outs=[ccs_out.ap().opt()]),
        reads=[buf("ccs")], writes=[buf("ccso")], strict=True)

    MARK["P2"] = len(S.ops)
    tap("kefm0", kefm[0][:, 0:512], ["kefm0"], lvl=4)
    tap("srs", srs[:, :], ["srs"], lvl=4)
    tap("mtbf0", mtbf[0][:, :], ["mtbf0"], lvl=4)
    S.add("dve", lambda e: e.memset(accb[0][:, :], 0.0), writes=[buf("accb0")],
          deps=prep_guard)
    for r in range(NCORES):
        S.add("pool", lambda e, r=r: e.dma_start(
            out=gth[:, :], in_=ccs_out[33 * r:33 * (r + 1), :]),
            reads=[buf("ccso")], writes=[buf("gth")], is_dma=True)
        a, b2 = r % 2, (r + 1) % 2
        S.add("dve", lambda e, r=r, a=a, b2=b2: e.scalar_tensor_tensor(
            out=accb[b2][:, :], in0=gth[:, :], scalar=rvec[:, r:r + 1],
            in1=accb[a][:, :], op0=ALU.mult, op1=ALU.add),
            reads=[buf("gth"), buf(f"accb{a}"), buf("rvec")],
            writes=[buf(f"accb{b2}")],
            deps=(prep_guard if r == 0 else ()))
    accf = buf(f"accb{NCORES % 2}")
    accT = accb[NCORES % 2]
    S.add("pool", lambda e: e.dma_start(out=dup_s[:, :], in_=accT[:, :]),
          reads=[accf], writes=[buf("dup")], is_dma=True)
    S.add("pool", lambda e: e.dma_start(out=S32[0][32:64, :], in_=dup_s[0:32, :]),
          reads=[buf("dup")], writes=[buf("S32_0")], is_dma=True, excl=False,
          dkey="sdup", deps=prep_guard)
    S.add("pool", lambda e: e.dma_start(out=row127[:, :],
                                        in_=dup_s[32:33, 0:NR]),
          reads=[buf("dup"), buf("zr")], writes=[buf("row127"), buf("zr")],
          is_dma=True)
    S.add("dve", lambda e: e.tensor_scalar(
        out=carrf[0][:, :], in0=row127[:, :], scalar1=EPS_Z, scalar2=8.0,
        op0=ALU.add, op1=ALU.mult),
        reads=[buf("row127")], writes=[buf("carrf0")])
    S.add("act", lambda e: e.copy(out=carrb[0][:, :], in_=carrf[0][:, :]),
          reads=[buf("carrf0")], writes=[buf("carrb0")])
    S.add("dve", lambda e: e.tensor_copy(out=S32[0][0:32, :], in_=accT[0:32, :]),
          reads=[accf], writes=[buf("S32_0")], excl=False)
    S.add("act", lambda e: e.copy(out=Sbf[0][:, :], in_=S32[0][:, :]),
          reads=[buf("S32_0")], writes=[buf("Sbf_0")])

    MARK["P3"] = len(S.ops)
    tap("acc", accT[:, :], [f"accb{NCORES % 2}"], lvl=4)
    tap("carrf0", carrf[0][:, :], ["carrf0"], lvl=4)
    tap("S32i", S32[0][:, :], ["S32_0"], lvl=4)
    mtbf_rd = [o for dc2 in range(8) for o in _B[f"mtbf{dc2}"].readers]
    for s in range(2):
        for h in range(NH):
            if h % 2 == 0:
                S.add("dve", lambda e, s=s, h=h: e.memset(
                    qfm[s][h][32:64, :], 0.0), writes=[buf(f"qfm{s}_{h}")])
            else:
                S.add("dve", lambda e, s=s, h=h: e.memset(
                    qfm[s][h][0:32, :], 0.0), writes=[buf(f"qfm{s}_{h}")])
    prev_a_evacs = []

    def emit_ketm_load(jn):
        kn = jn % 2
        S.add("sp", lambda e, jn=jn, kn=kn: e.dma_start(
            out=ketm[kn][:, :], in_=kef_s[:, CH * jn:CH * jn + 128],
            transpose=True),
            reads=[buf("scr_kef")], writes=[buf(f"ketm{kn}")], is_dma=True)

    def emit_prefix(j):
        sj = j % 2
        ks = j % 2
        if j + 1 < NCH:
            emit_ketm_load(j + 1)
        pbq = buf("bk_pb0")
        for dc in range(8):
            S.add("pe", lambda e, j=j, dc=dc: e.matmul(
                PB[0][:, 0:NR], xfm[dc][:, CH * j:CH * j + 128], wtq[dc][:, :],
                start=(dc == 0), stop=(dc == 7)),
                reads=[buf(f"xfm{dc}"), buf(f"wtq{dc}")], writes=[pbq],
                excl=(dc == 0))
        S.add("act", lambda e, sj=sj: e.activation(
            out=etm[sj][:, :], in_=PB[0][:, 0:NR], func=AF.Exp),
            reads=[pbq], writes=[buf(f"etm{sj}")])
        for half in range(2):
            pbv = buf(f"bk_pb{2 + half}")
            for dc in range(8):
                S.add("pe", lambda e, j=j, dc=dc, half=half: e.matmul(
                    PB[2 + half][:, 0:512], xfm[dc][:, CH * j:CH * j + 128],
                    wvT[dc][:, 512 * half:512 * half + 512],
                    start=(dc == 0), stop=(dc == 7)),
                    reads=[buf(f"xfm{dc}"), buf(f"wvT{dc}")], writes=[pbv],
                    excl=(dc == 0),
                    deps=(woT_ops if (j == 0 and dc == 0 and half == 0) else ()))
            S.add("dve", lambda e, half=half, sj=sj: e.tensor_copy(
                out=vtm[sj][:, 512 * half:512 * half + 512],
                in_=PB[2 + half][:, 0:512]),
                reads=[pbv], writes=[buf(f"vtm{sj}")], excl=(half == 0),
                deps=(mtbf_rd if j < 2 else ()))
        pbz = buf("bk_pb1")
        zop1 = S.add("pe", lambda e, j=j, ks=ks: e.matmul(
            PB[1][:, 0:NR], u8t[:, :], ketm[ks][:, :], start=True, stop=False),
            reads=[buf("u8t"), buf(f"ketm{ks}")], writes=[pbz],
            deps=prev_a_evacs)
        zop2 = S.add("pe", lambda e, j=j: e.matmul(
            PB[1][:, 0:NR], onesr[:, :], carrb[j % 2][:, :],
            start=False, stop=True),
            reads=[buf("onesr"), buf(f"carrb{j % 2}")], writes=[pbz], excl=False)
        S.add("dve", lambda e, sj=sj: e.tensor_reduce(
            out=hsb[sj][:, :],
            in_=etm[sj].rearrange("p (h c) -> p h c", c=32)[:, :, 0:16],
            axis=AX.X, op=ALU.add),
            reads=[buf(f"etm{sj}")], writes=[buf(f"hsb{sj}")], strict=True)
        dop = S.add("dve", lambda e, sj=sj: e.tensor_tensor(
            out=dbf.rearrange("p (h z) -> p h z", z=32),
            in0=PB[1][:, 0:NR].rearrange("p (h z) -> p h z", z=32),
            in1=hsb[sj].rearrange("p (h o) -> p h o", o=1).broadcast_to(
                [128, 16, 32]),
            op=ALU.mult),
            reads=[pbz, buf(f"hsb{sj}")], writes=[buf("dbf")],
            deps=(mtbf_rd if j == 0 else ()))
        S.add("act", lambda e: e.activation(out=lnd[:, :], in_=dbf[:, :],
                                            func=AF.Ln),
              reads=[buf("dbf")], writes=[buf("lnd")])
        S.add("dve", lambda e: e.tensor_tensor(
            out=dbf[:, :], in0=PB[0][:, 0:NR], in1=lnd[:, :], op=ALU.subtract),
            reads=[pbq, buf("lnd")], writes=[buf("dbf")])
        S.add("act", lambda e, sj=sj: e.activation(
            out=qtm[sj][:, :], in_=dbf[:, :], func=AF.Exp),
            reads=[buf("dbf")], writes=[buf(f"qtm{sj}")])
        S.add("pe", lambda e, j=j, ks=ks: e.matmul(
            PB[0][0:1, 0:NR], ones1[:, :], ketm[ks][:, :],
            start=True, stop=True),
            reads=[buf("ones1"), buf(f"ketm{ks}")], writes=[pbq])
        S.add("dve", lambda e, j=j: e.scalar_tensor_tensor(
            out=carrf[(j + 1) % 2][:, :], in0=PB[0][0:1, 0:NR], scalar=8.0,
            in1=carrf[j % 2][:, :], op0=ALU.mult, op1=ALU.add),
            reads=[pbq, buf(f"carrf{j % 2}")],
            writes=[buf(f"carrf{(j + 1) % 2}")])
        S.add("act", lambda e, j=j: e.copy(out=carrb[(j + 1) % 2][:, :],
                                           in_=carrf[(j + 1) % 2][:, :]),
              reads=[buf(f"carrf{(j + 1) % 2}")],
              writes=[buf(f"carrb{(j + 1) % 2}")])
        ptq = buf("bk_pt")
        for h in range(NH):
            p, lo = h >> 1, 32 * (h & 1)
            S.add("pe", lambda e, sj=sj, h=h, p=p, lo=lo: e.transpose(
                PT[lo:lo + 32, 128 * p:128 * p + 128],
                qtm[sj][:, 32 * h:32 * h + 32], ident[:, :]),
                reads=[buf(f"qtm{sj}"), b_id], writes=[ptq], excl=(h == 0))
        for h in range(NH):
            p, lo = h >> 1, 32 * (h & 1)
            S.add("act", lambda e, sj=sj, h=h, p=p, lo=lo: e.copy(
                out=qfm[sj][h][lo:lo + 32, :],
                in_=PT[lo:lo + 32, 128 * p:128 * p + 128]),
                reads=[ptq], writes=[buf(f"qfm{sj}_{h}")], excl=False)
        return [dop, zop1, zop2]

    def emit_body(j, dops):
        nonlocal_holder = None
        a_evacs = []

        def emit_amm(h):
            p = h >> 1
            pa = buf(f"pa{h % 4}")
            dd = (list(dops) if h < 4 else []) + a_evacs[-1:]
            S.add("pe", lambda e, j=j, p=p, h=h: e.matmul(
                PB[1][:, 128 * (h % 4):128 * (h % 4) + 128],
                kefm[p][0:64, CH * j:CH * j + 128],
                qfm[j % 2][h][0:64, :], start=True, stop=True),
                reads=[buf(f"kefm{p}"), buf(f"qfm{j % 2}_{h}")], writes=[pa],
                deps=dd)
            aev = S.add("dve", lambda e, h=h: e.scalar_tensor_tensor(
                out=asb[h % 4][:, :],
                in0=PB[1][:, 128 * (h % 4):128 * (h % 4) + 128],
                scalar=1.0, in1=m1t[:, :], op0=ALU.mult, op1=ALU.mult),
                reads=[pa, buf("m1t")], writes=[buf(f"asb{h % 4}")])
            a_evacs.append(aev)

        emit_amm(0)
        for h in range(NH):
            p, vc = h >> 1, 64 * h
            ti = h >> 1
            hb = h // 8
            pds_b = buf("bk_pds")
            ks = j % 2
            S.add("pe", lambda e, h=h, vc=vc, j=j, ks=ks, hb=hb: e.matmul(
                PDS[0:32, 64 * h - 512 * hb:64 * h - 512 * hb + 64],
                ketm[ks][:, 32 * h:32 * h + 32], vtm[j % 2][:, vc:vc + 64],
                start=True, stop=True),
                reads=[buf(f"ketm{ks}"), buf(f"vtm{j % 2}")], writes=[pds_b],
                excl=(h == 8 * hb))
            S.add("pe", lambda e, h=h, vc=vc, j=j, ks=ks, hb=hb: e.matmul(
                PDS[32:64, 64 * h - 512 * hb:64 * h - 512 * hb + 64],
                ketm[ks][:, 32 * h:32 * h + 32], vtm[j % 2][:, vc:vc + 64],
                start=True, stop=True),
                reads=[buf(f"ketm{ks}"), buf(f"vtm{j % 2}")], writes=[pds_b],
                excl=False)
            if h + 1 < NH:
                emit_amm(h + 1)
            pxo = buf(f"bk_pxo{ti // 4}")
            S.add("pe", lambda e, h=h, vc=vc, ti=ti, j=j: e.matmul(
                PXO[ti // 4][64 * (h & 1):64 * (h & 1) + 64,
                             128 * (ti % 4):128 * (ti % 4) + 128],
                vtm[j % 2][:, vc:vc + 64], asb[h % 4][:, :],
                start=True, stop=False),
                reads=[buf(f"vtm{j % 2}"), buf(f"asb{h % 4}")], writes=[pxo],
                excl=((h & 1) == 0))
            S.add("pe", lambda e, h=h, p=p, vc=vc, ti=ti, j=j:
                  e.matmul(
                PXO[ti // 4][64 * (h & 1):64 * (h & 1) + 64,
                             128 * (ti % 4):128 * (ti % 4) + 128],
                Sbf[j % 2][0:64, vc:vc + 64],
                qfm[j % 2][h][0:64, :],
                start=False, stop=True),
                reads=[buf(f"Sbf_{j % 2}"), buf(f"qfm{j % 2}_{h}")],
                writes=[pxo], excl=False)
            if h % 8 == 7:
                S.add("dve", lambda e, j=j, hb=hb: e.tensor_tensor(
                    out=S32[(j + 1) % 2][0:64, 512 * hb:512 * hb + 512],
                    in0=S32[j % 2][0:64, 512 * hb:512 * hb + 512],
                    in1=PDS[:, :], op=ALU.add),
                    reads=[buf(f"S32_{j % 2}"), pds_b],
                    writes=[buf(f"S32_{(j + 1) % 2}")], excl=(hb == 0))
        return a_evacs

    def emit_tail(j):
        sj = j % 2
        so = 8 * sj
        S.add("act", lambda e, j=j: e.copy(out=Sbf[(j + 1) % 2][:, :],
                                           in_=S32[(j + 1) % 2][:, :]),
              reads=[buf(f"S32_{(j + 1) % 2}")],
              writes=[buf(f"Sbf_{(j + 1) % 2}")])
        for ti in range(8):
            S.add("dve", lambda e, ti=ti: e.tensor_copy(
                out=xofm[ti][:, :],
                in_=PXO[ti // 4][:, 128 * (ti % 4):128 * (ti % 4) + 128]),
                reads=[buf(f"bk_pxo{ti // 4}")], writes=[buf(f"xofm{ti}")])
        for half in range(2):
            pbo = buf(f"bk_pb{2 + half}")
            for ti in range(8):
                S.add("pe", lambda e, ti=ti, half=half: e.matmul(
                    PB[2 + half][:, 0:512], xofm[ti][:, :],
                    woT[ti][:, 512 * half:512 * half + 512],
                    start=(ti == 0), stop=(ti == 7)),
                    reads=[buf(f"xofm{ti}"), buf(f"woT{ti}")], writes=[pbo],
                    excl=(ti == 0))
        S.add("pool", lambda e, j=j, sj=sj: e.dma_start(
            out=xres[sj][:, :], in_=x_d[CH * j:CH * (j + 1), :]),
            writes=[buf(f"xres{sj}")], is_dma=True,
            deps=(mtbf_rd if j < 2 else ()))
        for half in range(2):
            S.add("dve", lambda e, half=half, sj=sj: e.tensor_tensor(
                out=pre1[:, 512 * half:512 * half + 512],
                in0=PB[2 + half][:, 0:512],
                in1=xres[sj][:, 512 * half:512 * half + 512], op=ALU.add),
                reads=[buf(f"bk_pb{2 + half}"), buf(f"xres{sj}")],
                writes=[buf("pre1")], excl=(half == 0))
        if j == 0:
            tap("etm0", etm[0][:, :], ["etm0"], lvl=2)
            tap("hsb0", hsb[0][:, :], ["hsb0"], lvl=2)
            tap("qtm0", qtm[0][:, :], ["qtm0"], lvl=2)
            tap("ketm0", ketm[0][:, :], ["ketm0"], lvl=2)
            tap("vtm0", vtm[0][:, :], ["vtm0"], lvl=2)
            tap("qfm0", qfm[0][0][0:64, :], ["qfm0_0"], lvl=2)
            tap("qfm1", qfm[0][1][0:64, :], ["qfm0_1"], lvl=2)
            tap("asb0", asb[0][:, :], ["asb0"], lvl=2)
            tap("xofm0", xofm[0][:, :], ["xofm0"], lvl=2)
            tap("S32n", S32[1][:, :], ["S32_1"], lvl=2)
            tap("pre1", pre1[:, :], ["pre1"], lvl=2)
        _emit_ln(S, buf, pre1, sqb, stat, so, ln1, "ln1", "pre1", "sqb")
        if j == 0:
            tap("ln1", ln1[:, :], ["ln1"], lvl=2)
        S.add("act", lambda e, sj=sj: e.copy(out=ln1b[sj][:, :], in_=ln1[:, :]),
              reads=[buf("ln1")], writes=[buf(f"ln1b{sj}")])
        S.add("pool", lambda e, j=j: e.dma_start(
            out=lntm_s[CH * j:CH * (j + 1), :], in_=ln1[:, :]),
            reads=[buf("ln1")], writes=[buf("scr_lntm")], is_dma=True,
            excl=False, dkey="lntm")
        S.add("pool", lambda e, j=j, sj=sj: e.dma_start(
            out=lnb_s[CH * j:CH * (j + 1), :], in_=ln1b[sj][:, :]),
            reads=[buf(f"ln1b{sj}")], writes=[buf(f"scr_lnb")], is_dma=True,
            excl=False, dkey=f"lnb{sj}")

    emit_ketm_load(0)
    dops = emit_prefix(0)
    for j in range(NCH):
        MARK[f"C{j}"] = len(S.ops)
        a_evacs = emit_body(j, dops)
        prev_a_evacs = a_evacs[-4:] + [dops[0]]
        if j + 1 < NCH:
            dops_next = emit_prefix(j + 1)
        emit_tail(j)
        if j + 1 < NCH:
            dops = dops_next
    MARK["P4"] = len(S.ops)
    xfm_readers = [o for dc in range(8) for o in _B[f"xfm{dc}"].readers]
    lnfm_ops = []
    for dc in range(8):
        lnfm_ops.append(S.add("sp", lambda e, dc=dc: e.dma_start(
            out=lnfm[dc][:, :], in_=lnb_s[:, 128 * dc:128 * dc + 128],
            transpose=True),
            reads=[buf("scr_lnb")], writes=[buf(f"lnfm{dc}")], is_dma=True,
            deps=xfm_readers, dkey="lnfmT"))
    wtqk_readers = [o for i in range(8)
                    for o in (_B[f"wtq{i}"].readers + _B[f"wtk{i}"].readers)]
    scanB_readers = []
    for nm in (["xres0", "xres1", "asb0", "asb1", "asb2", "asb3"] +
               [f"xofm{i}" for i in range(8)] +
               [f"qfm{s}_{h}" for s in range(2) for h in range(16)]):
        scanB_readers += _B[nm].readers + _B[nm].writers
    for hg in range(6):
        sl = hg % 2
        slot_ops = []
        for dc in range(8):
            dep0 = []
            if hg == 0:
                dep0 = wtqk_readers
            elif hg == 1:
                dep0 = scanB_readers
            slot_ops.append(S.add("sp", lambda e, hg=hg, sl=sl, dc=dc: e.dma_start(
                out=wgT[sl][dc][:, :],
                in_=wgb_s[512 * hg:512 * (hg + 1), 128 * dc:128 * dc + 128],
                transpose=True),
                reads=[buf("scr_wg")], writes=[buf(f"wgT{sl}_{dc}")],
                is_dma=True, deps=dep0, dkey=f"wgT{sl}"))
            slot_ops.append(S.add("sp", lambda e, hg=hg, sl=sl, dc=dc: e.dma_start(
                out=wuT[sl][dc][:, :],
                in_=wub_s[512 * hg:512 * (hg + 1), 128 * dc:128 * dc + 128],
                transpose=True),
                reads=[buf("scr_wu")], writes=[buf(f"wuT{sl}_{dc}")],
                is_dma=True, deps=dep0, dkey=f"wuT{sl}"))
        for hc in range(4):
            hh = 4 * hg + hc
            for ts2 in range(4):
                pg, pu = buf("bk_pb0"), buf("bk_pb1")
                for dc in range(8):
                    S.add("pe", lambda e, sl=sl, hc=hc, ts2=ts2, dc=dc: e.matmul(
                        PB[0][:, 0:512], wgT[sl][dc][:, 128 * hc:128 * hc + 128],
                        lnfm[dc][:, 512 * ts2:512 * ts2 + 512],
                        start=(dc == 0), stop=(dc == 7)),
                        reads=[buf(f"wgT{sl}_{dc}"), buf(f"lnfm{dc}")],
                        writes=[pg], excl=(dc == 0),
                        deps=(slot_ops + lnfm_ops) if (hc == 0 and ts2 == 0
                                                       and dc == 0) else ())
                for dc in range(8):
                    S.add("pe", lambda e, sl=sl, hc=hc, ts2=ts2, dc=dc: e.matmul(
                        PB[1][:, 0:512], wuT[sl][dc][:, 128 * hc:128 * hc + 128],
                        lnfm[dc][:, 512 * ts2:512 * ts2 + 512],
                        start=(dc == 0), stop=(dc == 7)),
                        reads=[buf(f"wuT{sl}_{dc}"), buf(f"lnfm{dc}")],
                        writes=[pu], excl=(dc == 0))
                S.add("act", lambda e: e.activation(
                    out=silb[:, 0:512], in_=PB[0][:, 0:512], func=AF.Silu,
                    scale=QS),
                    reads=[pg], writes=[buf("silb")],
                    deps=(scanB_readers if (hg == 0 and hc == 0 and ts2 == 0)
                          else ()))
                S.add("dve", lambda e, hh=hh, ts2=ts2: e.tensor_tensor(
                    out=hpr[hh][:, 512 * ts2:512 * ts2 + 512],
                    in0=silb[:, 0:512], in1=PB[1][:, 0:512], op=ALU.mult),
                    reads=[buf("silb"), pu], writes=[buf(f"hpr{hh}")],
                    excl=False)
    tap("lnfm0", lnfm[0][:, 0:512], ["lnfm0"], lvl=2)
    tap("hpr0", hpr[0][:, 0:512], ["hpr0"], lvl=2)
    tap("hpr16", hpr[16][:, 0:512], ["hpr16"], lvl=2)
    wvwo_readers = [o for dc in range(8)
                    for o in (_B[f"wvT{dc}"].readers + _B[f"woT{dc}"].readers)]
    sket_readers = (_B["ketm0"].readers + _B["ketm1"].readers +
                    _B["S32_0"].readers + _B["S32_1"].readers +
                    _B["Sbf_0"].readers + _B["Sbf_1"].readers)
    wdT_ops = []
    for hh in range(24):
        wdT_ops.append(S.add("sp", lambda e, hh=hh: e.dma_start(
            out=wdT[hh][:, :], in_=wdb_s[:, 128 * hh:128 * hh + 128],
            transpose=True),
            reads=[buf("scr_wd")], writes=[buf(f"wdT{hh}")], is_dma=True,
            deps=((sket_readers if hh == 0 else _B["silb"].readers)
                  if hh < 2 else
                  (wvwo_readers if hh == 2 else
                   ([o for sl2 in range(1) for dc2 in range(8)
                     for o in _B[f"wgT0_{dc2}"].readers +
                     _B[f"wuT0_{dc2}"].readers] if hh == 18 else []))),
            dkey=f"wdT{hh % 3}"))
    yops = []
    for tj in range(NCH):
        sj = tj % 2
        so = 8 * sj
        S.add("pool", lambda e, tj=tj, sj=sj: e.dma_start(
            out=ltm[sj][:, :], in_=lntm_s[CH * tj:CH * (tj + 1), :]),
            reads=[buf("scr_lntm")], writes=[buf(f"ltm{sj}")], is_dma=True,
            deps=([o for dc2 in range(8) for o in _B[f"lnfm{dc2}"].readers]
                  if tj < 2 else []))
        for half in range(2):
            pbo = buf(f"bk_pb{2 + half}")
            for hh in range(24):
                S.add("pe", lambda e, tj=tj, half=half, hh=hh: e.matmul(
                    PB[2 + half][:, 0:512], hpr[hh][:, CH * tj:CH * tj + 128],
                    wdT[hh][:, 512 * half:512 * half + 512],
                    start=(hh == 0), stop=(hh == 23)),
                    reads=[buf(f"hpr{hh}"), buf(f"wdT{hh}")], writes=[pbo],
                    excl=(hh == 0),
                    deps=(wdT_ops if (tj == 0 and half == 0 and hh == 0)
                          else ()))
            S.add("dve", lambda e, half=half, sj=sj: e.scalar_tensor_tensor(
                out=pr2[:, 512 * half:512 * half + 512],
                in0=PB[2 + half][:, 0:512], scalar=QS2,
                in1=ltm[sj][:, 512 * half:512 * half + 512],
                op0=ALU.mult, op1=ALU.add),
                reads=[buf(f"bk_pb{2 + half}"), buf(f"ltm{sj}")],
                writes=[buf("pr2")], excl=(half == 0))
        if tj == 0:
            tap("ltm0", ltm[0][:, :], ["ltm0"], lvl=2)
            tap("pr2", pr2[:, :], ["pr2"], lvl=2)
        _emit_ln(S, buf, pr2, sq2, stat, so, outb[sj], f"outb{sj}", "pr2", "sq2")
        yops.append(S.add("pool", lambda e, tj=tj, sj=sj: e.dma_start(
            out=y_d[CH * tj:CH * (tj + 1), :], in_=outb[sj][:, :]),
            reads=[buf(f"outb{sj}")], writes=[buf("yout")], is_dma=True,
            excl=False, dkey=f"y{sj}"))
    S.add("sp", lambda e: e.nop(), deps=yops + tap_list)
    MARK["ALL"] = len(S.ops)

    import builtins
    builtins._KMARK = dict(MARK)
    if stop is not None:
        k = MARK[stop] if isinstance(stop, str) else stop
        S.ops = S.ops[:k]
        last = {}
        for op in S.ops:
            last[op.engine] = op
        S.add("sp", lambda e: e.nop(),
              deps=[o for o in last.values()])

    with nc.Block() as block:
        S.finalize(nc, block, ctx)
    ctx.close()
    return nc


def _emit_ln(S, buf, src, sq, stat, so, dst, dst_name, src_name, sq_name):
    st = buf(f"stat{so}")
    S.add("dve", lambda e: e.tensor_reduce(
        out=stat[:, so:so + 1], in_=src[:, :], axis=AX.X, op=ALU.add),
        reads=[buf(src_name)], writes=[st], strict=True)
    S.add("dve", lambda e: e.tensor_tensor(out=sq[:, :], in0=src[:, :],
                                           in1=src[:, :], op=ALU.mult),
          reads=[buf(src_name)], writes=[buf(sq_name)], strict=True)
    S.add("dve", lambda e: e.tensor_reduce(
        out=stat[:, so + 1:so + 2], in_=sq[:, :], axis=AX.X, op=ALU.add),
        reads=[buf(sq_name)], writes=[st], excl=False, strict=True)
    S.add("dve", lambda e: e.tensor_scalar(
        out=stat[:, so + 2:so + 3], in0=stat[:, so:so + 1], scalar1=1.0 / D,
        scalar2=None, op0=ALU.mult),
        reads=[st], writes=[st], excl=False, strict=True)
    S.add("dve", lambda e: e.tensor_scalar(
        out=stat[:, so + 3:so + 4], in0=stat[:, so + 2:so + 3], scalar1=-1.0,
        scalar2=None, op0=ALU.mult),
        reads=[st], writes=[st], excl=False, strict=True)
    S.add("dve", lambda e: e.tensor_scalar(
        out=stat[:, so + 4:so + 5], in0=stat[:, so + 1:so + 2], scalar1=1.0 / D,
        scalar2=EPS_LN, op0=ALU.mult, op1=ALU.add),
        reads=[st], writes=[st], excl=False, strict=True)
    S.add("dve", lambda e: e.scalar_tensor_tensor(
        out=stat[:, so + 5:so + 6], in0=stat[:, so + 2:so + 3],
        scalar=stat[:, so + 2:so + 3], in1=stat[:, so + 4:so + 5],
        op0=ALU.mult, op1=ALU.subtract),
        reads=[st], writes=[st], excl=False, strict=True)
    S.add("act", lambda e: e.activation(
        out=stat[:, so + 6:so + 7], in_=stat[:, so + 5:so + 6], func=AF.Ln,
        scale=-1.0),
        reads=[st], writes=[st], excl=False, strict=True)
    S.add("act", lambda e: e.activation(
        out=stat[:, so + 7:so + 8], in_=stat[:, so + 6:so + 7], func=AF.Exp,
        scale=-0.5),
        reads=[st], writes=[st], excl=False, strict=True)
    S.add("dve", lambda e: e.tensor_scalar(
        out=dst[:, :], in0=src[:, :], scalar1=stat[:, so + 3:so + 4],
        scalar2=stat[:, so + 7:so + 8], op0=ALU.add, op1=ALU.mult),
        reads=[buf(src_name), st], writes=[buf(dst_name)], strict=True)


_NC_CACHE = {}


def kernel(**inputs):
    x = np.ascontiguousarray(np.asarray(inputs["L0_attn_x"]).reshape(T, D))
    wqkv = np.ascontiguousarray(np.asarray(inputs["L0_attn_w_qkv"]))
    wout = np.ascontiguousarray(np.asarray(inputs["L0_attn_w_out"]))
    code = np.ascontiguousarray(np.asarray(inputs["L0_attn_fc_code"])
                                .reshape(NH * CS, HD))
    wg = np.ascontiguousarray(np.asarray(inputs["L0_ffn_w_gate"], np.int32))
    wu = np.ascontiguousarray(np.asarray(inputs["L0_ffn_w_up"], np.int32))
    wd = np.ascontiguousarray(np.asarray(inputs["L0_ffn_w_down"], np.int32))

    ident = np.eye(128, dtype=np.float32).astype(ml_dtypes.bfloat16)
    st = np.arange(128)
    u8 = (8.0 * (st[:, None] <= st[None, :])).astype(ml_dtypes.bfloat16)
    m1 = (st[:, None] <= st[None, :]).astype(np.float32)
    onesr = np.ones((1, 128), dtype=ml_dtypes.bfloat16)
    ones1 = np.ones((128, 1), dtype=ml_dtypes.bfloat16)
    rr = np.arange(64)
    padv = np.where((rr % 32) >= 16, 1e9, -3.0e38).astype(np.float32)[:, None]

    if "nc" not in _NC_CACHE:
        _NC_CACHE["nc"] = build_nc()
    nc = _NC_CACHE["nc"]

    in_maps = []
    for c in range(NCORES):
        rvec = np.zeros((33, 8), dtype=np.float32)
        rvec[:, :c] = 1.0
        in_maps.append({
            "x": np.asarray(x[TLOC * c:TLOC * (c + 1)], np.float32),
            "w_qkv": np.asarray(wqkv, np.float32),
            "w_out": np.asarray(wout, np.float32),
            "fc_code": np.asarray(code, np.float32),
            "w_gate": wg, "w_up": wu, "w_down": wd,
            "c_ident": ident, "c_u8": u8, "c_m1": m1, "c_onesr": onesr, "c_ones1": ones1,
            "c_padv": padv, "c_rvec": rvec,
        })
    res = run_bass_kernel_spmd(nc, in_maps, core_ids=list(range(NCORES)))
    out = np.concatenate([np.asarray(r["y"]) for r in res.results], axis=0)
    return out.reshape(1, T, D)


# version 15
# speedup vs baseline: 1.1949x; 1.0036x over previous
import numpy as np
import ml_dtypes
import concourse.bass as bass
import concourse.mybir as mybir
from concourse.dt import dt
from concourse.bass_utils import run_bass_kernel_spmd
from contextlib import ExitStack

F32 = dt.float32
BF16 = dt.bfloat16
I32 = dt.int32
AF = mybir.ActivationFunctionType
ALU = mybir.AluOpType
AX = mybir.AxisListType

NCORES = 8
T = 16384
TLOC = T // NCORES
D = 1024
NH = 16
HD = 64
CS = 16
HID = 3072
CH = 128
NCH = TLOC // CH
SCALE = 0.125
QS = 1.0 / 127.0
QS2 = QS * QS
EPS_Z = 1e-9
EPS_LN = 1e-5
NR = 512
ENGS = ("pe", "act", "dve", "pool", "sp")


class Op:
    __slots__ = ("engine", "emit", "deps", "inc", "val", "is_dma", "sem",
                 "needed", "strict")

    def __init__(self, engine, emit, deps, is_dma, sem, strict):
        self.engine = engine
        self.emit = emit
        self.deps = deps
        self.inc = 0
        self.val = None
        self.is_dma = is_dma
        self.sem = sem
        self.needed = False
        self.strict = strict


class Buf:
    __slots__ = ("writers", "readers", "name")

    def __init__(self, name):
        self.writers = []
        self.readers = []
        self.name = name


class Sched:
    def __init__(self):
        self.ops = []

    def add(self, engine, emit, reads=(), writes=(), deps=(), is_dma=False,
            excl=True, dkey=None, strict=False):
        dd = list(deps)
        for b in reads:
            dd += b.writers
        for b in writes:
            dd += b.readers + b.writers
        sem = None
        if is_dma:
            sem = "dma:" + (dkey or (writes[0].name if writes else "misc"))
        op = Op(engine, emit, dd, is_dma, sem, strict)
        self.ops.append(op)
        for b in reads:
            b.readers.append(op)
        for b in writes:
            if excl:
                b.writers = [op]
                b.readers = []
            else:
                b.writers.append(op)
        return op

    def finalize(self, nc, block, ctx):
        for op in self.ops:
            for d in op.deps:
                d.needed = True
        sem_names = list(ENGS)
        for op in self.ops:
            if op.is_dma and op.sem not in sem_names:
                sem_names.append(op.sem)
        sems = {n: ctx.enter_context(nc.semaphore(f"s{i}"))
                for i, n in enumerate(sem_names)}
        counts = {n: 0 for n in sem_names}
        prefix = {}
        for idx, op in enumerate(self.ops):
            if op.is_dma:
                op.inc = 16
            elif op.needed:
                op.inc = 1
                op.sem = op.engine
            if op.inc:
                counts[op.sem] += op.inc
                op.val = counts[op.sem]
                if op.is_dma:
                    prefix.setdefault(op.sem, []).append((idx, op.val))
        import bisect as _bis

        def dma_wait_val(sem, cidx):
            lst = prefix[sem]
            k = _bis.bisect_left(lst, (cidx, -1))
            return lst[k - 1][1] if k else 0

        idx_of = {id(op): i for i, op in enumerate(self.ops)}

        def body(eng_handle, ename):
            waited = {}
            for op in self.ops:
                if op.engine != ename:
                    continue
                cidx = idx_of[id(op)]
                want = {}
                for d in op.deps:
                    if d.inc == 0:
                        continue
                    if d.engine == ename and not d.is_dma and not d.strict:
                        continue
                    v = dma_wait_val(d.sem, cidx) if d.is_dma else d.val
                    if v > want.get(d.sem, 0):
                        want[d.sem] = v
                for se, v in want.items():
                    if waited.get(se, 0) >= v:
                        continue
                    eng_handle.wait_ge(sems[se], v)
                    waited[se] = v
                ins = op.emit(eng_handle)
                if op.inc:
                    ins.then_inc(sems[op.sem], op.inc)

        block.tensor(lambda e: body(e, "pe"))
        block.scalar(lambda e: body(e, "act"))
        block.vector(lambda e: body(e, "dve"))
        block.gpsimd(lambda e: body(e, "pool"))
        block.sync(lambda e: body(e, "sp"))


def build_nc(stop=None, taps=False):
    nc = bass.Bass("TRN2", target_bir_lowering=False,
                   dynamic_dma_scratch_size=16384,
                   detect_race_conditions=False)
    x_d = nc.dram_tensor("x", [TLOC, D], F32, kind="ExternalInput")
    wqkv_d = nc.dram_tensor("w_qkv", [3 * D, D], F32, kind="ExternalInput")
    wout_d = nc.dram_tensor("w_out", [D, D], F32, kind="ExternalInput")
    code_d = nc.dram_tensor("fc_code", [NH * CS, HD], F32, kind="ExternalInput")
    wg_d = nc.dram_tensor("w_gate", [HID, D], I32, kind="ExternalInput")
    wu_d = nc.dram_tensor("w_up", [HID, D], I32, kind="ExternalInput")
    wd_d = nc.dram_tensor("w_down", [D, HID], I32, kind="ExternalInput")
    ident_d = nc.dram_tensor("c_ident", [128, 128], BF16, kind="ExternalInput")
    u8_d = nc.dram_tensor("c_u8", [128, 128], BF16, kind="ExternalInput")
    onesr_d = nc.dram_tensor("c_onesr", [1, 128], BF16, kind="ExternalInput")
    m1_d = nc.dram_tensor("c_m1", [128, 128], F32, kind="ExternalInput")
    ones1_d = nc.dram_tensor("c_ones1", [128, 1], BF16, kind="ExternalInput")
    padv_d = nc.dram_tensor("c_padv", [64, 1], F32, kind="ExternalInput")
    rvec_d = nc.dram_tensor("c_rvec", [33, 8], F32, kind="ExternalInput")
    y_d = nc.dram_tensor("y", [TLOC, D], F32, kind="ExternalOutput")

    xb_s = nc.dram_tensor("s_xb", [TLOC, D], BF16, kind="Internal")
    wvb_s = nc.dram_tensor("s_wvb", [D, D], BF16, kind="Internal")
    wob_s = nc.dram_tensor("s_wob", [D, D], BF16, kind="Internal")
    wgb_s = nc.dram_tensor("s_wgb", [HID, D], BF16, kind="Internal")
    wub_s = nc.dram_tensor("s_wub", [HID, D], BF16, kind="Internal")
    wdb_s = nc.dram_tensor("s_wdb", [D, HID], BF16, kind="Internal")
    kef_s = nc.dram_tensor("s_kef", [NR, TLOC], BF16, kind="Internal")
    lnb_s = nc.dram_tensor("s_lnb", [TLOC, D], BF16, kind="Internal")
    lntm_s = nc.dram_tensor("s_lntm", [TLOC, D], F32, kind="Internal")
    dup_s = nc.dram_tensor("s_dup", [33, 1024], F32, kind="Internal")
    ccm_in = nc.dram_tensor("cc_m_in", [64, 8], F32, kind="Internal")
    ccm_out = nc.dram_tensor("cc_m_out", [64, 8], F32, kind="Internal")
    ccs_in = nc.dram_tensor("cc_s_in", [33, 1024], F32, kind="Internal")
    ccs_out = nc.dram_tensor("cc_s_out", [33 * 8, 1024], F32, kind="Internal")

    ctx = ExitStack()

    def sb(name, shape, dtp):
        return ctx.enter_context(nc.sbuf_tensor(name, shape, dtp))

    def psum(name, shape, dtp):
        return ctx.enter_context(nc.psum_tensor(name, shape, dtp))

    ident = sb("ident", [128, 128], BF16)
    u8t = sb("u8t", [128, 128], BF16)
    m1t = sb("m1t", [128, 128], F32)
    onesr = sb("onesr", [1, 128], BF16)
    ones1 = sb("ones1", [128, 1], BF16)
    padv = sb("padv", [64, 1], F32)
    rvec = sb("rvec", [33, 8], F32)
    negm = sb("negm", [64, 8], F32)
    mred = sb("mred", [64, 4], F32)
    mloc = sb("mloc", [64, 8], F32)
    carrf = [sb(f"carrf{i}", [1, NR], F32) for i in range(2)]
    carrb = [sb(f"carrb{i}", [1, NR], BF16) for i in range(2)]
    row127 = sb("row127", [1, NR], F32)
    hsb = [sb(f"hsb{i}", [128, 16], F32) for i in range(2)]
    stat = sb("stat", [128, 16], F32)

    ARK = sb("ARK", [128, 16384], BF16)
    ARB = sb("ARB", [128, 16384], BF16)
    ARA = sb("ARA", [128, 8192], BF16)
    ARG = sb("ARG", [128, 8192], BF16)
    ARG2 = sb("ARG2", [128, 8192], BF16)
    ARD = sb("ARD", [128, 2048], BF16)
    ARE = sb("ARE", [128, 16384], BF16)
    ARF = sb("ARF", [128, 8192], BF16)
    ARH = sb("ARH", [128, 8192], BF16)
    ARI = sb("ARI", [128, 8192], BF16)

    qk_bf = [ARA[:, 1024 * i:1024 * (i + 1)] for i in range(8)]
    wst = [ARG[:, 0:2048].bitcast(F32), ARG[:, 2048:4096].bitcast(F32)]
    wsti = [ARG[:, 0:2048].bitcast(I32), ARG[:, 2048:4096].bitcast(I32)]
    wbf = [ARG[:, 4096:5120], ARG[:, 5120:6144]]
    xtb = [ARG[:, 6144:7168], ARG[:, 7168:8192]]
    codeb = ARG2[0:16, 0:1024]
    codeT = ARG2[:, 1024:1280]
    codeblk = [ARG2[:, 1536 + 256 * i:1536 + 256 * (i + 1)] for i in range(8)]
    xfm = [ARB[:, 2048 * i:2048 * (i + 1)] for i in range(8)]
    wtq = [ARF[:, 512 * i:512 * (i + 1)] for i in range(8)]
    wtk = [ARF[:, 4096 + 512 * i:4096 + 512 * (i + 1)] for i in range(8)]
    wvT = [ARE[:, 1024 * i:1024 * (i + 1)] for i in range(8)]
    woT = [ARE[:, 8192 + 1024 * i:8192 + 1024 * (i + 1)] for i in range(8)]
    xsb = [ARA[:, 1024 * s:1024 * (s + 1)] for s in range(2)]
    kefm = [ARK[0:64, 2048 * p:2048 * (p + 1)] for p in range(8)]
    ketm = [ARD[:, 512 * i:512 * (i + 1)] for i in range(2)]
    mtbf = [ARG[:, 512 * i:512 * (i + 1)] for i in range(8)]
    srs = ARI[:, 0:2048].bitcast(F32)[0:33, :]
    accb = [ARI[:, 2048 + 2048 * i:2048 + 2048 * (i + 1)].bitcast(F32)[0:33, :]
            for i in range(2)]
    gth = ARI[:, 6144:8192].bitcast(F32)[0:33, :]
    dbf = ARG[:, 0:1024].bitcast(F32)
    lnd = ARG[:, 1024:2048].bitcast(F32)
    vtm = [ARG[:, 2048 + 1024 * i:2048 + 1024 * (i + 1)] for i in range(2)]
    pre1 = ARG[:, 4096 + 2048:4096 + 4096].bitcast(F32)
    sqb = ARG2[:, 0:2048].bitcast(F32)
    ln1 = ARG2[:, 2048:4096].bitcast(F32)
    ln1b = [ARG2[:, 4096 + 1024 * i:4096 + 1024 * (i + 1)] for i in range(2)]
    etm = [ARG2[:, 6144 + 512 * i:6144 + 512 * (i + 1)] for i in range(2)]
    qtm = [ARG2[:, 7168 + 512 * i:7168 + 512 * (i + 1)] for i in range(2)]
    xres = [ARH[:, 2048 * i:2048 * (i + 1)].bitcast(F32) for i in range(2)]
    qfm = [[ARH[0:64, 4096 + 2048 * s + 128 * h:4096 + 2048 * s + 128 * (h + 1)]
            for h in range(16)] for s in range(2)]
    asb = [ARI[:, 6144 + 128 * i:6144 + 128 * (i + 1)] for i in range(4)]
    xofm = [ARI[:, 6656 + 128 * i:6656 + 128 * (i + 1)] for i in range(8)]
    S32 = [ARI[0:64, 2048 * i:2048 * (i + 1)].bitcast(F32) for i in range(2)]
    Sbf = [ARI[0:64, 4096 + 1024 * i:4096 + 1024 * (i + 1)] for i in range(2)]
    hpr = ([ARK[:, 2048 * i:2048 * (i + 1)] for i in range(8)] +
           [ARA[:, 2048 * i:2048 * (i + 1)] for i in range(4)] +
           [ARG[:, 2048 * i:2048 * (i + 1)] for i in range(4)] +
           [ARG2[:, 2048 * i:2048 * (i + 1)] for i in range(4)] +
           [ARI[:, 2048 * i:2048 * (i + 1)] for i in range(4)])
    lnfm = [ARB[:, 2048 * i:2048 * (i + 1)] for i in range(8)]
    wgT = [[ARF[:, 512 * i:512 * (i + 1)] for i in range(8)],
           [ARH[:, 512 * i:512 * (i + 1)] for i in range(8)]]
    wuT = [[ARF[:, 4096 + 512 * i:4096 + 512 * (i + 1)] for i in range(8)],
           [ARH[:, 4096 + 512 * i:4096 + 512 * (i + 1)] for i in range(8)]]
    wdT = ([ARD[:, 1024 * i:1024 * (i + 1)] for i in range(2)] +
           [ARE[:, 1024 * i:1024 * (i + 1)] for i in range(16)] +
           [ARF[:, 1024 * i:1024 * (i + 1)] for i in range(6)])
    ltm = [ARB[:, 2048 * i:2048 * (i + 1)].bitcast(F32) for i in range(2)]
    outb = [ARB[:, 4096 + 2048 * i:4096 + 2048 * (i + 1)].bitcast(F32)
            for i in range(2)]
    sq2 = ARB[:, 8192:10240].bitcast(F32)
    pr2 = ARB[:, 10240:12288].bitcast(F32)
    silb = ARD[:, 1024:2048].bitcast(F32)

    PB = [psum(f"PB{i}", [128, 512], F32) for i in range(4)]
    PXO = [psum(f"PXO{i}", [128, 512], F32) for i in range(2)]
    PDS = psum("PDS", [64, 512], F32)
    PT = psum("PT", [128, 1024], BF16)

    S = Sched()
    MARK = {}
    _B = {}

    def buf(name):
        if name not in _B:
            _B[name] = Buf(name)
        return _B[name]

    tap_list = []

    def tap(name, ap, bnames, lvl=1):
        if not taps or lvl > int(taps):
            return
        td = nc.dram_tensor("tap_" + name, list(ap.shape), ap.dtype,
                            kind="ExternalOutput")
        op = S.add("pool", lambda e, td=td, ap=ap: e.dma_start(
            out=td[tuple(slice(None) for _ in ap.shape)], in_=ap),
            reads=[buf(b) for b in bnames], is_dma=True, dkey="taps")
        tap_list.append(op)

    for t, d_, nm in ((ident, ident_d, "ident"), (u8t, u8_d, "u8t"),
                      (m1t, m1_d, "m1t"),
                      (onesr, onesr_d, "onesr"), (ones1, ones1_d, "ones1"),
                      (padv, padv_d, "padv"), (rvec, rvec_d, "rvec")):
        S.add("sp", lambda e, t=t, d_=d_: e.dma_start(out=t[:, :], in_=d_[:, :]),
              writes=[buf(nm)], is_dma=True, dkey="consts")
    b_id = buf("ident")
    const_ops = list(S.ops)

    for i in range(8):
        s = i % 2
        S.add("sp", lambda e, i=i, s=s: e.dma_start(
            out=wst[s][:, :], in_=wqkv_d[128 * i:128 * (i + 1), :]),
            writes=[buf(f"wst{s}")], is_dma=True)
        S.add("act", lambda e, i=i, s=s: e.copy(out=qk_bf[i][:, :], in_=wst[s][:, :]),
              reads=[buf(f"wst{s}")], writes=[buf(f"qkbf{i}")])
    S.add("sp", lambda e: e.dma_start(
        out=wst[0][0:16, 0:NH * HD].rearrange("c (h d) -> c h d", d=HD),
        in_=code_d[:, :].rearrange("(h c) d -> c h d", c=CS)),
        writes=[buf("wst0")], is_dma=True)
    S.add("act", lambda e: e.mul(out=codeb[:, :], in_=wst[0][0:16, 0:NH * HD],
                                 mul=SCALE),
          reads=[buf("wst0")], writes=[buf("codeb")], deps=const_ops)
    b_ptc = buf("bk_pt")
    for h in range(NH):
        S.add("pe", lambda e, h=h: e.transpose(
            PT[0:64, 16 * h:16 * h + 16], codeb[:, 64 * h:64 * h + 64],
            ident[0:16, 0:16]),
            reads=[buf("codeb"), b_id], writes=[b_ptc], excl=(h == 0))
        S.add("pe", lambda e, h=h: e.transpose(
            PT[64:128, 16 * h:16 * h + 16], codeb[:, 64 * h:64 * h + 64],
            ident[0:16, 0:16]),
            reads=[buf("codeb"), b_id], writes=[b_ptc], excl=False)
    S.add("dve", lambda e: e.tensor_copy(out=codeT[:, 0:NH * CS],
                                         in_=PT[:, 0:NH * CS]),
          reads=[b_ptc], writes=[buf("codeT")])
    for i in range(8):
        S.add("dve", lambda e, i=i: e.memset(wtq[i][:, :], 0.0),
              writes=[buf(f"wtq{i}")])
        S.add("dve", lambda e, i=i: e.memset(wtk[i][:, :], 0.0),
              writes=[buf(f"wtk{i}")])
    for i in range(8):
        S.add("dve", lambda e, i=i: e.memset(codeblk[i][:, :], 0.0),
              writes=[buf(f"cbk{i}")])
        S.add("dve", lambda e, i=i: e.tensor_copy(
            out=codeblk[i][0:64, 32 * i:32 * i + 16],
            in_=codeT[0:64, 32 * i:32 * i + 16]),
            reads=[buf("codeT")], writes=[buf(f"cbk{i}")], excl=False)
        S.add("dve", lambda e, i=i: e.tensor_copy(
            out=codeblk[i][64:128, 32 * i + 16:32 * i + 32],
            in_=codeT[64:128, 32 * i + 16:32 * i + 32]),
            reads=[buf("codeT")], writes=[buf(f"cbk{i}")], excl=False)
    for which, wt in (("q", wtq), ("k", wtk)):
        if which == "k":
            for i in range(8):
                s = i % 2
                S.add("sp", lambda e, i=i, s=s: e.dma_start(
                    out=wst[s][:, :],
                    in_=wqkv_d[1024 + 128 * i:1024 + 128 * (i + 1), :]),
                    writes=[buf(f"wst{s}")], is_dma=True)
                S.add("act", lambda e, i=i, s=s: e.copy(out=qk_bf[i][:, :],
                                                        in_=wst[s][:, :]),
                      reads=[buf(f"wst{s}")], writes=[buf(f"qkbf{i}")])
        for dc in range(8):
            pb = buf("bk_pb0")
            for i in range(8):
                S.add("pe", lambda e, i=i, dc=dc: e.matmul(
                    PB[0][:, 0:256],
                    qk_bf[i][:, 128 * dc:128 * dc + 128],
                    codeblk[i][:, :], start=(i == 0), stop=(i == 7)),
                    reads=[buf(f"qkbf{i}"), buf(f"cbk{i}")], writes=[pb],
                    excl=(i == 0))
            S.add("dve", lambda e, wt=wt, dc=dc: e.tensor_copy(
                out=wt[dc].rearrange("p (h c) -> p h c", c=32)[:, :, 0:16],
                in_=PB[0][:, 0:256].rearrange("p (h c) -> p h c", c=16)),
                reads=[pb], writes=[buf(f"wt{which}{dc}")], excl=False)
    wt_done = [o for i in range(8)
               for o in (_B[f"wtq{i}"].writers + _B[f"wtk{i}"].writers)]

    pst = [ARI[:, 0:2048].bitcast(I32), ARI[:, 2048:4096].bitcast(I32)]
    pbf = [ARI[:, 4096:5120], ARI[:, 5120:6144]]
    prep_jobs = []
    for nm2, src2, scr2 in (("wg", wg_d, wgb_s), ("wu", wu_d, wub_s)):
        for i2 in range(24):
            prep_jobs.append((nm2, scr2,
                              src2[128 * i2:128 * (i2 + 1), :],
                              (128 * i2, 0)))
    for i2 in range(8):
        for c3 in range(3):
            prep_jobs.append(("wd", wdb_s,
                              wd_d[128 * i2:128 * (i2 + 1),
                                   1024 * c3:1024 * (c3 + 1)],
                              (128 * i2, 1024 * c3)))
    prep_loads, prep_casts, prep_stores = [], [], []

    def prep_load(k):
        src_ap = prep_jobs[k][2]
        s2 = k % 2
        prep_loads.append(S.add("pool", lambda e, src_ap=src_ap, s2=s2:
                                e.dma_start(out=pst[s2][:, :], in_=src_ap),
                                writes=[buf(f"pst{s2}")], is_dma=True))
    pk = [0]

    def prep_step(n):
        for _ in range(n):
            k = pk[0]
            if k >= len(prep_jobs):
                return
            pk[0] += 1
            nm2, scr2, _, (r0, c0) = prep_jobs[k]
            s2 = k % 2
            prep_casts.append(S.add("pool", lambda e, s2=s2: e.tensor_copy(
                out=pbf[s2][:, :], in_=pst[s2][:, :]),
                reads=[buf(f"pst{s2}")], writes=[buf(f"pbf{s2}")]))
            prep_stores.append(S.add(
                "pool", lambda e, scr2=scr2, r0=r0, c0=c0, s2=s2: e.dma_start(
                    out=scr2[r0:r0 + 128, c0:c0 + 1024], in_=pbf[s2][:, :]),
                reads=[buf(f"pbf{s2}")], writes=[buf(f"scr_{nm2}")],
                is_dma=True, excl=False, dkey=f"pbf{s2}"))
            if k + 2 < len(prep_jobs):
                prep_load(k + 2)

    def xres_load(j):
        s = j % 2
        S.add("pool", lambda e, j=j, s=s: e.dma_start(
            out=xres[s][:, :], in_=x_d[CH * j:CH * (j + 1), :]),
            writes=[buf(f"xres{s}")], is_dma=True)
    prep_load(0)
    prep_load(1)
    xres_load(0)
    xres_load(1)
    for j in range(NCH):
        s = j % 2
        S.add("pool", lambda e, s=s: e.tensor_copy(out=xtb[s][:, :],
                                                   in_=xres[s][:, :]),
              reads=[buf(f"xres{s}")], writes=[buf(f"xtb{s}")])
        S.add("pool", lambda e, j=j, s=s: e.dma_start(
            out=xb_s[CH * j:CH * (j + 1), :], in_=xtb[s][:, :]),
            reads=[buf(f"xtb{s}")], writes=[buf("scr_x")], is_dma=True,
            excl=False, dkey=f"xtb{s}")
        if j + 2 < NCH:
            xres_load(j + 2)
        prep_step(2)
    xfm_ops = []
    for dc in range(8):
        xfm_ops.append(S.add("sp", lambda e, dc=dc: e.dma_start(
            out=xfm[dc][:, :], in_=xb_s[:, 128 * dc:128 * dc + 128],
            transpose=True),
            reads=[buf("scr_x")], writes=[buf(f"xfm{dc}")], is_dma=True,
            deps=wt_done, dkey=f"xfmT{dc % 2}"))
    prep_step(len(prep_jobs))
    prep_guard = [prep_casts[-1], prep_loads[-1], prep_loads[-2],
                  prep_stores[-1], prep_stores[-2]]

    wvwo_casts = []
    for name, scr in (("wv", wvb_s), ("wo", wob_s)):
        for i in range(8):
            s = i % 2
            if name == "wv":
                S.add("sp", lambda e, i=i, s=s: e.dma_start(
                    out=wst[s][:, :],
                    in_=wqkv_d[2048 + 128 * i:2048 + 128 * (i + 1), :]),
                    writes=[buf(f"wst{s}")], is_dma=True)
            else:
                S.add("sp", lambda e, i=i, s=s: e.dma_start(
                    out=wst[s][:, :], in_=wout_d[128 * i:128 * (i + 1), :]),
                    writes=[buf(f"wst{s}")], is_dma=True)
            wvwo_casts.append(S.add(
                "act", lambda e, s=s: e.copy(out=wbf[s][:, :], in_=wst[s][:, :]),
                reads=[buf(f"wst{s}")], writes=[buf(f"wbf{s}")]))
            S.add("sp", lambda e, scr=scr, i=i, s=s: e.dma_start(
                out=scr[128 * i:128 * (i + 1), :], in_=wbf[s][:, :]),
                reads=[buf(f"wbf{s}")], writes=[buf(f"scr_{name}")],
                is_dma=True, excl=False, dkey=f"wbf{s}")
    wvT_ops, woT_ops = [], []
    for name, scr, dstT, lst in (("wv", wvb_s, wvT, wvT_ops),
                                 ("wo", wob_s, woT, woT_ops)):
        for dc in range(8):
            lst.append(S.add("sp", lambda e, scr=scr, dstT=dstT, dc=dc: e.dma_start(
                out=dstT[dc][:, :], in_=scr[:, 128 * dc:128 * dc + 128],
                transpose=True),
                reads=[buf(f"scr_{name}")], writes=[buf(f"{name}T{dc}")],
                is_dma=True, dkey=f"{name}T"))

    MARK["P0"] = len(S.ops)
    tap("wtq0", wtq[0][:, :], ["wtq0"], lvl=1)
    tap("wtk0", wtk[0][:, :], ["wtk0"], lvl=1)
    tap("xfm0", xfm[0][:, 0:512], ["xfm0"], lvl=1)
    tap("wvT0", wvT[0][:, :], ["wvT0"], lvl=1)
    tap("woT0", woT[0][:, :], ["woT0"], lvl=1)

    for p in range(8):
        for ts in range(4):
            gi = (p * 4 + ts) % 2
            pb = buf(f"bk_pb{gi}")
            for dc in range(8):
                S.add("pe", lambda e, p=p, ts=ts, dc=dc, gi=gi: e.matmul(
                    PB[gi][0:64, 0:512], wtk[dc][:, 64 * p:64 * p + 64],
                    xfm[dc][:, 512 * ts:512 * ts + 512],
                    start=(dc == 0), stop=(dc == 7)),
                    reads=[buf(f"wtk{dc}"), buf(f"xfm{dc}")], writes=[pb],
                    excl=(dc == 0),
                    deps=(wt_done + xfm_ops) if (p == 0 and ts == 0) else ())
            S.add("dve", lambda e, p=p, ts=ts, gi=gi: e.tensor_reduce(
                out=mred[:, ts:ts + 1], in_=PB[gi][0:64, 0:512],
                axis=AX.X, op=ALU.max),
                reads=[pb], writes=[buf("mred")], excl=(ts == 0), strict=True)
        S.add("dve", lambda e, p=p: e.tensor_reduce(
            out=mloc[:, p:p + 1], in_=mred[:, :], axis=AX.X, op=ALU.max),
            reads=[buf("mred")], writes=[buf("mloc")], excl=False, strict=True)

    tap("mlocpre", mloc[:, :], ["mloc"], lvl=1)
    S.add("pool", lambda e: e.dma_start(out=ccm_in[:, :], in_=mloc[:, :]),
          reads=[buf("mloc")], writes=[buf("ccm")], is_dma=True)
    if taps and int(taps) >= 1:
        tccin = nc.dram_tensor("tap_ccmin", [64, 8], F32, kind="ExternalOutput")
        tap_list.append(S.add("act", lambda e: e.dma_start(
            out=tccin[:, :], in_=ccm_in[:, :]),
            reads=[buf("ccm")], is_dma=True, dkey="taps"))
    S.add("pool", lambda e: e.collective_compute(
        "AllReduce", ALU.max, replica_groups=[list(range(NCORES))],
        ins=[ccm_in.ap().opt()], outs=[ccm_out.ap().opt()]),
        reads=[buf("ccm")], writes=[buf("ccmo")], strict=True)
    if taps and int(taps) >= 1:
        tccout = nc.dram_tensor("tap_ccmout", [64, 8], F32, kind="ExternalOutput")
        tap_list.append(S.add("act", lambda e: e.dma_start(
            out=tccout[:, :], in_=ccm_out[:, :]),
            reads=[buf("ccmo")], is_dma=True, dkey="taps"))
    S.add("pool", lambda e: e.dma_start(out=mloc[:, :], in_=ccm_out[:, :]),
          reads=[buf("ccmo")], writes=[buf("mloc")], is_dma=True, dkey="mback")
    tap("padv2", padv[:, :], ["padv"], lvl=1)
    S.add("dve", lambda e: e.tensor_scalar(
        out=negm[:, :], in0=mloc[:, :], scalar1=padv[:, 0:1], scalar2=-1.0,
        op0=ALU.max, op1=ALU.mult),
        reads=[buf("mloc"), buf("padv")], writes=[buf("negm")])

    MARK["P1"] = len(S.ops)
    tap("mloc", mloc[:, :], ["mloc"], lvl=1)
    tap("negm", negm[:, :], ["negm"], lvl=1)
    kexp_ops = []
    for p in range(8):
        for ts in range(4):
            gi = (p * 4 + ts) % 2
            pb = buf(f"bk_pb{gi}")
            for dc in range(8):
                S.add("pe", lambda e, p=p, ts=ts, dc=dc, gi=gi: e.matmul(
                    PB[gi][0:64, 0:512], wtk[dc][:, 64 * p:64 * p + 64],
                    xfm[dc][:, 512 * ts:512 * ts + 512],
                    start=(dc == 0), stop=(dc == 7)),
                    reads=[buf(f"wtk{dc}"), buf(f"xfm{dc}")], writes=[pb],
                    excl=(dc == 0))
            o = S.add("act", lambda e, p=p, ts=ts, gi=gi: e.activation(
                out=kefm[p][:, 512 * ts:512 * ts + 512], in_=PB[gi][0:64, 0:512],
                func=AF.Exp, bias=negm[:, p:p + 1], scale=1.0),
                reads=[pb, buf("negm")], writes=[buf(f"kefm{p}")], excl=False)
            kexp_ops.append(o)
        S.add("act", lambda e, p=p: e.dma_start(
            out=kef_s[64 * p:64 * (p + 1), :], in_=kefm[p][:, :]),
            reads=[buf(f"kefm{p}")], writes=[buf("scr_kef")], is_dma=True,
            excl=False, dkey="scr_kef")

    ztot_pb = buf("bk_pds")
    mt_pb = [buf("bk_pb0"), buf("bk_pb1"), buf("bk_pb2"), buf("bk_pb3"),
             buf("bk_pxo0"), buf("bk_pxo1"), buf("bk_pb0"), buf("bk_pb1")]
    for j in range(NCH):
        ks = j % 2
        S.add("sp", lambda e, j=j, ks=ks: e.dma_start(
            out=ketm[ks][:, :], in_=kef_s[:, CH * j:CH * j + 128], transpose=True),
            reads=[buf("scr_kef")], writes=[buf(f"ketm{ks}")], is_dma=True)
        S.add("pool", lambda e, j=j, ks=ks: e.dma_start(
            out=xres[ks][:, :], in_=x_d[CH * j:CH * (j + 1), :]),
            writes=[buf(f"xres{ks}")], is_dma=True)
        S.add("act", lambda e, j=j, ks=ks: e.copy(out=xsb[ks][:, :],
                                                  in_=xres[ks][:, :]),
              reads=[buf(f"xres{ks}")], writes=[buf(f"xsb{ks}")],
              deps=(kexp_ops + wt_done) if j < 2 else ())
        S.add("pe", lambda e, j=j, ks=ks: e.matmul(
            PDS[0:1, 0:NR], ones1[:, :], ketm[ks][:, :],
            start=(j == 0), stop=(j == NCH - 1)),
            reads=[buf(f"ketm{ks}"), buf("ones1")], writes=[ztot_pb],
            excl=(j == 0))
        for dc in range(6):
            pp = PB[dc] if dc < 4 else PXO[dc - 4]
            S.add("pe", lambda e, j=j, ks=ks, dc=dc, pp=pp: e.matmul(
                pp[:, 0:NR], xsb[ks][:, 128 * dc:128 * dc + 128],
                ketm[ks][:, :], start=(j == 0), stop=(j == NCH - 1)),
                reads=[buf(f"xsb{ks}"), buf(f"ketm{ks}")], writes=[mt_pb[dc]],
                excl=(j == 0))
    S.add("dve", lambda e: e.tensor_copy(out=row127[0:1, 0:NR],
                                         in_=PDS[0:1, 0:NR]),
          reads=[ztot_pb], writes=[buf("zr")], excl=False)
    for dc in range(6):
        pp = PB[dc] if dc < 4 else PXO[dc - 4]
        S.add("dve", lambda e, dc=dc, pp=pp: e.tensor_copy(
            out=mtbf[dc][:, :], in_=pp[:, 0:NR]),
            reads=[mt_pb[dc]], writes=[buf(f"mtbf{dc}")],
            deps=(wvwo_casts if dc == 0 else ()))
    for j in range(NCH):
        ks = j % 2
        S.add("sp", lambda e, j=j, ks=ks: e.dma_start(
            out=ketm[ks][:, :], in_=kef_s[:, CH * j:CH * j + 128], transpose=True),
            reads=[buf("scr_kef")], writes=[buf(f"ketm{ks}")], is_dma=True)
        S.add("pool", lambda e, j=j, ks=ks: e.dma_start(
            out=xres[ks][:, :], in_=x_d[CH * j:CH * (j + 1), :]),
            writes=[buf(f"xres{ks}")], is_dma=True)
        S.add("act", lambda e, j=j, ks=ks: e.copy(out=xsb[ks][:, :],
                                                  in_=xres[ks][:, :]),
              reads=[buf(f"xres{ks}")], writes=[buf(f"xsb{ks}")])
        for dc in range(6, 8):
            S.add("pe", lambda e, j=j, ks=ks, dc=dc: e.matmul(
                PB[dc - 6][:, 0:NR], xsb[ks][:, 128 * dc:128 * dc + 128],
                ketm[ks][:, :], start=(j == 0), stop=(j == NCH - 1)),
                reads=[buf(f"xsb{ks}"), buf(f"ketm{ks}")], writes=[mt_pb[dc]],
                excl=(j == 0))
    for dc in range(6, 8):
        S.add("dve", lambda e, dc=dc: e.tensor_copy(
            out=mtbf[dc][:, :], in_=PB[dc - 6][:, 0:NR]),
            reads=[mt_pb[dc]], writes=[buf(f"mtbf{dc}")])
    for hb in range(2):
        pbsr = buf("bk_pds")
        for h in range(8 * hb, 8 * hb + 8):
            for dc in range(8):
                S.add("pe", lambda e, h=h, dc=dc, hb=hb: e.matmul(
                    PDS[0:32, 64 * h - 512 * hb:64 * h - 512 * hb + 64],
                    mtbf[dc][:, 32 * h:32 * h + 32],
                    wvT[dc][:, 64 * h:64 * h + 64], start=(dc == 0),
                    stop=(dc == 7)),
                    reads=[buf(f"mtbf{dc}"), buf(f"wvT{dc}")], writes=[pbsr],
                    excl=(h == 8 * hb and dc == 0),
                    deps=(wvT_ops if (h == 0 and dc == 0) else ()))
        S.add("dve", lambda e, hb=hb: e.tensor_copy(
            out=srs[0:32, 512 * hb:512 * hb + 512], in_=PDS[0:32, :]),
            reads=[pbsr], writes=[buf("srs")], excl=False, deps=prep_guard)
    S.add("pool", lambda e: e.dma_start(out=ccs_in[0:32, :], in_=srs[0:32, :]),
          reads=[buf("srs")], writes=[buf("ccs")], is_dma=True, excl=False)
    S.add("pool", lambda e: e.dma_start(out=ccs_in[32:33, 0:NR],
                                        in_=row127[:, :]),
          reads=[buf("zr")], writes=[buf("ccs")], is_dma=True, excl=False,
          dkey="ccs")
    S.add("dve", lambda e: e.memset(carrf[1][:, :], 0.0),
          writes=[buf("carrf1")])
    S.add("pool", lambda e: e.dma_start(out=ccs_in[32:33, NR:1024],
                                        in_=carrf[1][:, :]),
          reads=[buf("carrf1")], writes=[buf("ccs")], is_dma=True, excl=False,
          dkey="ccs")
    S.add("pool", lambda e: e.collective_compute(
        "AllGather", ALU.bypass, replica_groups=[list(range(NCORES))],
        ins=[ccs_in.ap().opt()], outs=[ccs_out.ap().opt()]),
        reads=[buf("ccs")], writes=[buf("ccso")], strict=True)

    MARK["P2"] = len(S.ops)
    tap("kefm0", kefm[0][:, 0:512], ["kefm0"], lvl=4)
    tap("srs", srs[:, :], ["srs"], lvl=4)
    tap("mtbf0", mtbf[0][:, :], ["mtbf0"], lvl=4)
    S.add("dve", lambda e: e.memset(accb[0][:, :], 0.0), writes=[buf("accb0")],
          deps=prep_guard)
    for r in range(NCORES):
        S.add("pool", lambda e, r=r: e.dma_start(
            out=gth[:, :], in_=ccs_out[33 * r:33 * (r + 1), :]),
            reads=[buf("ccso")], writes=[buf("gth")], is_dma=True)
        a, b2 = r % 2, (r + 1) % 2
        S.add("dve", lambda e, r=r, a=a, b2=b2: e.scalar_tensor_tensor(
            out=accb[b2][:, :], in0=gth[:, :], scalar=rvec[:, r:r + 1],
            in1=accb[a][:, :], op0=ALU.mult, op1=ALU.add),
            reads=[buf("gth"), buf(f"accb{a}"), buf("rvec")],
            writes=[buf(f"accb{b2}")],
            deps=(prep_guard if r == 0 else ()))
    accf = buf(f"accb{NCORES % 2}")
    accT = accb[NCORES % 2]
    S.add("pool", lambda e: e.dma_start(out=dup_s[:, :], in_=accT[:, :]),
          reads=[accf], writes=[buf("dup")], is_dma=True)
    S.add("pool", lambda e: e.dma_start(out=S32[0][32:64, :], in_=dup_s[0:32, :]),
          reads=[buf("dup")], writes=[buf("S32_0")], is_dma=True, excl=False,
          dkey="sdup", deps=prep_guard)
    S.add("pool", lambda e: e.dma_start(out=row127[:, :],
                                        in_=dup_s[32:33, 0:NR]),
          reads=[buf("dup"), buf("zr")], writes=[buf("row127"), buf("zr")],
          is_dma=True)
    S.add("dve", lambda e: e.tensor_scalar(
        out=carrf[0][:, :], in0=row127[:, :], scalar1=EPS_Z, scalar2=8.0,
        op0=ALU.add, op1=ALU.mult),
        reads=[buf("row127")], writes=[buf("carrf0")])
    S.add("act", lambda e: e.copy(out=carrb[0][:, :], in_=carrf[0][:, :]),
          reads=[buf("carrf0")], writes=[buf("carrb0")])
    S.add("dve", lambda e: e.tensor_copy(out=S32[0][0:32, :], in_=accT[0:32, :]),
          reads=[accf], writes=[buf("S32_0")], excl=False)
    S.add("act", lambda e: e.copy(out=Sbf[0][:, :], in_=S32[0][:, :]),
          reads=[buf("S32_0")], writes=[buf("Sbf_0")])

    MARK["P3"] = len(S.ops)
    tap("acc", accT[:, :], [f"accb{NCORES % 2}"], lvl=4)
    tap("carrf0", carrf[0][:, :], ["carrf0"], lvl=4)
    tap("S32i", S32[0][:, :], ["S32_0"], lvl=4)
    mtbf_rd = [o for dc2 in range(8) for o in _B[f"mtbf{dc2}"].readers]
    for s in range(2):
        for h in range(NH):
            if h % 2 == 0:
                S.add("dve", lambda e, s=s, h=h: e.memset(
                    qfm[s][h][32:64, :], 0.0), writes=[buf(f"qfm{s}_{h}")])
            else:
                S.add("dve", lambda e, s=s, h=h: e.memset(
                    qfm[s][h][0:32, :], 0.0), writes=[buf(f"qfm{s}_{h}")])
    prev_a_evacs = []

    def emit_ketm_load(jn):
        kn = jn % 2
        S.add("sp", lambda e, jn=jn, kn=kn: e.dma_start(
            out=ketm[kn][:, :], in_=kef_s[:, CH * jn:CH * jn + 128],
            transpose=True),
            reads=[buf("scr_kef")], writes=[buf(f"ketm{kn}")], is_dma=True)

    def emit_prefix(j):
        sj = j % 2
        ks = j % 2
        if j + 1 < NCH:
            emit_ketm_load(j + 1)
        pbq = buf("bk_pb0")
        for dc in range(8):
            S.add("pe", lambda e, j=j, dc=dc: e.matmul(
                PB[0][:, 0:NR], xfm[dc][:, CH * j:CH * j + 128], wtq[dc][:, :],
                start=(dc == 0), stop=(dc == 7)),
                reads=[buf(f"xfm{dc}"), buf(f"wtq{dc}")], writes=[pbq],
                excl=(dc == 0))
        S.add("act", lambda e, sj=sj: e.activation(
            out=etm[sj][:, :], in_=PB[0][:, 0:NR], func=AF.Exp),
            reads=[pbq], writes=[buf(f"etm{sj}")])
        for half in range(2):
            pbv = buf(f"bk_pb{2 + half}")
            for dc in range(8):
                S.add("pe", lambda e, j=j, dc=dc, half=half: e.matmul(
                    PB[2 + half][:, 0:512], xfm[dc][:, CH * j:CH * j + 128],
                    wvT[dc][:, 512 * half:512 * half + 512],
                    start=(dc == 0), stop=(dc == 7)),
                    reads=[buf(f"xfm{dc}"), buf(f"wvT{dc}")], writes=[pbv],
                    excl=(dc == 0),
                    deps=(woT_ops if (j == 0 and dc == 0 and half == 0) else ()))
            S.add("dve", lambda e, half=half, sj=sj: e.tensor_copy(
                out=vtm[sj][:, 512 * half:512 * half + 512],
                in_=PB[2 + half][:, 0:512]),
                reads=[pbv], writes=[buf(f"vtm{sj}")], excl=(half == 0),
                deps=(mtbf_rd if j < 2 else ()))
        pbz = buf("bk_pb1")
        zop1 = S.add("pe", lambda e, j=j, ks=ks: e.matmul(
            PB[1][:, 0:NR], u8t[:, :], ketm[ks][:, :], start=True, stop=False),
            reads=[buf("u8t"), buf(f"ketm{ks}")], writes=[pbz],
            deps=prev_a_evacs)
        zop2 = S.add("pe", lambda e, j=j: e.matmul(
            PB[1][:, 0:NR], onesr[:, :], carrb[j % 2][:, :],
            start=False, stop=True),
            reads=[buf("onesr"), buf(f"carrb{j % 2}")], writes=[pbz], excl=False)
        S.add("dve", lambda e, sj=sj: e.tensor_reduce(
            out=hsb[sj][:, :],
            in_=etm[sj].rearrange("p (h c) -> p h c", c=32)[:, :, 0:16],
            axis=AX.X, op=ALU.add),
            reads=[buf(f"etm{sj}")], writes=[buf(f"hsb{sj}")], strict=True)
        dop = S.add("dve", lambda e, sj=sj: e.tensor_tensor(
            out=dbf.rearrange("p (h z) -> p h z", z=32),
            in0=PB[1][:, 0:NR].rearrange("p (h z) -> p h z", z=32),
            in1=hsb[sj].rearrange("p (h o) -> p h o", o=1).broadcast_to(
                [128, 16, 32]),
            op=ALU.mult),
            reads=[pbz, buf(f"hsb{sj}")], writes=[buf("dbf")],
            deps=(mtbf_rd if j == 0 else ()))
        S.add("act", lambda e: e.activation(out=lnd[:, :], in_=dbf[:, :],
                                            func=AF.Ln),
              reads=[buf("dbf")], writes=[buf("lnd")])
        S.add("dve", lambda e: e.tensor_tensor(
            out=dbf[:, :], in0=PB[0][:, 0:NR], in1=lnd[:, :], op=ALU.subtract),
            reads=[pbq, buf("lnd")], writes=[buf("dbf")])
        S.add("act", lambda e, sj=sj: e.activation(
            out=qtm[sj][:, :], in_=dbf[:, :], func=AF.Exp),
            reads=[buf("dbf")], writes=[buf(f"qtm{sj}")])
        S.add("pe", lambda e, j=j, ks=ks: e.matmul(
            PB[0][0:1, 0:NR], ones1[:, :], ketm[ks][:, :],
            start=True, stop=True),
            reads=[buf("ones1"), buf(f"ketm{ks}")], writes=[pbq])
        S.add("dve", lambda e, j=j: e.scalar_tensor_tensor(
            out=carrf[(j + 1) % 2][:, :], in0=PB[0][0:1, 0:NR], scalar=8.0,
            in1=carrf[j % 2][:, :], op0=ALU.mult, op1=ALU.add),
            reads=[pbq, buf(f"carrf{j % 2}")],
            writes=[buf(f"carrf{(j + 1) % 2}")])
        S.add("act", lambda e, j=j: e.copy(out=carrb[(j + 1) % 2][:, :],
                                           in_=carrf[(j + 1) % 2][:, :]),
              reads=[buf(f"carrf{(j + 1) % 2}")],
              writes=[buf(f"carrb{(j + 1) % 2}")])
        ptq = buf("bk_pt")
        for h in range(NH):
            p, lo = h >> 1, 32 * (h & 1)
            S.add("pe", lambda e, sj=sj, h=h, p=p, lo=lo: e.transpose(
                PT[lo:lo + 32, 128 * p:128 * p + 128],
                qtm[sj][:, 32 * h:32 * h + 32], ident[:, :]),
                reads=[buf(f"qtm{sj}"), b_id], writes=[ptq], excl=(h == 0))
        for h in range(NH):
            p, lo = h >> 1, 32 * (h & 1)
            S.add("act", lambda e, sj=sj, h=h, p=p, lo=lo: e.copy(
                out=qfm[sj][h][lo:lo + 32, :],
                in_=PT[lo:lo + 32, 128 * p:128 * p + 128]),
                reads=[ptq], writes=[buf(f"qfm{sj}_{h}")], excl=False)
        return [dop, zop1, zop2]

    def emit_body(j, dops):
        nonlocal_holder = None
        a_evacs = []

        def emit_amm(h):
            p = h >> 1
            pa = buf(f"pa{h % 4}")
            dd = (list(dops) if h < 4 else []) + a_evacs[-1:]
            S.add("pe", lambda e, j=j, p=p, h=h: e.matmul(
                PB[1][:, 128 * (h % 4):128 * (h % 4) + 128],
                kefm[p][0:64, CH * j:CH * j + 128],
                qfm[j % 2][h][0:64, :], start=True, stop=True),
                reads=[buf(f"kefm{p}"), buf(f"qfm{j % 2}_{h}")], writes=[pa],
                deps=dd)
            aev = S.add("dve", lambda e, h=h: e.scalar_tensor_tensor(
                out=asb[h % 4][:, :],
                in0=PB[1][:, 128 * (h % 4):128 * (h % 4) + 128],
                scalar=1.0, in1=m1t[:, :], op0=ALU.mult, op1=ALU.mult),
                reads=[pa, buf("m1t")], writes=[buf(f"asb{h % 4}")])
            a_evacs.append(aev)

        emit_amm(0)
        for h in range(NH):
            p, vc = h >> 1, 64 * h
            ti = h >> 1
            hb = h // 8
            pds_b = buf("bk_pds")
            ks = j % 2
            S.add("pe", lambda e, h=h, vc=vc, j=j, ks=ks, hb=hb: e.matmul(
                PDS[0:32, 64 * h - 512 * hb:64 * h - 512 * hb + 64],
                ketm[ks][:, 32 * h:32 * h + 32], vtm[j % 2][:, vc:vc + 64],
                start=True, stop=True),
                reads=[buf(f"ketm{ks}"), buf(f"vtm{j % 2}")], writes=[pds_b],
                excl=(h == 8 * hb))
            S.add("pe", lambda e, h=h, vc=vc, j=j, ks=ks, hb=hb: e.matmul(
                PDS[32:64, 64 * h - 512 * hb:64 * h - 512 * hb + 64],
                ketm[ks][:, 32 * h:32 * h + 32], vtm[j % 2][:, vc:vc + 64],
                start=True, stop=True),
                reads=[buf(f"ketm{ks}"), buf(f"vtm{j % 2}")], writes=[pds_b],
                excl=False)
            if h + 1 < NH:
                emit_amm(h + 1)
            pxo = buf(f"bk_pxo{ti // 4}")
            S.add("pe", lambda e, h=h, vc=vc, ti=ti, j=j: e.matmul(
                PXO[ti // 4][64 * (h & 1):64 * (h & 1) + 64,
                             128 * (ti % 4):128 * (ti % 4) + 128],
                vtm[j % 2][:, vc:vc + 64], asb[h % 4][:, :],
                start=True, stop=False),
                reads=[buf(f"vtm{j % 2}"), buf(f"asb{h % 4}")], writes=[pxo],
                excl=((h & 1) == 0))
            S.add("pe", lambda e, h=h, p=p, vc=vc, ti=ti, j=j:
                  e.matmul(
                PXO[ti // 4][64 * (h & 1):64 * (h & 1) + 64,
                             128 * (ti % 4):128 * (ti % 4) + 128],
                Sbf[j % 2][0:64, vc:vc + 64],
                qfm[j % 2][h][0:64, :],
                start=False, stop=True),
                reads=[buf(f"Sbf_{j % 2}"), buf(f"qfm{j % 2}_{h}")],
                writes=[pxo], excl=False)
            if h % 8 == 7:
                S.add("dve", lambda e, j=j, hb=hb: e.tensor_tensor(
                    out=S32[(j + 1) % 2][0:64, 512 * hb:512 * hb + 512],
                    in0=S32[j % 2][0:64, 512 * hb:512 * hb + 512],
                    in1=PDS[:, :], op=ALU.add),
                    reads=[buf(f"S32_{j % 2}"), pds_b],
                    writes=[buf(f"S32_{(j + 1) % 2}")], excl=(hb == 0))
        return a_evacs

    def emit_tail(j):
        sj = j % 2
        so = 8 * sj
        S.add("act", lambda e, j=j: e.copy(out=Sbf[(j + 1) % 2][:, :],
                                           in_=S32[(j + 1) % 2][:, :]),
              reads=[buf(f"S32_{(j + 1) % 2}")],
              writes=[buf(f"Sbf_{(j + 1) % 2}")])
        for ti in range(8):
            S.add("dve", lambda e, ti=ti: e.tensor_copy(
                out=xofm[ti][:, :],
                in_=PXO[ti // 4][:, 128 * (ti % 4):128 * (ti % 4) + 128]),
                reads=[buf(f"bk_pxo{ti // 4}")], writes=[buf(f"xofm{ti}")])
        for half in range(2):
            pbo = buf(f"bk_pb{2 + half}")
            for ti in range(8):
                S.add("pe", lambda e, ti=ti, half=half: e.matmul(
                    PB[2 + half][:, 0:512], xofm[ti][:, :],
                    woT[ti][:, 512 * half:512 * half + 512],
                    start=(ti == 0), stop=(ti == 7)),
                    reads=[buf(f"xofm{ti}"), buf(f"woT{ti}")], writes=[pbo],
                    excl=(ti == 0))
        S.add("pool", lambda e, j=j, sj=sj: e.dma_start(
            out=xres[sj][:, :], in_=x_d[CH * j:CH * (j + 1), :]),
            writes=[buf(f"xres{sj}")], is_dma=True,
            deps=(mtbf_rd if j < 2 else ()))
        for half in range(2):
            S.add("dve", lambda e, half=half, sj=sj: e.tensor_tensor(
                out=pre1[:, 512 * half:512 * half + 512],
                in0=PB[2 + half][:, 0:512],
                in1=xres[sj][:, 512 * half:512 * half + 512], op=ALU.add),
                reads=[buf(f"bk_pb{2 + half}"), buf(f"xres{sj}")],
                writes=[buf("pre1")], excl=(half == 0))
        if j == 0:
            tap("etm0", etm[0][:, :], ["etm0"], lvl=2)
            tap("hsb0", hsb[0][:, :], ["hsb0"], lvl=2)
            tap("qtm0", qtm[0][:, :], ["qtm0"], lvl=2)
            tap("ketm0", ketm[0][:, :], ["ketm0"], lvl=2)
            tap("vtm0", vtm[0][:, :], ["vtm0"], lvl=2)
            tap("qfm0", qfm[0][0][0:64, :], ["qfm0_0"], lvl=2)
            tap("qfm1", qfm[0][1][0:64, :], ["qfm0_1"], lvl=2)
            tap("asb0", asb[0][:, :], ["asb0"], lvl=2)
            tap("xofm0", xofm[0][:, :], ["xofm0"], lvl=2)
            tap("S32n", S32[1][:, :], ["S32_1"], lvl=2)
            tap("pre1", pre1[:, :], ["pre1"], lvl=2)
        _emit_ln(S, buf, pre1, sqb, stat, so, ln1, "ln1", "pre1", "sqb")
        if j == 0:
            tap("ln1", ln1[:, :], ["ln1"], lvl=2)
        S.add("act", lambda e, sj=sj: e.copy(out=ln1b[sj][:, :], in_=ln1[:, :]),
              reads=[buf("ln1")], writes=[buf(f"ln1b{sj}")])
        S.add("pool", lambda e, j=j: e.dma_start(
            out=lntm_s[CH * j:CH * (j + 1), :], in_=ln1[:, :]),
            reads=[buf("ln1")], writes=[buf("scr_lntm")], is_dma=True,
            excl=False, dkey="lntm")
        S.add("pool", lambda e, j=j, sj=sj: e.dma_start(
            out=lnb_s[CH * j:CH * (j + 1), :], in_=ln1b[sj][:, :]),
            reads=[buf(f"ln1b{sj}")], writes=[buf(f"scr_lnb")], is_dma=True,
            excl=False, dkey=f"lnb{sj}")

    emit_ketm_load(0)
    dops = emit_prefix(0)
    for j in range(NCH):
        MARK[f"C{j}"] = len(S.ops)
        a_evacs = emit_body(j, dops)
        prev_a_evacs = a_evacs[-4:] + [dops[0]]
        if j + 1 < NCH:
            dops_next = emit_prefix(j + 1)
        emit_tail(j)
        if j + 1 < NCH:
            dops = dops_next
    MARK["P4"] = len(S.ops)
    xfm_readers = [o for dc in range(8) for o in _B[f"xfm{dc}"].readers]
    wtq_readers = [o for i in range(8) for o in _B[f"wtq{i}"].readers]
    wtk_readers = [o for i in range(8) for o in _B[f"wtk{i}"].readers]
    pre_slot0 = []
    for dc in range(8):
        pre_slot0.append(S.add("sp", lambda e, dc=dc: e.dma_start(
            out=wuT[0][dc][:, :],
            in_=wub_s[0:512, 128 * dc:128 * dc + 128], transpose=True),
            reads=[buf("scr_wu")], writes=[buf(f"wuT0_{dc}")],
            is_dma=True, deps=wtk_readers, dkey=f"wuT0{dc % 2}"))
    for dc in range(8):
        pre_slot0.append(S.add("sp", lambda e, dc=dc: e.dma_start(
            out=wgT[0][dc][:, :],
            in_=wgb_s[0:512, 128 * dc:128 * dc + 128], transpose=True),
            reads=[buf("scr_wg")], writes=[buf(f"wgT0_{dc}")],
            is_dma=True, deps=wtq_readers, dkey=f"wgT0{dc % 2}"))
    lnfm_ops = []
    for dc in range(8):
        lnfm_ops.append(S.add("sp", lambda e, dc=dc: e.dma_start(
            out=lnfm[dc][:, :], in_=lnb_s[:, 128 * dc:128 * dc + 128],
            transpose=True),
            reads=[buf("scr_lnb")], writes=[buf(f"lnfm{dc}")], is_dma=True,
            deps=xfm_readers, dkey=f"lnfmT{dc % 4}"))
    scanB_readers = []
    for nm in (["xres0", "xres1", "asb0", "asb1", "asb2", "asb3"] +
               [f"xofm{i}" for i in range(8)] +
               [f"qfm{s}_{h}" for s in range(2) for h in range(16)]):
        scanB_readers += _B[nm].readers + _B[nm].writers
    for hg in range(6):
        sl = hg % 2
        if hg == 0:
            slot_ops = list(pre_slot0)
        else:
            slot_ops = []
        for dc in range(8) if hg else []:
            dep0 = scanB_readers if hg == 1 else []
            slot_ops.append(S.add("sp", lambda e, hg=hg, sl=sl, dc=dc: e.dma_start(
                out=wgT[sl][dc][:, :],
                in_=wgb_s[512 * hg:512 * (hg + 1), 128 * dc:128 * dc + 128],
                transpose=True),
                reads=[buf("scr_wg")], writes=[buf(f"wgT{sl}_{dc}")],
                is_dma=True, deps=dep0, dkey=f"wgT{sl}{dc % 2}"))
            slot_ops.append(S.add("sp", lambda e, hg=hg, sl=sl, dc=dc: e.dma_start(
                out=wuT[sl][dc][:, :],
                in_=wub_s[512 * hg:512 * (hg + 1), 128 * dc:128 * dc + 128],
                transpose=True),
                reads=[buf("scr_wu")], writes=[buf(f"wuT{sl}_{dc}")],
                is_dma=True, deps=dep0, dkey=f"wuT{sl}{dc % 2}"))
        for hc in range(4):
            hh = 4 * hg + hc
            for ts2 in range(4):
                pg, pu = buf("bk_pb0"), buf("bk_pb1")
                for dc in range(8):
                    S.add("pe", lambda e, sl=sl, hc=hc, ts2=ts2, dc=dc: e.matmul(
                        PB[0][:, 0:512], wgT[sl][dc][:, 128 * hc:128 * hc + 128],
                        lnfm[dc][:, 512 * ts2:512 * ts2 + 512],
                        start=(dc == 0), stop=(dc == 7)),
                        reads=[buf(f"wgT{sl}_{dc}"), buf(f"lnfm{dc}")],
                        writes=[pg], excl=(dc == 0),
                        deps=(slot_ops + lnfm_ops) if (hc == 0 and ts2 == 0
                                                       and dc == 0) else ())
                for dc in range(8):
                    S.add("pe", lambda e, sl=sl, hc=hc, ts2=ts2, dc=dc: e.matmul(
                        PB[1][:, 0:512], wuT[sl][dc][:, 128 * hc:128 * hc + 128],
                        lnfm[dc][:, 512 * ts2:512 * ts2 + 512],
                        start=(dc == 0), stop=(dc == 7)),
                        reads=[buf(f"wuT{sl}_{dc}"), buf(f"lnfm{dc}")],
                        writes=[pu], excl=(dc == 0))
                S.add("act", lambda e: e.activation(
                    out=silb[:, 0:512], in_=PB[0][:, 0:512], func=AF.Silu,
                    scale=QS),
                    reads=[pg], writes=[buf("silb")],
                    deps=(scanB_readers if (hg == 0 and hc == 0 and ts2 == 0)
                          else ()))
                S.add("dve", lambda e, hh=hh, ts2=ts2: e.tensor_tensor(
                    out=hpr[hh][:, 512 * ts2:512 * ts2 + 512],
                    in0=silb[:, 0:512], in1=PB[1][:, 0:512], op=ALU.mult),
                    reads=[buf("silb"), pu], writes=[buf(f"hpr{hh}")],
                    excl=False)
    tap("lnfm0", lnfm[0][:, 0:512], ["lnfm0"], lvl=2)
    tap("hpr0", hpr[0][:, 0:512], ["hpr0"], lvl=2)
    tap("hpr16", hpr[16][:, 0:512], ["hpr16"], lvl=2)
    wvwo_readers = [o for dc in range(8)
                    for o in (_B[f"wvT{dc}"].readers + _B[f"woT{dc}"].readers)]
    sket_readers = (_B["ketm0"].readers + _B["ketm1"].readers +
                    _B["S32_0"].readers + _B["S32_1"].readers +
                    _B["Sbf_0"].readers + _B["Sbf_1"].readers)
    wdT_ops = []
    for hh in range(24):
        wdT_ops.append(S.add("sp", lambda e, hh=hh: e.dma_start(
            out=wdT[hh][:, :], in_=wdb_s[:, 128 * hh:128 * hh + 128],
            transpose=True),
            reads=[buf("scr_wd")], writes=[buf(f"wdT{hh}")], is_dma=True,
            deps=((sket_readers if hh == 0 else _B["silb"].readers)
                  if hh < 2 else
                  (wvwo_readers if hh == 2 else
                   ([o for sl2 in range(1) for dc2 in range(8)
                     for o in _B[f"wgT0_{dc2}"].readers +
                     _B[f"wuT0_{dc2}"].readers] if hh == 18 else []))),
            dkey=f"wdT{hh % 3}"))
    yops = []
    for tj in range(NCH):
        sj = tj % 2
        so = 8 * sj
        S.add("pool", lambda e, tj=tj, sj=sj: e.dma_start(
            out=ltm[sj][:, :], in_=lntm_s[CH * tj:CH * (tj + 1), :]),
            reads=[buf("scr_lntm")], writes=[buf(f"ltm{sj}")], is_dma=True,
            deps=([o for dc2 in range(8) for o in _B[f"lnfm{dc2}"].readers]
                  if tj < 2 else []))
        for half in range(2):
            pbo = buf(f"bk_pb{2 + half}")
            for hh in range(24):
                S.add("pe", lambda e, tj=tj, half=half, hh=hh: e.matmul(
                    PB[2 + half][:, 0:512], hpr[hh][:, CH * tj:CH * tj + 128],
                    wdT[hh][:, 512 * half:512 * half + 512],
                    start=(hh == 0), stop=(hh == 23)),
                    reads=[buf(f"hpr{hh}"), buf(f"wdT{hh}")], writes=[pbo],
                    excl=(hh == 0),
                    deps=(wdT_ops if (tj == 0 and half == 0 and hh == 0)
                          else ()))
            S.add("dve", lambda e, half=half, sj=sj: e.scalar_tensor_tensor(
                out=pr2[:, 512 * half:512 * half + 512],
                in0=PB[2 + half][:, 0:512], scalar=QS2,
                in1=ltm[sj][:, 512 * half:512 * half + 512],
                op0=ALU.mult, op1=ALU.add),
                reads=[buf(f"bk_pb{2 + half}"), buf(f"ltm{sj}")],
                writes=[buf("pr2")], excl=(half == 0))
        if tj == 0:
            tap("ltm0", ltm[0][:, :], ["ltm0"], lvl=2)
            tap("pr2", pr2[:, :], ["pr2"], lvl=2)
        _emit_ln(S, buf, pr2, sq2, stat, so, outb[sj], f"outb{sj}", "pr2", "sq2")
        yops.append(S.add("pool", lambda e, tj=tj, sj=sj: e.dma_start(
            out=y_d[CH * tj:CH * (tj + 1), :], in_=outb[sj][:, :]),
            reads=[buf(f"outb{sj}")], writes=[buf("yout")], is_dma=True,
            excl=False, dkey=f"y{sj}"))
    S.add("sp", lambda e: e.nop(), deps=yops + tap_list)
    MARK["ALL"] = len(S.ops)

    import builtins
    builtins._KMARK = dict(MARK)
    if stop is not None:
        k = MARK[stop] if isinstance(stop, str) else stop
        S.ops = S.ops[:k]
        last = {}
        for op in S.ops:
            last[op.engine] = op
        S.add("sp", lambda e: e.nop(),
              deps=[o for o in last.values()])

    with nc.Block() as block:
        S.finalize(nc, block, ctx)
    ctx.close()
    return nc


def _emit_ln(S, buf, src, sq, stat, so, dst, dst_name, src_name, sq_name):
    st = buf(f"stat{so}")
    S.add("dve", lambda e: e.tensor_reduce(
        out=stat[:, so:so + 1], in_=src[:, :], axis=AX.X, op=ALU.add),
        reads=[buf(src_name)], writes=[st], strict=True)
    S.add("dve", lambda e: e.tensor_tensor(out=sq[:, :], in0=src[:, :],
                                           in1=src[:, :], op=ALU.mult),
          reads=[buf(src_name)], writes=[buf(sq_name)], strict=True)
    S.add("dve", lambda e: e.tensor_reduce(
        out=stat[:, so + 1:so + 2], in_=sq[:, :], axis=AX.X, op=ALU.add),
        reads=[buf(sq_name)], writes=[st], excl=False, strict=True)
    S.add("dve", lambda e: e.tensor_scalar(
        out=stat[:, so + 2:so + 3], in0=stat[:, so:so + 1], scalar1=1.0 / D,
        scalar2=None, op0=ALU.mult),
        reads=[st], writes=[st], excl=False, strict=True)
    S.add("dve", lambda e: e.tensor_scalar(
        out=stat[:, so + 3:so + 4], in0=stat[:, so + 2:so + 3], scalar1=-1.0,
        scalar2=None, op0=ALU.mult),
        reads=[st], writes=[st], excl=False, strict=True)
    S.add("dve", lambda e: e.tensor_scalar(
        out=stat[:, so + 4:so + 5], in0=stat[:, so + 1:so + 2], scalar1=1.0 / D,
        scalar2=EPS_LN, op0=ALU.mult, op1=ALU.add),
        reads=[st], writes=[st], excl=False, strict=True)
    S.add("dve", lambda e: e.scalar_tensor_tensor(
        out=stat[:, so + 5:so + 6], in0=stat[:, so + 2:so + 3],
        scalar=stat[:, so + 2:so + 3], in1=stat[:, so + 4:so + 5],
        op0=ALU.mult, op1=ALU.subtract),
        reads=[st], writes=[st], excl=False, strict=True)
    S.add("act", lambda e: e.activation(
        out=stat[:, so + 6:so + 7], in_=stat[:, so + 5:so + 6], func=AF.Ln,
        scale=-1.0),
        reads=[st], writes=[st], excl=False, strict=True)
    S.add("act", lambda e: e.activation(
        out=stat[:, so + 7:so + 8], in_=stat[:, so + 6:so + 7], func=AF.Exp,
        scale=-0.5),
        reads=[st], writes=[st], excl=False, strict=True)
    S.add("dve", lambda e: e.tensor_scalar(
        out=dst[:, :], in0=src[:, :], scalar1=stat[:, so + 3:so + 4],
        scalar2=stat[:, so + 7:so + 8], op0=ALU.add, op1=ALU.mult),
        reads=[buf(src_name), st], writes=[buf(dst_name)], strict=True)


_NC_CACHE = {}


def kernel(**inputs):
    x = np.ascontiguousarray(np.asarray(inputs["L0_attn_x"]).reshape(T, D))
    wqkv = np.ascontiguousarray(np.asarray(inputs["L0_attn_w_qkv"]))
    wout = np.ascontiguousarray(np.asarray(inputs["L0_attn_w_out"]))
    code = np.ascontiguousarray(np.asarray(inputs["L0_attn_fc_code"])
                                .reshape(NH * CS, HD))
    wg = np.ascontiguousarray(np.asarray(inputs["L0_ffn_w_gate"], np.int32))
    wu = np.ascontiguousarray(np.asarray(inputs["L0_ffn_w_up"], np.int32))
    wd = np.ascontiguousarray(np.asarray(inputs["L0_ffn_w_down"], np.int32))

    ident = np.eye(128, dtype=np.float32).astype(ml_dtypes.bfloat16)
    st = np.arange(128)
    u8 = (8.0 * (st[:, None] <= st[None, :])).astype(ml_dtypes.bfloat16)
    m1 = (st[:, None] <= st[None, :]).astype(np.float32)
    onesr = np.ones((1, 128), dtype=ml_dtypes.bfloat16)
    ones1 = np.ones((128, 1), dtype=ml_dtypes.bfloat16)
    rr = np.arange(64)
    padv = np.where((rr % 32) >= 16, 1e9, -3.0e38).astype(np.float32)[:, None]

    if "nc" not in _NC_CACHE:
        _NC_CACHE["nc"] = build_nc()
    nc = _NC_CACHE["nc"]

    in_maps = []
    for c in range(NCORES):
        rvec = np.zeros((33, 8), dtype=np.float32)
        rvec[:, :c] = 1.0
        in_maps.append({
            "x": np.asarray(x[TLOC * c:TLOC * (c + 1)], np.float32),
            "w_qkv": np.asarray(wqkv, np.float32),
            "w_out": np.asarray(wout, np.float32),
            "fc_code": np.asarray(code, np.float32),
            "w_gate": wg, "w_up": wu, "w_down": wd,
            "c_ident": ident, "c_u8": u8, "c_m1": m1, "c_onesr": onesr, "c_ones1": ones1,
            "c_padv": padv, "c_rvec": rvec,
        })
    res = run_bass_kernel_spmd(nc, in_maps, core_ids=list(range(NCORES)))
    out = np.concatenate([np.asarray(r["y"]) for r in res.results], axis=0)
    return out.reshape(1, T, D)


# version 16 (final)
# speedup vs baseline: 1.2047x; 1.0082x over previous
import numpy as np
import ml_dtypes
import concourse.bass as bass
import concourse.mybir as mybir
from concourse.dt import dt
from concourse.bass_utils import run_bass_kernel_spmd
from contextlib import ExitStack

F32 = dt.float32
BF16 = dt.bfloat16
I32 = dt.int32
AF = mybir.ActivationFunctionType
ALU = mybir.AluOpType
AX = mybir.AxisListType

NCORES = 8
T = 16384
TLOC = T // NCORES
D = 1024
NH = 16
HD = 64
CS = 16
HID = 3072
CH = 128
NCH = TLOC // CH
SCALE = 0.125
QS = 1.0 / 127.0
QS2 = QS * QS
EPS_Z = 1e-9
EPS_LN = 1e-5
NR = 512
ENGS = ("pe", "act", "dve", "pool", "sp")


class Op:
    __slots__ = ("engine", "emit", "deps", "inc", "val", "is_dma", "sem",
                 "needed", "strict")

    def __init__(self, engine, emit, deps, is_dma, sem, strict):
        self.engine = engine
        self.emit = emit
        self.deps = deps
        self.inc = 0
        self.val = None
        self.is_dma = is_dma
        self.sem = sem
        self.needed = False
        self.strict = strict


class Buf:
    __slots__ = ("writers", "readers", "name")

    def __init__(self, name):
        self.writers = []
        self.readers = []
        self.name = name


class Sched:
    def __init__(self):
        self.ops = []

    def add(self, engine, emit, reads=(), writes=(), deps=(), is_dma=False,
            excl=True, dkey=None, strict=False):
        dd = list(deps)
        for b in reads:
            dd += b.writers
        for b in writes:
            dd += b.readers + b.writers
        sem = None
        if is_dma:
            sem = "dma:" + (dkey or (writes[0].name if writes else "misc"))
        op = Op(engine, emit, dd, is_dma, sem, strict)
        self.ops.append(op)
        for b in reads:
            b.readers.append(op)
        for b in writes:
            if excl:
                b.writers = [op]
                b.readers = []
            else:
                b.writers.append(op)
        return op

    def finalize(self, nc, block, ctx):
        for op in self.ops:
            for d in op.deps:
                d.needed = True
        sem_names = list(ENGS)
        for op in self.ops:
            if op.is_dma and op.sem not in sem_names:
                sem_names.append(op.sem)
        sems = {n: ctx.enter_context(nc.semaphore(f"s{i}"))
                for i, n in enumerate(sem_names)}
        counts = {n: 0 for n in sem_names}
        prefix = {}
        for idx, op in enumerate(self.ops):
            if op.is_dma:
                op.inc = 16
            elif op.needed:
                op.inc = 1
                op.sem = op.engine
            if op.inc:
                counts[op.sem] += op.inc
                op.val = counts[op.sem]
                if op.is_dma:
                    prefix.setdefault(op.sem, []).append((idx, op.val))
        import bisect as _bis

        def dma_wait_val(sem, cidx):
            lst = prefix[sem]
            k = _bis.bisect_left(lst, (cidx, -1))
            return lst[k - 1][1] if k else 0

        idx_of = {id(op): i for i, op in enumerate(self.ops)}

        def body(eng_handle, ename):
            waited = {}
            for op in self.ops:
                if op.engine != ename:
                    continue
                cidx = idx_of[id(op)]
                want = {}
                for d in op.deps:
                    if d.inc == 0:
                        continue
                    if d.engine == ename and not d.is_dma and not d.strict:
                        continue
                    v = dma_wait_val(d.sem, cidx) if d.is_dma else d.val
                    if v > want.get(d.sem, 0):
                        want[d.sem] = v
                for se, v in want.items():
                    if waited.get(se, 0) >= v:
                        continue
                    eng_handle.wait_ge(sems[se], v)
                    waited[se] = v
                ins = op.emit(eng_handle)
                if op.inc:
                    ins.then_inc(sems[op.sem], op.inc)

        block.tensor(lambda e: body(e, "pe"))
        block.scalar(lambda e: body(e, "act"))
        block.vector(lambda e: body(e, "dve"))
        block.gpsimd(lambda e: body(e, "pool"))
        block.sync(lambda e: body(e, "sp"))


def build_nc(stop=None, taps=False):
    nc = bass.Bass("TRN2", target_bir_lowering=False,
                   dynamic_dma_scratch_size=16384,
                   detect_race_conditions=False)
    x_d = nc.dram_tensor("x", [TLOC, D], F32, kind="ExternalInput")
    wqkv_d = nc.dram_tensor("w_qkv", [3 * D, D], F32, kind="ExternalInput")
    wout_d = nc.dram_tensor("w_out", [D, D], F32, kind="ExternalInput")
    code_d = nc.dram_tensor("fc_code", [NH * CS, HD], F32, kind="ExternalInput")
    wg_d = nc.dram_tensor("w_gate", [HID, D], I32, kind="ExternalInput")
    wu_d = nc.dram_tensor("w_up", [HID, D], I32, kind="ExternalInput")
    wd_d = nc.dram_tensor("w_down", [D, HID], I32, kind="ExternalInput")
    ident_d = nc.dram_tensor("c_ident", [128, 128], BF16, kind="ExternalInput")
    u8_d = nc.dram_tensor("c_u8", [128, 128], BF16, kind="ExternalInput")
    onesr_d = nc.dram_tensor("c_onesr", [1, 128], BF16, kind="ExternalInput")
    m1_d = nc.dram_tensor("c_m1", [128, 128], F32, kind="ExternalInput")
    ones1_d = nc.dram_tensor("c_ones1", [128, 1], BF16, kind="ExternalInput")
    padv_d = nc.dram_tensor("c_padv", [64, 1], F32, kind="ExternalInput")
    rvec_d = nc.dram_tensor("c_rvec", [33, 8], F32, kind="ExternalInput")
    y_d = nc.dram_tensor("y", [TLOC, D], F32, kind="ExternalOutput")

    xb_s = nc.dram_tensor("s_xb", [TLOC, D], BF16, kind="Internal")
    wvb_s = nc.dram_tensor("s_wvb", [D, D], BF16, kind="Internal")
    wob_s = nc.dram_tensor("s_wob", [D, D], BF16, kind="Internal")
    wgb_s = nc.dram_tensor("s_wgb", [HID, D], BF16, kind="Internal")
    wub_s = nc.dram_tensor("s_wub", [HID, D], BF16, kind="Internal")
    wdb_s = nc.dram_tensor("s_wdb", [D, HID], BF16, kind="Internal")
    kef_s = nc.dram_tensor("s_kef", [NR, TLOC], BF16, kind="Internal")
    lnb_s = nc.dram_tensor("s_lnb", [TLOC, D], BF16, kind="Internal")
    lntm_s = nc.dram_tensor("s_lntm", [TLOC, D], F32, kind="Internal")
    dup_s = nc.dram_tensor("s_dup", [33, 1024], F32, kind="Internal")
    ccm_in = nc.dram_tensor("cc_m_in", [64, 8], F32, kind="Internal")
    ccm_out = nc.dram_tensor("cc_m_out", [64, 8], F32, kind="Internal")
    ccs_in = nc.dram_tensor("cc_s_in", [33, 1024], F32, kind="Internal")
    ccs_out = nc.dram_tensor("cc_s_out", [33 * 8, 1024], F32, kind="Internal")

    ctx = ExitStack()

    def sb(name, shape, dtp):
        return ctx.enter_context(nc.sbuf_tensor(name, shape, dtp))

    def psum(name, shape, dtp):
        return ctx.enter_context(nc.psum_tensor(name, shape, dtp))

    ident = sb("ident", [128, 128], BF16)
    u8t = sb("u8t", [128, 128], BF16)
    m1t = sb("m1t", [128, 128], F32)
    onesr = sb("onesr", [1, 128], BF16)
    ones1 = sb("ones1", [128, 1], BF16)
    padv = sb("padv", [64, 1], F32)
    rvec = sb("rvec", [33, 8], F32)
    negm = sb("negm", [64, 8], F32)
    mred = sb("mred", [64, 4], F32)
    mloc = sb("mloc", [64, 8], F32)
    carrf = [sb(f"carrf{i}", [1, NR], F32) for i in range(2)]
    carrb = [sb(f"carrb{i}", [1, NR], BF16) for i in range(2)]
    row127 = sb("row127", [1, NR], F32)
    hsb = [sb(f"hsb{i}", [128, 16], F32) for i in range(2)]
    stat = sb("stat", [128, 16], F32)

    ARK = sb("ARK", [128, 16384], BF16)
    ARB = sb("ARB", [128, 16384], BF16)
    ARA = sb("ARA", [128, 8192], BF16)
    ARG = sb("ARG", [128, 8192], BF16)
    ARG2 = sb("ARG2", [128, 8192], BF16)
    ARD = sb("ARD", [128, 2048], BF16)
    ARE = sb("ARE", [128, 16384], BF16)
    ARF = sb("ARF", [128, 8192], BF16)
    ARH = sb("ARH", [128, 8192], BF16)
    ARI = sb("ARI", [128, 8192], BF16)

    qk_bf = [ARA[:, 1024 * i:1024 * (i + 1)] for i in range(8)]
    wst = [ARG[:, 0:2048].bitcast(F32), ARG[:, 2048:4096].bitcast(F32)]
    wsti = [ARG[:, 0:2048].bitcast(I32), ARG[:, 2048:4096].bitcast(I32)]
    wbf = [ARG[:, 4096:5120], ARG[:, 5120:6144]]
    xtb = [ARG[:, 6144:7168], ARG[:, 7168:8192]]
    codeb = ARG2[0:16, 0:1024]
    codeT = ARG2[:, 1024:1280]
    codeblk = [ARG2[:, 1536 + 256 * i:1536 + 256 * (i + 1)] for i in range(8)]
    xfm = [ARB[:, 2048 * i:2048 * (i + 1)] for i in range(8)]
    wtq = [ARF[:, 512 * i:512 * (i + 1)] for i in range(8)]
    wtk = [ARF[:, 4096 + 512 * i:4096 + 512 * (i + 1)] for i in range(8)]
    wvT = [ARE[:, 1024 * i:1024 * (i + 1)] for i in range(8)]
    woT = [ARE[:, 8192 + 1024 * i:8192 + 1024 * (i + 1)] for i in range(8)]
    xsb = [ARA[:, 1024 * s:1024 * (s + 1)] for s in range(2)]
    kefm = [ARK[0:64, 2048 * p:2048 * (p + 1)] for p in range(8)]
    ketm = [ARD[:, 512 * i:512 * (i + 1)] for i in range(2)]
    mtbf = [ARG[:, 512 * i:512 * (i + 1)] for i in range(8)]
    srs = ARI[:, 0:2048].bitcast(F32)[0:33, :]
    accb = [ARI[:, 2048 + 2048 * i:2048 + 2048 * (i + 1)].bitcast(F32)[0:33, :]
            for i in range(2)]
    gth = ARI[:, 6144:8192].bitcast(F32)[0:33, :]
    dbf = ARG[:, 0:1024].bitcast(F32)
    lnd = ARG[:, 1024:2048].bitcast(F32)
    vtm = [ARG[:, 2048 + 1024 * i:2048 + 1024 * (i + 1)] for i in range(2)]
    pre1 = ARG[:, 4096 + 2048:4096 + 4096].bitcast(F32)
    sqb = ARG2[:, 0:2048].bitcast(F32)
    ln1 = ARG2[:, 2048:4096].bitcast(F32)
    ln1b = [ARG2[:, 4096 + 1024 * i:4096 + 1024 * (i + 1)] for i in range(2)]
    etm = [ARG2[:, 6144 + 512 * i:6144 + 512 * (i + 1)] for i in range(2)]
    qtm = [ARG2[:, 7168 + 512 * i:7168 + 512 * (i + 1)] for i in range(2)]
    xres = [ARH[:, 2048 * i:2048 * (i + 1)].bitcast(F32) for i in range(2)]
    qfm = [[ARH[0:64, 4096 + 2048 * s + 128 * h:4096 + 2048 * s + 128 * (h + 1)]
            for h in range(16)] for s in range(2)]
    asb = [ARI[:, 6144 + 128 * i:6144 + 128 * (i + 1)] for i in range(4)]
    xofm = [ARI[:, 6656 + 128 * i:6656 + 128 * (i + 1)] for i in range(8)]
    S32 = [ARI[0:64, 2048 * i:2048 * (i + 1)].bitcast(F32) for i in range(2)]
    Sbf = [ARI[0:64, 4096 + 1024 * i:4096 + 1024 * (i + 1)] for i in range(2)]
    hpr = ([ARK[:, 2048 * i:2048 * (i + 1)] for i in range(8)] +
           [ARA[:, 2048 * i:2048 * (i + 1)] for i in range(4)] +
           [ARG[:, 2048 * i:2048 * (i + 1)] for i in range(4)] +
           [ARG2[:, 2048 * i:2048 * (i + 1)] for i in range(4)] +
           [ARI[:, 2048 * i:2048 * (i + 1)] for i in range(4)])
    lnfm = [ARB[:, 2048 * i:2048 * (i + 1)] for i in range(8)]
    wgT = [[ARF[:, 512 * i:512 * (i + 1)] for i in range(8)],
           [ARH[:, 512 * i:512 * (i + 1)] for i in range(8)]]
    wuT = [[ARF[:, 4096 + 512 * i:4096 + 512 * (i + 1)] for i in range(8)],
           [ARH[:, 4096 + 512 * i:4096 + 512 * (i + 1)] for i in range(8)]]
    wdT = ([ARD[:, 1024 * i:1024 * (i + 1)] for i in range(2)] +
           [ARE[:, 1024 * i:1024 * (i + 1)] for i in range(16)] +
           [ARF[:, 1024 * i:1024 * (i + 1)] for i in range(6)])
    ltm = [ARB[:, 2048 * i:2048 * (i + 1)].bitcast(F32) for i in range(2)]
    outb = [ARB[:, 4096 + 2048 * i:4096 + 2048 * (i + 1)].bitcast(F32)
            for i in range(2)]
    sq2 = ARB[:, 8192:10240].bitcast(F32)
    pr2 = ARB[:, 10240:12288].bitcast(F32)
    silb = ARD[:, 1024:2048].bitcast(F32)

    PB = [psum(f"PB{i}", [128, 512], F32) for i in range(4)]
    PXO = [psum(f"PXO{i}", [128, 512], F32) for i in range(2)]
    PDS = psum("PDS", [64, 512], F32)
    PT = psum("PT", [128, 1024], BF16)

    S = Sched()
    MARK = {}
    _B = {}

    def buf(name):
        if name not in _B:
            _B[name] = Buf(name)
        return _B[name]

    tap_list = []

    def tap(name, ap, bnames, lvl=1):
        if not taps or lvl > int(taps):
            return
        td = nc.dram_tensor("tap_" + name, list(ap.shape), ap.dtype,
                            kind="ExternalOutput")
        op = S.add("pool", lambda e, td=td, ap=ap: e.dma_start(
            out=td[tuple(slice(None) for _ in ap.shape)], in_=ap),
            reads=[buf(b) for b in bnames], is_dma=True, dkey="taps")
        tap_list.append(op)

    for t, d_, nm in ((ident, ident_d, "ident"), (u8t, u8_d, "u8t"),
                      (m1t, m1_d, "m1t"),
                      (onesr, onesr_d, "onesr"), (ones1, ones1_d, "ones1"),
                      (padv, padv_d, "padv"), (rvec, rvec_d, "rvec")):
        S.add("sp", lambda e, t=t, d_=d_: e.dma_start(out=t[:, :], in_=d_[:, :]),
              writes=[buf(nm)], is_dma=True, dkey="consts")
    b_id = buf("ident")
    const_ops = list(S.ops)

    for i in range(8):
        s = i % 2
        S.add("sp", lambda e, i=i, s=s: e.dma_start(
            out=wst[s][:, :], in_=wqkv_d[128 * i:128 * (i + 1), :]),
            writes=[buf(f"wst{s}")], is_dma=True)
        S.add("act", lambda e, i=i, s=s: e.copy(out=qk_bf[i][:, :], in_=wst[s][:, :]),
              reads=[buf(f"wst{s}")], writes=[buf(f"qkbf{i}")])
    S.add("sp", lambda e: e.dma_start(
        out=wst[0][0:16, 0:NH * HD].rearrange("c (h d) -> c h d", d=HD),
        in_=code_d[:, :].rearrange("(h c) d -> c h d", c=CS)),
        writes=[buf("wst0")], is_dma=True)
    S.add("act", lambda e: e.mul(out=codeb[:, :], in_=wst[0][0:16, 0:NH * HD],
                                 mul=SCALE),
          reads=[buf("wst0")], writes=[buf("codeb")], deps=const_ops)
    b_ptc = buf("bk_pt")
    for h in range(NH):
        S.add("pe", lambda e, h=h: e.transpose(
            PT[0:64, 16 * h:16 * h + 16], codeb[:, 64 * h:64 * h + 64],
            ident[0:16, 0:16]),
            reads=[buf("codeb"), b_id], writes=[b_ptc], excl=(h == 0))
        S.add("pe", lambda e, h=h: e.transpose(
            PT[64:128, 16 * h:16 * h + 16], codeb[:, 64 * h:64 * h + 64],
            ident[0:16, 0:16]),
            reads=[buf("codeb"), b_id], writes=[b_ptc], excl=False)
    S.add("dve", lambda e: e.tensor_copy(out=codeT[:, 0:NH * CS],
                                         in_=PT[:, 0:NH * CS]),
          reads=[b_ptc], writes=[buf("codeT")])
    for i in range(8):
        S.add("dve", lambda e, i=i: e.memset(wtq[i][:, :], 0.0),
              writes=[buf(f"wtq{i}")])
        S.add("dve", lambda e, i=i: e.memset(wtk[i][:, :], 0.0),
              writes=[buf(f"wtk{i}")])
    for i in range(8):
        S.add("dve", lambda e, i=i: e.memset(codeblk[i][:, :], 0.0),
              writes=[buf(f"cbk{i}")])
        S.add("dve", lambda e, i=i: e.tensor_copy(
            out=codeblk[i][0:64, 32 * i:32 * i + 16],
            in_=codeT[0:64, 32 * i:32 * i + 16]),
            reads=[buf("codeT")], writes=[buf(f"cbk{i}")], excl=False)
        S.add("dve", lambda e, i=i: e.tensor_copy(
            out=codeblk[i][64:128, 32 * i + 16:32 * i + 32],
            in_=codeT[64:128, 32 * i + 16:32 * i + 32]),
            reads=[buf("codeT")], writes=[buf(f"cbk{i}")], excl=False)
    for which, wt in (("q", wtq), ("k", wtk)):
        if which == "k":
            for i in range(8):
                s = i % 2
                S.add("sp", lambda e, i=i, s=s: e.dma_start(
                    out=wst[s][:, :],
                    in_=wqkv_d[1024 + 128 * i:1024 + 128 * (i + 1), :]),
                    writes=[buf(f"wst{s}")], is_dma=True)
                S.add("act", lambda e, i=i, s=s: e.copy(out=qk_bf[i][:, :],
                                                        in_=wst[s][:, :]),
                      reads=[buf(f"wst{s}")], writes=[buf(f"qkbf{i}")])
        for dc in range(8):
            pb = buf("bk_pb0")
            for i in range(8):
                S.add("pe", lambda e, i=i, dc=dc: e.matmul(
                    PB[0][:, 0:256],
                    qk_bf[i][:, 128 * dc:128 * dc + 128],
                    codeblk[i][:, :], start=(i == 0), stop=(i == 7)),
                    reads=[buf(f"qkbf{i}"), buf(f"cbk{i}")], writes=[pb],
                    excl=(i == 0))
            S.add("dve", lambda e, wt=wt, dc=dc: e.tensor_copy(
                out=wt[dc].rearrange("p (h c) -> p h c", c=32)[:, :, 0:16],
                in_=PB[0][:, 0:256].rearrange("p (h c) -> p h c", c=16)),
                reads=[pb], writes=[buf(f"wt{which}{dc}")], excl=False)
    wt_done = [o for i in range(8)
               for o in (_B[f"wtq{i}"].writers + _B[f"wtk{i}"].writers)]

    pst = [ARI[:, 0:2048].bitcast(I32), ARI[:, 2048:4096].bitcast(I32)]
    pbf = [ARI[:, 4096:5120], ARI[:, 5120:6144]]
    prep_jobs = []
    for nm2, src2, scr2 in (("wg", wg_d, wgb_s), ("wu", wu_d, wub_s)):
        for i2 in range(24):
            prep_jobs.append((nm2, scr2,
                              src2[128 * i2:128 * (i2 + 1), :],
                              (128 * i2, 0)))
    for i2 in range(8):
        for c3 in range(3):
            prep_jobs.append(("wd", wdb_s,
                              wd_d[128 * i2:128 * (i2 + 1),
                                   1024 * c3:1024 * (c3 + 1)],
                              (128 * i2, 1024 * c3)))
    prep_loads, prep_casts, prep_stores = [], [], []

    def prep_load(k):
        src_ap = prep_jobs[k][2]
        s2 = k % 2
        prep_loads.append(S.add("pool", lambda e, src_ap=src_ap, s2=s2:
                                e.dma_start(out=pst[s2][:, :], in_=src_ap),
                                writes=[buf(f"pst{s2}")], is_dma=True))
    pk = [0]

    def prep_step(n):
        for _ in range(n):
            k = pk[0]
            if k >= len(prep_jobs):
                return
            pk[0] += 1
            nm2, scr2, _, (r0, c0) = prep_jobs[k]
            s2 = k % 2
            prep_casts.append(S.add("pool", lambda e, s2=s2: e.tensor_copy(
                out=pbf[s2][:, :], in_=pst[s2][:, :]),
                reads=[buf(f"pst{s2}")], writes=[buf(f"pbf{s2}")]))
            prep_stores.append(S.add(
                "pool", lambda e, scr2=scr2, r0=r0, c0=c0, s2=s2: e.dma_start(
                    out=scr2[r0:r0 + 128, c0:c0 + 1024], in_=pbf[s2][:, :]),
                reads=[buf(f"pbf{s2}")], writes=[buf(f"scr_{nm2}")],
                is_dma=True, excl=False, dkey=f"pbf{s2}"))
            if k + 2 < len(prep_jobs):
                prep_load(k + 2)

    def xres_load(j):
        s = j % 2
        S.add("pool", lambda e, j=j, s=s: e.dma_start(
            out=xres[s][:, :], in_=x_d[CH * j:CH * (j + 1), :]),
            writes=[buf(f"xres{s}")], is_dma=True)
    prep_load(0)
    prep_load(1)
    xres_load(0)
    xres_load(1)
    for j in range(NCH):
        s = j % 2
        S.add("pool", lambda e, s=s: e.tensor_copy(out=xtb[s][:, :],
                                                   in_=xres[s][:, :]),
              reads=[buf(f"xres{s}")], writes=[buf(f"xtb{s}")])
        S.add("pool", lambda e, j=j, s=s: e.dma_start(
            out=xb_s[CH * j:CH * (j + 1), :], in_=xtb[s][:, :]),
            reads=[buf(f"xtb{s}")], writes=[buf("scr_x")], is_dma=True,
            excl=False, dkey=f"xtb{s}")
        if j + 2 < NCH:
            xres_load(j + 2)
        prep_step(2)
    xfm_ops = []
    for dc in range(8):
        xfm_ops.append(S.add("sp", lambda e, dc=dc: e.dma_start(
            out=xfm[dc][:, :], in_=xb_s[:, 128 * dc:128 * dc + 128],
            transpose=True),
            reads=[buf("scr_x")], writes=[buf(f"xfm{dc}")], is_dma=True,
            deps=wt_done, dkey=f"xfmT{dc % 2}"))
    prep_step(len(prep_jobs))
    prep_guard = [prep_casts[-1], prep_loads[-1], prep_loads[-2],
                  prep_stores[-1], prep_stores[-2]]

    wvwo_casts = []
    for name, scr in (("wv", wvb_s), ("wo", wob_s)):
        for i in range(8):
            s = i % 2
            if name == "wv":
                S.add("sp", lambda e, i=i, s=s: e.dma_start(
                    out=wst[s][:, :],
                    in_=wqkv_d[2048 + 128 * i:2048 + 128 * (i + 1), :]),
                    writes=[buf(f"wst{s}")], is_dma=True)
            else:
                S.add("sp", lambda e, i=i, s=s: e.dma_start(
                    out=wst[s][:, :], in_=wout_d[128 * i:128 * (i + 1), :]),
                    writes=[buf(f"wst{s}")], is_dma=True)
            wvwo_casts.append(S.add(
                "act", lambda e, s=s: e.copy(out=wbf[s][:, :], in_=wst[s][:, :]),
                reads=[buf(f"wst{s}")], writes=[buf(f"wbf{s}")]))
            S.add("sp", lambda e, scr=scr, i=i, s=s: e.dma_start(
                out=scr[128 * i:128 * (i + 1), :], in_=wbf[s][:, :]),
                reads=[buf(f"wbf{s}")], writes=[buf(f"scr_{name}")],
                is_dma=True, excl=False, dkey=f"wbf{s}")
    wvT_ops, woT_ops = [], []
    for name, scr, dstT, lst in (("wv", wvb_s, wvT, wvT_ops),
                                 ("wo", wob_s, woT, woT_ops)):
        for dc in range(8):
            lst.append(S.add("sp", lambda e, scr=scr, dstT=dstT, dc=dc: e.dma_start(
                out=dstT[dc][:, :], in_=scr[:, 128 * dc:128 * dc + 128],
                transpose=True),
                reads=[buf(f"scr_{name}")], writes=[buf(f"{name}T{dc}")],
                is_dma=True, dkey=f"{name}T"))

    MARK["P0"] = len(S.ops)
    tap("wtq0", wtq[0][:, :], ["wtq0"], lvl=1)
    tap("wtk0", wtk[0][:, :], ["wtk0"], lvl=1)
    tap("xfm0", xfm[0][:, 0:512], ["xfm0"], lvl=1)
    tap("wvT0", wvT[0][:, :], ["wvT0"], lvl=1)
    tap("woT0", woT[0][:, :], ["woT0"], lvl=1)

    for p in range(8):
        for ts in range(4):
            gi = (p * 4 + ts) % 2
            pb = buf(f"bk_pb{gi}")
            for dc in range(8):
                S.add("pe", lambda e, p=p, ts=ts, dc=dc, gi=gi: e.matmul(
                    PB[gi][0:64, 0:512], wtk[dc][:, 64 * p:64 * p + 64],
                    xfm[dc][:, 512 * ts:512 * ts + 512],
                    start=(dc == 0), stop=(dc == 7)),
                    reads=[buf(f"wtk{dc}"), buf(f"xfm{dc}")], writes=[pb],
                    excl=(dc == 0),
                    deps=(wt_done + xfm_ops) if (p == 0 and ts == 0) else ())
            S.add("dve", lambda e, p=p, ts=ts, gi=gi: e.tensor_reduce(
                out=mred[:, ts:ts + 1], in_=PB[gi][0:64, 0:512],
                axis=AX.X, op=ALU.max),
                reads=[pb], writes=[buf("mred")], excl=(ts == 0), strict=True)
        S.add("dve", lambda e, p=p: e.tensor_reduce(
            out=mloc[:, p:p + 1], in_=mred[:, :], axis=AX.X, op=ALU.max),
            reads=[buf("mred")], writes=[buf("mloc")], excl=False, strict=True)

    tap("mlocpre", mloc[:, :], ["mloc"], lvl=1)
    S.add("pool", lambda e: e.dma_start(out=ccm_in[:, :], in_=mloc[:, :]),
          reads=[buf("mloc")], writes=[buf("ccm")], is_dma=True)
    if taps and int(taps) >= 1:
        tccin = nc.dram_tensor("tap_ccmin", [64, 8], F32, kind="ExternalOutput")
        tap_list.append(S.add("act", lambda e: e.dma_start(
            out=tccin[:, :], in_=ccm_in[:, :]),
            reads=[buf("ccm")], is_dma=True, dkey="taps"))
    S.add("pool", lambda e: e.collective_compute(
        "AllReduce", ALU.max, replica_groups=[list(range(NCORES))],
        ins=[ccm_in.ap().opt()], outs=[ccm_out.ap().opt()]),
        reads=[buf("ccm")], writes=[buf("ccmo")], strict=True)
    if taps and int(taps) >= 1:
        tccout = nc.dram_tensor("tap_ccmout", [64, 8], F32, kind="ExternalOutput")
        tap_list.append(S.add("act", lambda e: e.dma_start(
            out=tccout[:, :], in_=ccm_out[:, :]),
            reads=[buf("ccmo")], is_dma=True, dkey="taps"))
    S.add("pool", lambda e: e.dma_start(out=mloc[:, :], in_=ccm_out[:, :]),
          reads=[buf("ccmo")], writes=[buf("mloc")], is_dma=True, dkey="mback")
    tap("padv2", padv[:, :], ["padv"], lvl=1)
    S.add("dve", lambda e: e.tensor_scalar(
        out=negm[:, :], in0=mloc[:, :], scalar1=padv[:, 0:1], scalar2=-1.0,
        op0=ALU.max, op1=ALU.mult),
        reads=[buf("mloc"), buf("padv")], writes=[buf("negm")])

    MARK["P1"] = len(S.ops)
    tap("mloc", mloc[:, :], ["mloc"], lvl=1)
    tap("negm", negm[:, :], ["negm"], lvl=1)
    kexp_ops = []
    for p in range(8):
        for ts in range(4):
            gi = (p * 4 + ts) % 2
            pb = buf(f"bk_pb{gi}")
            for dc in range(8):
                S.add("pe", lambda e, p=p, ts=ts, dc=dc, gi=gi: e.matmul(
                    PB[gi][0:64, 0:512], wtk[dc][:, 64 * p:64 * p + 64],
                    xfm[dc][:, 512 * ts:512 * ts + 512],
                    start=(dc == 0), stop=(dc == 7)),
                    reads=[buf(f"wtk{dc}"), buf(f"xfm{dc}")], writes=[pb],
                    excl=(dc == 0))
            o = S.add("act", lambda e, p=p, ts=ts, gi=gi: e.activation(
                out=kefm[p][:, 512 * ts:512 * ts + 512], in_=PB[gi][0:64, 0:512],
                func=AF.Exp, bias=negm[:, p:p + 1], scale=1.0),
                reads=[pb, buf("negm")], writes=[buf(f"kefm{p}")], excl=False)
            kexp_ops.append(o)
        S.add("act", lambda e, p=p: e.dma_start(
            out=kef_s[64 * p:64 * (p + 1), :], in_=kefm[p][:, :]),
            reads=[buf(f"kefm{p}")], writes=[buf("scr_kef")], is_dma=True,
            excl=False, dkey="scr_kef")

    ztot_pb = buf("bk_pds")
    mt_pb = [buf("bk_pb0"), buf("bk_pb1"), buf("bk_pb2"), buf("bk_pb3"),
             buf("bk_pxo0"), buf("bk_pxo1"), buf("bk_pb0"), buf("bk_pb1")]
    for j in range(NCH):
        ks = j % 2
        S.add("sp", lambda e, j=j, ks=ks: e.dma_start(
            out=ketm[ks][:, :], in_=kef_s[:, CH * j:CH * j + 128], transpose=True),
            reads=[buf("scr_kef")], writes=[buf(f"ketm{ks}")], is_dma=True)
        S.add("pool", lambda e, j=j, ks=ks: e.dma_start(
            out=xres[ks][:, :], in_=x_d[CH * j:CH * (j + 1), :]),
            writes=[buf(f"xres{ks}")], is_dma=True)
        S.add("act", lambda e, j=j, ks=ks: e.copy(out=xsb[ks][:, :],
                                                  in_=xres[ks][:, :]),
              reads=[buf(f"xres{ks}")], writes=[buf(f"xsb{ks}")],
              deps=(kexp_ops + wt_done) if j < 2 else ())
        S.add("pe", lambda e, j=j, ks=ks: e.matmul(
            PDS[0:1, 0:NR], ones1[:, :], ketm[ks][:, :],
            start=(j == 0), stop=(j == NCH - 1)),
            reads=[buf(f"ketm{ks}"), buf("ones1")], writes=[ztot_pb],
            excl=(j == 0))
        for dc in range(6):
            pp = PB[dc] if dc < 4 else PXO[dc - 4]
            S.add("pe", lambda e, j=j, ks=ks, dc=dc, pp=pp: e.matmul(
                pp[:, 0:NR], xsb[ks][:, 128 * dc:128 * dc + 128],
                ketm[ks][:, :], start=(j == 0), stop=(j == NCH - 1)),
                reads=[buf(f"xsb{ks}"), buf(f"ketm{ks}")], writes=[mt_pb[dc]],
                excl=(j == 0))
    S.add("dve", lambda e: e.tensor_copy(out=row127[0:1, 0:NR],
                                         in_=PDS[0:1, 0:NR]),
          reads=[ztot_pb], writes=[buf("zr")], excl=False)
    for dc in range(6):
        pp = PB[dc] if dc < 4 else PXO[dc - 4]
        S.add("dve", lambda e, dc=dc, pp=pp: e.tensor_copy(
            out=mtbf[dc][:, :], in_=pp[:, 0:NR]),
            reads=[mt_pb[dc]], writes=[buf(f"mtbf{dc}")],
            deps=(wvwo_casts if dc == 0 else ()))
    for j in range(NCH):
        ks = j % 2
        S.add("sp", lambda e, j=j, ks=ks: e.dma_start(
            out=ketm[ks][:, :], in_=kef_s[:, CH * j:CH * j + 128], transpose=True),
            reads=[buf("scr_kef")], writes=[buf(f"ketm{ks}")], is_dma=True)
        S.add("pool", lambda e, j=j, ks=ks: e.dma_start(
            out=xres[ks][:, :], in_=x_d[CH * j:CH * (j + 1), :]),
            writes=[buf(f"xres{ks}")], is_dma=True)
        S.add("act", lambda e, j=j, ks=ks: e.copy(out=xsb[ks][:, :],
                                                  in_=xres[ks][:, :]),
              reads=[buf(f"xres{ks}")], writes=[buf(f"xsb{ks}")])
        for dc in range(6, 8):
            S.add("pe", lambda e, j=j, ks=ks, dc=dc: e.matmul(
                PB[dc - 6][:, 0:NR], xsb[ks][:, 128 * dc:128 * dc + 128],
                ketm[ks][:, :], start=(j == 0), stop=(j == NCH - 1)),
                reads=[buf(f"xsb{ks}"), buf(f"ketm{ks}")], writes=[mt_pb[dc]],
                excl=(j == 0))
    for dc in range(6, 8):
        S.add("dve", lambda e, dc=dc: e.tensor_copy(
            out=mtbf[dc][:, :], in_=PB[dc - 6][:, 0:NR]),
            reads=[mt_pb[dc]], writes=[buf(f"mtbf{dc}")])
    for hb in range(2):
        pbsr = buf("bk_pds")
        for h in range(8 * hb, 8 * hb + 8):
            for dc in range(8):
                S.add("pe", lambda e, h=h, dc=dc, hb=hb: e.matmul(
                    PDS[0:32, 64 * h - 512 * hb:64 * h - 512 * hb + 64],
                    mtbf[dc][:, 32 * h:32 * h + 32],
                    wvT[dc][:, 64 * h:64 * h + 64], start=(dc == 0),
                    stop=(dc == 7)),
                    reads=[buf(f"mtbf{dc}"), buf(f"wvT{dc}")], writes=[pbsr],
                    excl=(h == 8 * hb and dc == 0),
                    deps=(wvT_ops if (h == 0 and dc == 0) else ()))
        S.add("dve", lambda e, hb=hb: e.tensor_copy(
            out=srs[0:32, 512 * hb:512 * hb + 512], in_=PDS[0:32, :]),
            reads=[pbsr], writes=[buf("srs")], excl=False, deps=prep_guard)
    S.add("pool", lambda e: e.dma_start(out=ccs_in[0:32, :], in_=srs[0:32, :]),
          reads=[buf("srs")], writes=[buf("ccs")], is_dma=True, excl=False)
    S.add("pool", lambda e: e.dma_start(out=ccs_in[32:33, 0:NR],
                                        in_=row127[:, :]),
          reads=[buf("zr")], writes=[buf("ccs")], is_dma=True, excl=False,
          dkey="ccs")
    S.add("dve", lambda e: e.memset(carrf[1][:, :], 0.0),
          writes=[buf("carrf1")])
    S.add("pool", lambda e: e.dma_start(out=ccs_in[32:33, NR:1024],
                                        in_=carrf[1][:, :]),
          reads=[buf("carrf1")], writes=[buf("ccs")], is_dma=True, excl=False,
          dkey="ccs")
    S.add("pool", lambda e: e.collective_compute(
        "AllGather", ALU.bypass, replica_groups=[list(range(NCORES))],
        ins=[ccs_in.ap().opt()], outs=[ccs_out.ap().opt()]),
        reads=[buf("ccs")], writes=[buf("ccso")], strict=True)

    MARK["P2"] = len(S.ops)
    tap("kefm0", kefm[0][:, 0:512], ["kefm0"], lvl=4)
    tap("srs", srs[:, :], ["srs"], lvl=4)
    tap("mtbf0", mtbf[0][:, :], ["mtbf0"], lvl=4)
    S.add("dve", lambda e: e.memset(accb[0][:, :], 0.0), writes=[buf("accb0")],
          deps=prep_guard)
    for r in range(NCORES):
        S.add("pool", lambda e, r=r: e.dma_start(
            out=gth[:, :], in_=ccs_out[33 * r:33 * (r + 1), :]),
            reads=[buf("ccso")], writes=[buf("gth")], is_dma=True)
        a, b2 = r % 2, (r + 1) % 2
        S.add("dve", lambda e, r=r, a=a, b2=b2: e.scalar_tensor_tensor(
            out=accb[b2][:, :], in0=gth[:, :], scalar=rvec[:, r:r + 1],
            in1=accb[a][:, :], op0=ALU.mult, op1=ALU.add),
            reads=[buf("gth"), buf(f"accb{a}"), buf("rvec")],
            writes=[buf(f"accb{b2}")],
            deps=(prep_guard if r == 0 else ()))
    accf = buf(f"accb{NCORES % 2}")
    accT = accb[NCORES % 2]
    S.add("pool", lambda e: e.dma_start(out=dup_s[:, :], in_=accT[:, :]),
          reads=[accf], writes=[buf("dup")], is_dma=True)
    S.add("pool", lambda e: e.dma_start(out=S32[0][32:64, :], in_=dup_s[0:32, :]),
          reads=[buf("dup")], writes=[buf("S32_0")], is_dma=True, excl=False,
          dkey="sdup", deps=prep_guard)
    S.add("pool", lambda e: e.dma_start(out=row127[:, :],
                                        in_=dup_s[32:33, 0:NR]),
          reads=[buf("dup"), buf("zr")], writes=[buf("row127"), buf("zr")],
          is_dma=True)
    S.add("dve", lambda e: e.tensor_scalar(
        out=carrf[0][:, :], in0=row127[:, :], scalar1=EPS_Z, scalar2=8.0,
        op0=ALU.add, op1=ALU.mult),
        reads=[buf("row127")], writes=[buf("carrf0")])
    S.add("act", lambda e: e.copy(out=carrb[0][:, :], in_=carrf[0][:, :]),
          reads=[buf("carrf0")], writes=[buf("carrb0")])
    S.add("dve", lambda e: e.tensor_copy(out=S32[0][0:32, :], in_=accT[0:32, :]),
          reads=[accf], writes=[buf("S32_0")], excl=False)
    S.add("act", lambda e: e.copy(out=Sbf[0][:, :], in_=S32[0][:, :]),
          reads=[buf("S32_0")], writes=[buf("Sbf_0")])

    MARK["P3"] = len(S.ops)
    tap("acc", accT[:, :], [f"accb{NCORES % 2}"], lvl=4)
    tap("carrf0", carrf[0][:, :], ["carrf0"], lvl=4)
    tap("S32i", S32[0][:, :], ["S32_0"], lvl=4)
    mtbf_rd = [o for dc2 in range(8) for o in _B[f"mtbf{dc2}"].readers]
    for s in range(2):
        for h in range(NH):
            if h % 2 == 0:
                S.add("dve", lambda e, s=s, h=h: e.memset(
                    qfm[s][h][32:64, :], 0.0), writes=[buf(f"qfm{s}_{h}")])
            else:
                S.add("dve", lambda e, s=s, h=h: e.memset(
                    qfm[s][h][0:32, :], 0.0), writes=[buf(f"qfm{s}_{h}")])
    prev_a_evacs = []

    def emit_ketm_load(jn):
        kn = jn % 2
        S.add("sp", lambda e, jn=jn, kn=kn: e.dma_start(
            out=ketm[kn][:, :], in_=kef_s[:, CH * jn:CH * jn + 128],
            transpose=True),
            reads=[buf("scr_kef")], writes=[buf(f"ketm{kn}")], is_dma=True)

    def emit_prefix(j):
        sj = j % 2
        ks = j % 2
        if j + 1 < NCH:
            emit_ketm_load(j + 1)
        pbq = buf("bk_pb0")
        for dc in range(8):
            S.add("pe", lambda e, j=j, dc=dc: e.matmul(
                PB[0][:, 0:NR], xfm[dc][:, CH * j:CH * j + 128], wtq[dc][:, :],
                start=(dc == 0), stop=(dc == 7)),
                reads=[buf(f"xfm{dc}"), buf(f"wtq{dc}")], writes=[pbq],
                excl=(dc == 0))
        S.add("act", lambda e, sj=sj: e.activation(
            out=etm[sj][:, :], in_=PB[0][:, 0:NR], func=AF.Exp),
            reads=[pbq], writes=[buf(f"etm{sj}")])
        for half in range(2):
            pbv = buf(f"bk_pb{2 + half}")
            for dc in range(8):
                S.add("pe", lambda e, j=j, dc=dc, half=half: e.matmul(
                    PB[2 + half][:, 0:512], xfm[dc][:, CH * j:CH * j + 128],
                    wvT[dc][:, 512 * half:512 * half + 512],
                    start=(dc == 0), stop=(dc == 7)),
                    reads=[buf(f"xfm{dc}"), buf(f"wvT{dc}")], writes=[pbv],
                    excl=(dc == 0),
                    deps=(woT_ops if (j == 0 and dc == 0 and half == 0) else ()))
            S.add("dve", lambda e, half=half, sj=sj: e.tensor_copy(
                out=vtm[sj][:, 512 * half:512 * half + 512],
                in_=PB[2 + half][:, 0:512]),
                reads=[pbv], writes=[buf(f"vtm{sj}")], excl=(half == 0),
                deps=(mtbf_rd if j < 2 else ()))
        pbz = buf("bk_pb1")
        zop1 = S.add("pe", lambda e, j=j, ks=ks: e.matmul(
            PB[1][:, 0:NR], u8t[:, :], ketm[ks][:, :], start=True, stop=False),
            reads=[buf("u8t"), buf(f"ketm{ks}")], writes=[pbz],
            deps=prev_a_evacs)
        zop2 = S.add("pe", lambda e, j=j: e.matmul(
            PB[1][:, 0:NR], onesr[:, :], carrb[j % 2][:, :],
            start=False, stop=True),
            reads=[buf("onesr"), buf(f"carrb{j % 2}")], writes=[pbz], excl=False)
        S.add("dve", lambda e, sj=sj: e.tensor_reduce(
            out=hsb[sj][:, :],
            in_=etm[sj].rearrange("p (h c) -> p h c", c=32)[:, :, 0:16],
            axis=AX.X, op=ALU.add),
            reads=[buf(f"etm{sj}")], writes=[buf(f"hsb{sj}")], strict=True)
        dop = S.add("dve", lambda e, sj=sj: e.tensor_tensor(
            out=dbf.rearrange("p (h z) -> p h z", z=32),
            in0=PB[1][:, 0:NR].rearrange("p (h z) -> p h z", z=32),
            in1=hsb[sj].rearrange("p (h o) -> p h o", o=1).broadcast_to(
                [128, 16, 32]),
            op=ALU.mult),
            reads=[pbz, buf(f"hsb{sj}")], writes=[buf("dbf")],
            deps=(mtbf_rd if j == 0 else ()))
        S.add("act", lambda e: e.activation(out=lnd[:, :], in_=dbf[:, :],
                                            func=AF.Ln),
              reads=[buf("dbf")], writes=[buf("lnd")])
        S.add("dve", lambda e: e.tensor_tensor(
            out=dbf[:, :], in0=PB[0][:, 0:NR], in1=lnd[:, :], op=ALU.subtract),
            reads=[pbq, buf("lnd")], writes=[buf("dbf")])
        S.add("act", lambda e, sj=sj: e.activation(
            out=qtm[sj][:, :], in_=dbf[:, :], func=AF.Exp),
            reads=[buf("dbf")], writes=[buf(f"qtm{sj}")])
        S.add("pe", lambda e, j=j, ks=ks: e.matmul(
            PB[0][0:1, 0:NR], ones1[:, :], ketm[ks][:, :],
            start=True, stop=True),
            reads=[buf("ones1"), buf(f"ketm{ks}")], writes=[pbq])
        S.add("dve", lambda e, j=j: e.scalar_tensor_tensor(
            out=carrf[(j + 1) % 2][:, :], in0=PB[0][0:1, 0:NR], scalar=8.0,
            in1=carrf[j % 2][:, :], op0=ALU.mult, op1=ALU.add),
            reads=[pbq, buf(f"carrf{j % 2}")],
            writes=[buf(f"carrf{(j + 1) % 2}")])
        S.add("act", lambda e, j=j: e.copy(out=carrb[(j + 1) % 2][:, :],
                                           in_=carrf[(j + 1) % 2][:, :]),
              reads=[buf(f"carrf{(j + 1) % 2}")],
              writes=[buf(f"carrb{(j + 1) % 2}")])
        ptq = buf("bk_pt")
        for h in range(NH):
            p, lo = h >> 1, 32 * (h & 1)
            S.add("pe", lambda e, sj=sj, h=h, p=p, lo=lo: e.transpose(
                PT[lo:lo + 32, 128 * p:128 * p + 128],
                qtm[sj][:, 32 * h:32 * h + 32], ident[:, :]),
                reads=[buf(f"qtm{sj}"), b_id], writes=[ptq], excl=(h == 0))
        for h in range(NH):
            p, lo = h >> 1, 32 * (h & 1)
            S.add("act", lambda e, sj=sj, h=h, p=p, lo=lo: e.copy(
                out=qfm[sj][h][lo:lo + 32, :],
                in_=PT[lo:lo + 32, 128 * p:128 * p + 128]),
                reads=[ptq], writes=[buf(f"qfm{sj}_{h}")], excl=False)
        return [dop, zop1, zop2]

    def emit_body(j, dops):
        nonlocal_holder = None
        a_evacs = []

        def emit_amm(h):
            p = h >> 1
            pa = buf(f"pa{h % 4}")
            dd = (list(dops) if h < 4 else []) + a_evacs[-1:]
            S.add("pe", lambda e, j=j, p=p, h=h: e.matmul(
                PB[1][:, 128 * (h % 4):128 * (h % 4) + 128],
                kefm[p][0:64, CH * j:CH * j + 128],
                qfm[j % 2][h][0:64, :], start=True, stop=True),
                reads=[buf(f"kefm{p}"), buf(f"qfm{j % 2}_{h}")], writes=[pa],
                deps=dd)
            aev = S.add("dve", lambda e, h=h: e.scalar_tensor_tensor(
                out=asb[h % 4][:, :],
                in0=PB[1][:, 128 * (h % 4):128 * (h % 4) + 128],
                scalar=1.0, in1=m1t[:, :], op0=ALU.mult, op1=ALU.mult),
                reads=[pa, buf("m1t")], writes=[buf(f"asb{h % 4}")])
            a_evacs.append(aev)

        emit_amm(0)
        for h in range(NH):
            p, vc = h >> 1, 64 * h
            ti = h >> 1
            hb = h // 8
            pds_b = buf("bk_pds")
            ks = j % 2
            S.add("pe", lambda e, h=h, vc=vc, j=j, ks=ks, hb=hb: e.matmul(
                PDS[0:32, 64 * h - 512 * hb:64 * h - 512 * hb + 64],
                ketm[ks][:, 32 * h:32 * h + 32], vtm[j % 2][:, vc:vc + 64],
                start=True, stop=True),
                reads=[buf(f"ketm{ks}"), buf(f"vtm{j % 2}")], writes=[pds_b],
                excl=(h == 8 * hb))
            S.add("pe", lambda e, h=h, vc=vc, j=j, ks=ks, hb=hb: e.matmul(
                PDS[32:64, 64 * h - 512 * hb:64 * h - 512 * hb + 64],
                ketm[ks][:, 32 * h:32 * h + 32], vtm[j % 2][:, vc:vc + 64],
                start=True, stop=True),
                reads=[buf(f"ketm{ks}"), buf(f"vtm{j % 2}")], writes=[pds_b],
                excl=False)
            if h + 1 < NH:
                emit_amm(h + 1)
            pxo = buf(f"bk_pxo{ti // 4}")
            S.add("pe", lambda e, h=h, vc=vc, ti=ti, j=j: e.matmul(
                PXO[ti // 4][64 * (h & 1):64 * (h & 1) + 64,
                             128 * (ti % 4):128 * (ti % 4) + 128],
                vtm[j % 2][:, vc:vc + 64], asb[h % 4][:, :],
                start=True, stop=False),
                reads=[buf(f"vtm{j % 2}"), buf(f"asb{h % 4}")], writes=[pxo],
                excl=((h & 1) == 0))
            S.add("pe", lambda e, h=h, p=p, vc=vc, ti=ti, j=j:
                  e.matmul(
                PXO[ti // 4][64 * (h & 1):64 * (h & 1) + 64,
                             128 * (ti % 4):128 * (ti % 4) + 128],
                Sbf[j % 2][0:64, vc:vc + 64],
                qfm[j % 2][h][0:64, :],
                start=False, stop=True),
                reads=[buf(f"Sbf_{j % 2}"), buf(f"qfm{j % 2}_{h}")],
                writes=[pxo], excl=False)
            if h % 8 == 7:
                S.add("dve", lambda e, j=j, hb=hb: e.tensor_tensor(
                    out=S32[(j + 1) % 2][0:64, 512 * hb:512 * hb + 512],
                    in0=S32[j % 2][0:64, 512 * hb:512 * hb + 512],
                    in1=PDS[:, :], op=ALU.add),
                    reads=[buf(f"S32_{j % 2}"), pds_b],
                    writes=[buf(f"S32_{(j + 1) % 2}")], excl=(hb == 0))
        return a_evacs

    def emit_tail(j):
        sj = j % 2
        so = 8 * sj
        S.add("act", lambda e, j=j: e.copy(out=Sbf[(j + 1) % 2][:, :],
                                           in_=S32[(j + 1) % 2][:, :]),
              reads=[buf(f"S32_{(j + 1) % 2}")],
              writes=[buf(f"Sbf_{(j + 1) % 2}")])
        for ti in range(8):
            S.add("dve", lambda e, ti=ti: e.tensor_copy(
                out=xofm[ti][:, :],
                in_=PXO[ti // 4][:, 128 * (ti % 4):128 * (ti % 4) + 128]),
                reads=[buf(f"bk_pxo{ti // 4}")], writes=[buf(f"xofm{ti}")])
        for half in range(2):
            pbo = buf(f"bk_pb{2 + half}")
            for ti in range(8):
                S.add("pe", lambda e, ti=ti, half=half: e.matmul(
                    PB[2 + half][:, 0:512], xofm[ti][:, :],
                    woT[ti][:, 512 * half:512 * half + 512],
                    start=(ti == 0), stop=(ti == 7)),
                    reads=[buf(f"xofm{ti}"), buf(f"woT{ti}")], writes=[pbo],
                    excl=(ti == 0))
        S.add("pool", lambda e, j=j, sj=sj: e.dma_start(
            out=xres[sj][:, :], in_=x_d[CH * j:CH * (j + 1), :]),
            writes=[buf(f"xres{sj}")], is_dma=True,
            deps=(mtbf_rd if j < 2 else ()))
        for half in range(2):
            S.add("dve", lambda e, half=half, sj=sj: e.tensor_tensor(
                out=pre1[:, 512 * half:512 * half + 512],
                in0=PB[2 + half][:, 0:512],
                in1=xres[sj][:, 512 * half:512 * half + 512], op=ALU.add),
                reads=[buf(f"bk_pb{2 + half}"), buf(f"xres{sj}")],
                writes=[buf("pre1")], excl=(half == 0))
        if j == 0:
            tap("etm0", etm[0][:, :], ["etm0"], lvl=2)
            tap("hsb0", hsb[0][:, :], ["hsb0"], lvl=2)
            tap("qtm0", qtm[0][:, :], ["qtm0"], lvl=2)
            tap("ketm0", ketm[0][:, :], ["ketm0"], lvl=2)
            tap("vtm0", vtm[0][:, :], ["vtm0"], lvl=2)
            tap("qfm0", qfm[0][0][0:64, :], ["qfm0_0"], lvl=2)
            tap("qfm1", qfm[0][1][0:64, :], ["qfm0_1"], lvl=2)
            tap("asb0", asb[0][:, :], ["asb0"], lvl=2)
            tap("xofm0", xofm[0][:, :], ["xofm0"], lvl=2)
            tap("S32n", S32[1][:, :], ["S32_1"], lvl=2)
            tap("pre1", pre1[:, :], ["pre1"], lvl=2)
        _emit_ln(S, buf, pre1, sqb, stat, so, ln1, "ln1", "pre1", "sqb")
        if j == 0:
            tap("ln1", ln1[:, :], ["ln1"], lvl=2)
        S.add("act", lambda e, sj=sj: e.copy(out=ln1b[sj][:, :], in_=ln1[:, :]),
              reads=[buf("ln1")], writes=[buf(f"ln1b{sj}")])
        S.add("pool", lambda e, j=j: e.dma_start(
            out=lntm_s[CH * j:CH * (j + 1), :], in_=ln1[:, :]),
            reads=[buf("ln1")], writes=[buf("scr_lntm")], is_dma=True,
            excl=False, dkey="lntm")
        S.add("pool", lambda e, j=j, sj=sj: e.dma_start(
            out=lnb_s[CH * j:CH * (j + 1), :], in_=ln1b[sj][:, :]),
            reads=[buf(f"ln1b{sj}")], writes=[buf(f"scr_lnb")], is_dma=True,
            excl=False, dkey=f"lnb{sj}")

    emit_ketm_load(0)
    dops = emit_prefix(0)
    for j in range(NCH):
        MARK[f"C{j}"] = len(S.ops)
        a_evacs = emit_body(j, dops)
        prev_a_evacs = a_evacs[-4:] + [dops[0]]
        if j + 1 < NCH:
            dops_next = emit_prefix(j + 1)
        emit_tail(j)
        if j + 1 < NCH:
            dops = dops_next
    MARK["P4"] = len(S.ops)
    xfm_readers = [o for dc in range(8) for o in _B[f"xfm{dc}"].readers]
    wtq_readers = [o for i in range(8) for o in _B[f"wtq{i}"].readers]
    wtk_readers = [o for i in range(8) for o in _B[f"wtk{i}"].readers]
    pre_slot0 = []
    for dc in range(8):
        pre_slot0.append(S.add("sp", lambda e, dc=dc: e.dma_start(
            out=wuT[0][dc][:, :],
            in_=wub_s[0:512, 128 * dc:128 * dc + 128], transpose=True),
            reads=[buf("scr_wu")], writes=[buf(f"wuT0_{dc}")],
            is_dma=True, deps=wtk_readers, dkey=f"wuT0{dc % 2}"))
    for dc in range(8):
        pre_slot0.append(S.add("sp", lambda e, dc=dc: e.dma_start(
            out=wgT[0][dc][:, :],
            in_=wgb_s[0:512, 128 * dc:128 * dc + 128], transpose=True),
            reads=[buf("scr_wg")], writes=[buf(f"wgT0_{dc}")],
            is_dma=True, deps=wtq_readers, dkey=f"wgT0{dc % 2}"))
    lnfm_ops = []
    for dc in range(8):
        lnfm_ops.append(S.add("sp", lambda e, dc=dc: e.dma_start(
            out=lnfm[dc][:, :], in_=lnb_s[:, 128 * dc:128 * dc + 128],
            transpose=True),
            reads=[buf("scr_lnb")], writes=[buf(f"lnfm{dc}")], is_dma=True,
            deps=xfm_readers, dkey=f"lnfmT{dc % 4}"))
    scanB_readers = []
    for nm in (["xres0", "xres1", "asb0", "asb1", "asb2", "asb3"] +
               [f"xofm{i}" for i in range(8)] +
               [f"qfm{s}_{h}" for s in range(2) for h in range(16)]):
        scanB_readers += _B[nm].readers + _B[nm].writers
    for hg in range(6):
        sl = hg % 2
        if hg == 0:
            slot_ops = list(pre_slot0)
        else:
            slot_ops = []
        for dc in range(8) if hg else []:
            dep0 = scanB_readers if hg == 1 else []
            slot_ops.append(S.add("sp", lambda e, hg=hg, sl=sl, dc=dc: e.dma_start(
                out=wgT[sl][dc][:, :],
                in_=wgb_s[512 * hg:512 * (hg + 1), 128 * dc:128 * dc + 128],
                transpose=True),
                reads=[buf("scr_wg")], writes=[buf(f"wgT{sl}_{dc}")],
                is_dma=True, deps=dep0, dkey=f"wgT{sl}{dc % 2}"))
            slot_ops.append(S.add("sp", lambda e, hg=hg, sl=sl, dc=dc: e.dma_start(
                out=wuT[sl][dc][:, :],
                in_=wub_s[512 * hg:512 * (hg + 1), 128 * dc:128 * dc + 128],
                transpose=True),
                reads=[buf("scr_wu")], writes=[buf(f"wuT{sl}_{dc}")],
                is_dma=True, deps=dep0, dkey=f"wuT{sl}{dc % 2}"))
        for hc in range(4):
            hh = 4 * hg + hc
            for ts2 in range(4):
                pg, pu = buf("bk_pb0"), buf("bk_pb1")
                for dc in range(8):
                    S.add("pe", lambda e, sl=sl, hc=hc, ts2=ts2, dc=dc: e.matmul(
                        PB[0][:, 0:512], wgT[sl][dc][:, 128 * hc:128 * hc + 128],
                        lnfm[dc][:, 512 * ts2:512 * ts2 + 512],
                        start=(dc == 0), stop=(dc == 7)),
                        reads=[buf(f"wgT{sl}_{dc}"), buf(f"lnfm{dc}")],
                        writes=[pg], excl=(dc == 0),
                        deps=(slot_ops + lnfm_ops) if (hc == 0 and ts2 == 0
                                                       and dc == 0) else ())
                for dc in range(8):
                    S.add("pe", lambda e, sl=sl, hc=hc, ts2=ts2, dc=dc: e.matmul(
                        PB[1][:, 0:512], wuT[sl][dc][:, 128 * hc:128 * hc + 128],
                        lnfm[dc][:, 512 * ts2:512 * ts2 + 512],
                        start=(dc == 0), stop=(dc == 7)),
                        reads=[buf(f"wuT{sl}_{dc}"), buf(f"lnfm{dc}")],
                        writes=[pu], excl=(dc == 0))
                S.add("act", lambda e: e.activation(
                    out=silb[:, 0:512], in_=PB[0][:, 0:512], func=AF.Silu,
                    scale=QS),
                    reads=[pg], writes=[buf("silb")],
                    deps=(scanB_readers if (hg == 0 and hc == 0 and ts2 == 0)
                          else ()))
                S.add("dve", lambda e, hh=hh, ts2=ts2: e.tensor_tensor(
                    out=hpr[hh][:, 512 * ts2:512 * ts2 + 512],
                    in0=silb[:, 0:512], in1=PB[1][:, 0:512], op=ALU.mult),
                    reads=[buf("silb"), pu], writes=[buf(f"hpr{hh}")],
                    excl=False)
    tap("lnfm0", lnfm[0][:, 0:512], ["lnfm0"], lvl=2)
    tap("hpr0", hpr[0][:, 0:512], ["hpr0"], lvl=2)
    tap("hpr16", hpr[16][:, 0:512], ["hpr16"], lvl=2)
    wvwo_readers = [o for dc in range(8)
                    for o in (_B[f"wvT{dc}"].readers + _B[f"woT{dc}"].readers)]
    sket_readers = (_B["ketm0"].readers + _B["ketm1"].readers +
                    _B["S32_0"].readers + _B["S32_1"].readers +
                    _B["Sbf_0"].readers + _B["Sbf_1"].readers)
    wdT_ops = []
    for hh in range(24):
        wdT_ops.append(S.add("sp", lambda e, hh=hh: e.dma_start(
            out=wdT[hh][:, :], in_=wdb_s[:, 128 * hh:128 * hh + 128],
            transpose=True),
            reads=[buf("scr_wd")], writes=[buf(f"wdT{hh}")], is_dma=True,
            deps=((sket_readers if hh == 0 else _B["silb"].readers)
                  if hh < 2 else
                  (wvwo_readers if hh == 2 else
                   ([o for sl2 in range(1) for dc2 in range(8)
                     for o in _B[f"wgT0_{dc2}"].readers +
                     _B[f"wuT0_{dc2}"].readers] if hh == 18 else []))),
            dkey=f"wdT{hh % 6}"))
    yops = []
    for tj in range(NCH):
        sj = tj % 2
        so = 8 * sj
        S.add("pool", lambda e, tj=tj, sj=sj: e.dma_start(
            out=ltm[sj][:, :], in_=lntm_s[CH * tj:CH * (tj + 1), :]),
            reads=[buf("scr_lntm")], writes=[buf(f"ltm{sj}")], is_dma=True,
            deps=([o for dc2 in range(8) for o in _B[f"lnfm{dc2}"].readers]
                  if tj < 2 else []))
        for half in range(2):
            pbo = buf(f"bk_pb{2 + half}")
            for hh in range(24):
                S.add("pe", lambda e, tj=tj, half=half, hh=hh: e.matmul(
                    PB[2 + half][:, 0:512], hpr[hh][:, CH * tj:CH * tj + 128],
                    wdT[hh][:, 512 * half:512 * half + 512],
                    start=(hh == 0), stop=(hh == 23)),
                    reads=[buf(f"hpr{hh}"), buf(f"wdT{hh}")], writes=[pbo],
                    excl=(hh == 0),
                    deps=(wdT_ops if (tj == 0 and half == 0 and hh == 0)
                          else ()))
            S.add("dve", lambda e, half=half, sj=sj: e.scalar_tensor_tensor(
                out=pr2[:, 512 * half:512 * half + 512],
                in0=PB[2 + half][:, 0:512], scalar=QS2,
                in1=ltm[sj][:, 512 * half:512 * half + 512],
                op0=ALU.mult, op1=ALU.add),
                reads=[buf(f"bk_pb{2 + half}"), buf(f"ltm{sj}")],
                writes=[buf("pr2")], excl=(half == 0))
        if tj == 0:
            tap("ltm0", ltm[0][:, :], ["ltm0"], lvl=2)
            tap("pr2", pr2[:, :], ["pr2"], lvl=2)
        _emit_ln(S, buf, pr2, sq2, stat, so, outb[sj], f"outb{sj}", "pr2", "sq2")
        yops.append(S.add("pool", lambda e, tj=tj, sj=sj: e.dma_start(
            out=y_d[CH * tj:CH * (tj + 1), :], in_=outb[sj][:, :]),
            reads=[buf(f"outb{sj}")], writes=[buf("yout")], is_dma=True,
            excl=False, dkey=f"y{sj}"))
    S.add("sp", lambda e: e.nop(), deps=yops + tap_list)
    MARK["ALL"] = len(S.ops)

    import builtins
    builtins._KMARK = dict(MARK)
    if stop is not None:
        k = MARK[stop] if isinstance(stop, str) else stop
        S.ops = S.ops[:k]
        last = {}
        for op in S.ops:
            last[op.engine] = op
        S.add("sp", lambda e: e.nop(),
              deps=[o for o in last.values()])

    with nc.Block() as block:
        S.finalize(nc, block, ctx)
    ctx.close()
    return nc


def _emit_ln(S, buf, src, sq, stat, so, dst, dst_name, src_name, sq_name):
    st = buf(f"stat{so}")
    S.add("dve", lambda e: e.tensor_reduce(
        out=stat[:, so:so + 1], in_=src[:, :], axis=AX.X, op=ALU.add),
        reads=[buf(src_name)], writes=[st], strict=True)
    S.add("dve", lambda e: e.tensor_tensor(out=sq[:, :], in0=src[:, :],
                                           in1=src[:, :], op=ALU.mult),
          reads=[buf(src_name)], writes=[buf(sq_name)], strict=True)
    S.add("dve", lambda e: e.tensor_reduce(
        out=stat[:, so + 1:so + 2], in_=sq[:, :], axis=AX.X, op=ALU.add),
        reads=[buf(sq_name)], writes=[st], excl=False, strict=True)
    S.add("dve", lambda e: e.tensor_scalar(
        out=stat[:, so + 2:so + 3], in0=stat[:, so:so + 1], scalar1=1.0 / D,
        scalar2=None, op0=ALU.mult),
        reads=[st], writes=[st], excl=False, strict=True)
    S.add("dve", lambda e: e.tensor_scalar(
        out=stat[:, so + 3:so + 4], in0=stat[:, so + 2:so + 3], scalar1=-1.0,
        scalar2=None, op0=ALU.mult),
        reads=[st], writes=[st], excl=False, strict=True)
    S.add("dve", lambda e: e.tensor_scalar(
        out=stat[:, so + 4:so + 5], in0=stat[:, so + 1:so + 2], scalar1=1.0 / D,
        scalar2=EPS_LN, op0=ALU.mult, op1=ALU.add),
        reads=[st], writes=[st], excl=False, strict=True)
    S.add("dve", lambda e: e.scalar_tensor_tensor(
        out=stat[:, so + 5:so + 6], in0=stat[:, so + 2:so + 3],
        scalar=stat[:, so + 2:so + 3], in1=stat[:, so + 4:so + 5],
        op0=ALU.mult, op1=ALU.subtract),
        reads=[st], writes=[st], excl=False, strict=True)
    S.add("act", lambda e: e.activation(
        out=stat[:, so + 6:so + 7], in_=stat[:, so + 5:so + 6], func=AF.Ln,
        scale=-1.0),
        reads=[st], writes=[st], excl=False, strict=True)
    S.add("act", lambda e: e.activation(
        out=stat[:, so + 7:so + 8], in_=stat[:, so + 6:so + 7], func=AF.Exp,
        scale=-0.5),
        reads=[st], writes=[st], excl=False, strict=True)
    S.add("dve", lambda e: e.tensor_scalar(
        out=dst[:, :], in0=src[:, :], scalar1=stat[:, so + 3:so + 4],
        scalar2=stat[:, so + 7:so + 8], op0=ALU.add, op1=ALU.mult),
        reads=[buf(src_name), st], writes=[buf(dst_name)], strict=True)


_NC_CACHE = {}


def kernel(**inputs):
    x = np.ascontiguousarray(np.asarray(inputs["L0_attn_x"]).reshape(T, D))
    wqkv = np.ascontiguousarray(np.asarray(inputs["L0_attn_w_qkv"]))
    wout = np.ascontiguousarray(np.asarray(inputs["L0_attn_w_out"]))
    code = np.ascontiguousarray(np.asarray(inputs["L0_attn_fc_code"])
                                .reshape(NH * CS, HD))
    wg = np.ascontiguousarray(np.asarray(inputs["L0_ffn_w_gate"], np.int32))
    wu = np.ascontiguousarray(np.asarray(inputs["L0_ffn_w_up"], np.int32))
    wd = np.ascontiguousarray(np.asarray(inputs["L0_ffn_w_down"], np.int32))

    ident = np.eye(128, dtype=np.float32).astype(ml_dtypes.bfloat16)
    st = np.arange(128)
    u8 = (8.0 * (st[:, None] <= st[None, :])).astype(ml_dtypes.bfloat16)
    m1 = (st[:, None] <= st[None, :]).astype(np.float32)
    onesr = np.ones((1, 128), dtype=ml_dtypes.bfloat16)
    ones1 = np.ones((128, 1), dtype=ml_dtypes.bfloat16)
    rr = np.arange(64)
    padv = np.where((rr % 32) >= 16, 1e9, -3.0e38).astype(np.float32)[:, None]

    if "nc" not in _NC_CACHE:
        _NC_CACHE["nc"] = build_nc()
    nc = _NC_CACHE["nc"]

    in_maps = []
    for c in range(NCORES):
        rvec = np.zeros((33, 8), dtype=np.float32)
        rvec[:, :c] = 1.0
        in_maps.append({
            "x": np.asarray(x[TLOC * c:TLOC * (c + 1)], np.float32),
            "w_qkv": np.asarray(wqkv, np.float32),
            "w_out": np.asarray(wout, np.float32),
            "fc_code": np.asarray(code, np.float32),
            "w_gate": wg, "w_up": wu, "w_down": wd,
            "c_ident": ident, "c_u8": u8, "c_m1": m1, "c_onesr": onesr, "c_ones1": ones1,
            "c_padv": padv, "c_rvec": rvec,
        })
    res = run_bass_kernel_spmd(nc, in_maps, core_ids=list(range(NCORES)))
    out = np.concatenate([np.asarray(r["y"]) for r in res.results], axis=0)
    return out.reshape(1, T, D)
